# version 05
# speedup vs baseline: 1.1238x; 1.0045x over previous
import numpy as np
from contextlib import ExitStack

import concourse.bass as bass
import concourse.mybir as mybir
from concourse.bass_utils import run_bass_kernel_spmd
from concourse.bass import IndirectOffsetOnAxis

F32 = mybir.dt.float32
BF16 = mybir.dt.bfloat16
I32 = mybir.dt.int32
AF = mybir.ActivationFunctionType
ALU = mybir.AluOpType
AX = mybir.AxisListType

TOK = 1024
NTT = TOK // 128
D = 4096
NDT = D // 128
EPS = 1e-5

A_DEPTH, A_DIM, A_DFF = 4, 512, 2048
H_HEADS, H_DH, H_M = 16, 256, 1024


class Sem:
    __slots__ = ("h", "v", "_cm")

    def __init__(self, nc, name):
        self._cm = nc.semaphore(name)
        self.h = self._cm.__enter__()
        self.v = 0

    def inc(self, instr, n):
        instr.then_inc(self.h, n)
        self.v += n
        return self.v


class P:
    def __init__(self, nc, es, cores):
        self.nc = nc
        self.es = es
        self.cores = cores
        self.sems = {}
        self.n_bar = 0
        self.zrow = self.sb("p_zrow", [1, 8], F32)

    def sem(self, name):
        if name not in self.sems:
            self.sems[name] = Sem(self.nc, name)
        return self.sems[name]

    def sb(self, name, shape, dtype, es=None):
        return (es or self.es).enter_context(self.nc.sbuf_tensor(name, shape, dtype))

    def psum(self, name, shape, dtype=F32, es=None):
        return (es or self.es).enter_context(self.nc.psum_tensor(name, shape, dtype))


def emit_barrier(p, block):
    nc = p.nc
    s_cc = p.sem("s_cc")
    i = p.n_bar
    p.n_bar += 1
    bar_in = nc.dram_tensor(f"bar_in{i}", [1, 8], F32)
    bar_out = nc.dram_tensor(f"bar_out{i}", [p.cores, 8], F32, addr_space="Shared")
    s_gp = p.sem("s_gp")

    s_gpc = p.sem("s_gpc")

    @block.gpsimd
    def _(gp):
        ins = gp.memset(p.zrow[:], 0.0)
        zc = s_gpc.inc(ins, 1)
        gp.wait_ge(s_gpc.h, zc)
        ins = gp.dma_start(out=bar_in.ap(), in_=p.zrow[:])
        zv = s_gp.inc(ins, 16)
        gp.wait_ge(s_gp.h, zv)
        ins = gp.collective_compute(
            "AllGather", ALU.bypass, replica_groups=[list(range(p.cores))],
            ins=[bar_in.ap()], outs=[bar_out.ap()],
        )
        s_cc.inc(ins, 1)

    return s_cc.v


def emit_allgather(p, block, local_in_ap, shared_out_ap, wait_gp=None):
    s_cc = p.sem("s_cc")

    @block.gpsimd
    def _(gp):
        if wait_gp is not None:
            gp.wait_ge(wait_gp[0].h, wait_gp[1])
        ins = gp.collective_compute(
            "AllGather", ALU.bypass, replica_groups=[list(range(p.cores))],
            ins=[local_in_ap], outs=[shared_out_ap],
        )
        s_cc.inc(ins, 1)

    return s_cc.v



def emit_ln(p, block, es, x_dram, xln_tm, xlnT, tag, fm_dram=None,
            x_ready=None):
    s_lx = [p.sem("s_lx0"), p.sem("s_lx1")]
    s_v = p.sem("s_v")
    s_s = p.sem("s_s")
    s_wm = [p.sem("s_wm0"), p.sem("s_wm1")]
    s_trd = p.sem("s_trd")

    NC5 = D // 512
    xt = [p.sb(f"{tag}_xt{i}", [128, D], F32, es) for i in range(2)]
    stats = p.sb(f"{tag}_bst", [128, 6 * NC5], F32, es)
    agg = p.sb(f"{tag}_agg", [128, 2 * NTT], F32, es)
    vep = p.sb(f"{tag}_vep", [128, NTT], F32, es)
    sd = p.sb(f"{tag}_sd", [128, NTT], F32, es)
    rs = p.sb(f"{tag}_rs", [128, NTT], F32, es)
    nb = p.sb(f"{tag}_nb", [128, NTT], F32, es)
    xlno = [p.sb(f"{tag}_xo{i}", [128, D], BF16, es) for i in range(2)]

    v0, s0 = s_v.v, s_s.v
    lx0 = [s_lx[0].v, s_lx[1].v]
    wm0 = [s_wm[0].v, s_wm[1].v]
    trd0 = s_trd.v
    lx1_p = [lx0[tt % 2] + 16 * (tt // 2 + 1) for tt in range(NTT)]
    lx2_p = [lx0[tt % 2] + 16 * (NTT // 2 + tt // 2 + 1) for tt in range(NTT)]
    v1_p = [v0 + tt + 1 for tt in range(NTT)]
    sqrt_p = s0 + 1
    rs_p = v0 + NTT + 1
    s2_p = [s0 + 1 + tt + 1 for tt in range(NTT)]
    wm_p = [wm0[tt % 2] + 16 * (tt // 2 + 1) for tt in range(NTT)]
    n_tr = NDT + (NDT if fm_dram is not None else 0)
    tr_all = trd0 + 16 * n_tr

    @block.sync
    def _(sync):
        for tt in range(NTT):
            if tt == 0 and x_ready:
                for sem, val in x_ready:
                    sync.wait_ge(sem.h, val)
            if tt >= 2:
                sync.wait_ge(s_v.h, v1_p[tt - 2])
            ins = sync.dma_start(out=xt[tt % 2][:],
                                 in_=x_dram[tt * 128:(tt + 1) * 128, :])
            assert s_lx[tt % 2].inc(ins, 16) == lx1_p[tt]
        for tt in range(NTT):
            if tt < 2:
                sync.wait_ge(s_v.h, v1_p[-1])
            ins = sync.dma_start(out=xt[tt % 2][:],
                                 in_=x_dram[tt * 128:(tt + 1) * 128, :])
            assert s_lx[tt % 2].inc(ins, 16) == lx2_p[tt]
            if tt >= 1:
                w = tt - 1
                sync.wait_ge(s_s.h, s2_p[w])
                ins = sync.dma_start(out=xln_tm[w * 128:(w + 1) * 128, :],
                                     in_=xlno[w % 2][:])
                assert s_wm[w % 2].inc(ins, 16) == wm_p[w]
        for w in (NTT - 1,):
            sync.wait_ge(s_s.h, s2_p[w])
            ins = sync.dma_start(out=xln_tm[w * 128:(w + 1) * 128, :],
                                 in_=xlno[w % 2][:])
            assert s_wm[w % 2].inc(ins, 16) == wm_p[w]

    @block.vector
    def _(v):
        s_ve = p.sem("s_ve")
        for tt in range(NTT):
            v.wait_ge(s_lx[tt % 2].h, lx1_p[tt])
            for c in range(NC5):
                ins = v.bn_stats(stats[:, 6 * c:6 * c + 6],
                                 xt[tt % 2][:, 512 * c:512 * (c + 1)])
            v.wait_ge(s_ve.h, s_ve.inc(ins, 1))
            ins = v.bn_aggr(agg[:, 2 * tt:2 * tt + 2], stats[:])
            v.wait_ge(s_ve.h, s_ve.inc(ins, 1))
            ins = v.tensor_scalar(vep[:, tt:tt + 1], agg[:, 2 * tt + 1:2 * tt + 2],
                                  EPS, None, ALU.add)
            assert s_v.inc(ins, 1) == v1_p[tt]
        v.wait_ge(s_s.h, sqrt_p)
        ins = v.reciprocal(rs[:], sd[:])
        v.wait_ge(s_ve.h, s_ve.inc(ins, 1))
        means = agg[:].rearrange("p (t two) -> p two t", two=2)[:, 0, :]
        ins = v.tensor_tensor(nb[:], means, rs[:], ALU.mult)
        v.wait_ge(s_ve.h, s_ve.inc(ins, 1))
        ins = v.tensor_scalar(nb[:], nb[:], -1.0, None, ALU.mult)
        assert s_v.inc(ins, 1) == rs_p

    @block.scalar
    def _(s):
        s.wait_ge(s_v.h, v1_p[-1])
        ins = s.activation(sd[:], vep[:], AF.Sqrt)
        assert s_s.inc(ins, 1) == sqrt_p
        for tt in range(NTT):
            s.wait_ge(s_lx[tt % 2].h, lx2_p[tt])
            if tt == 0:
                s.wait_ge(s_v.h, rs_p)
            if tt >= 2:
                s.wait_ge(s_wm[tt % 2].h, wm_p[tt - 2])
            ins = s.activation(xlno[tt % 2][:], xt[tt % 2][:], AF.Identity,
                               bias=nb[:, tt:tt + 1], scale=rs[:, tt:tt + 1])
            assert s_s.inc(ins, 1) == s2_p[tt]
        for i, w in enumerate(wm_p[-2:]):
            s.wait_ge(s_wm[(NTT - 2 + i) % 2].h, w)
        for dt in range(NDT):
            ins = s.dma_start(out=xlnT[:, dt * TOK:(dt + 1) * TOK],
                              in_=xln_tm[:, dt * 128:(dt + 1) * 128],
                              transpose=True)
            s_trd.inc(ins, 16)
        if fm_dram is not None:
            s.wait_ge(s_trd.h, trd0 + 16 * NDT)
            for dt in range(NDT):
                ins = s.dma_start(out=fm_dram[dt * 128:(dt + 1) * 128, :],
                                  in_=xlnT[:, dt * TOK:(dt + 1) * TOK])
                s_trd.inc(ins, 16)
    assert s_trd.v == tr_all

    return tr_all, tr_all



def emit_bcast_row(p, block, es, vec_dram_row, n, out_sb, scale, tag, ones1,
                   ones_ready):
    s_lb = p.sem("s_lx")
    s_mm = p.sem("s_mm")
    s_v = p.sem("s_v")
    vrow = p.sb(f"{tag}_vrow", [1, n], F32, es)
    pb = [p.psum(f"{tag}_bps{i}", [128, 512], F32, es=es) for i in range(2)]
    hold = {}
    mm_c, v_c = [], []
    nch = (n + 511) // 512

    @block.sync
    def _(sync):
        ins = sync.dma_start(out=vrow[:], in_=vec_dram_row)
        hold["ld"] = s_lb.inc(ins, 16)

    v0 = s_v.v
    v_pred = [v0 + c + 1 for c in range(nch)]

    @block.tensor
    def _(t):
        t.wait_ge(s_lb.h, hold["ld"])
        t.wait_ge(s_v.h, ones_ready)
        for c in range(nch):
            w = min(512, n - 512 * c)
            if c >= 2:
                t.wait_ge(s_v.h, v_pred[c - 2])
            ins = t.matmul(pb[c % 2][:, 0:w], ones1[:, 0:128],
                           vrow[:, 512 * c:512 * c + w], start=True, stop=True)
            mm_c.append(s_mm.inc(ins, 1))

    @block.vector
    def _(v):
        for c in range(nch):
            w = min(512, n - 512 * c)
            v.wait_ge(s_mm.h, mm_c[c])
            ins = v.tensor_scalar(out_sb[:, 512 * c:512 * c + w], pb[c % 2][:, 0:w],
                                  scale, None, ALU.mult)
            v_c.append(s_v.inc(ins, 1))

    assert v_c == v_pred, (v_c, v_pred)
    return v_c[-1]



def emit_mm_main(p, block, es, xlnT, tr_all, w_colblock_dma, n_out,
                 extra_mm=None, extra_ready=0, tag="mm", epi=None, nk=NDT):
    s_lw = [p.sem("s_lw0"), p.sem("s_lw1")]
    s_lxo = [p.sem(f"s_lxo{i}") for i in range(4)]
    s_mm = p.sem("s_mm")
    s_v = p.sem("s_v")
    s_s = p.sem("s_s")
    s_st = [p.sem("s_st0"), p.sem("s_st1")]
    s_trd = p.sem("s_trd")

    NOB = n_out // 512
    wcb = [p.sb(f"{tag}_wcb{i}", [128, nk * 512], BF16, es) for i in range(2)]
    ps = [p.psum(f"{tag}_ps{i}", [128, 512], F32, es=es) for i in range(2)]
    tmp = [p.sb(f"{tag}_tmp{i}", [128, 512], F32, es) for i in range(2)]
    xnew = [p.sb(f"{tag}_xn{i}", [128, 512], F32, es) for i in range(2)]
    xo = [p.sb(f"{tag}_xo{i}", [128, 512], F32, es) for i in range(4)]

    groups = [(ob, tt) for ob in range(NOB) for tt in range(NTT)]
    NG = len(groups)
    scale, bias_b, act = epi["scale"], epi.get("bias_b"), epi.get("act")
    assert act in (None, "gelu")

    mm0, v0, s0 = s_mm.v, s_v.v, s_s.v
    lw0 = [s.v for s in s_lw]
    lxo0 = [s.v for s in s_lxo]
    st0 = [s.v for s in s_st]
    mm_p = [mm0 + g + 1 for g in range(NG)]
    v1_p = [v0 + 2 * g + 1 for g in range(NG)]
    v2_p = [v0 + 2 * g + 2 for g in range(NG)]
    sact_p = [s0 + g + 1 for g in range(NG)]
    lw_p = [lw0[ob % 2] + 16 * (ob // 2 + 1) for ob in range(NOB)]
    lxo_p = [lxo0[g % 4] + 16 * (g // 4 + 1) for g in range(NG)]
    st_p = [st0[g % 2] + 16 * (g // 2 + 1) for g in range(NG)]

    def emit_store(sync, g):
        ob, tt = groups[g]
        sync.wait_ge(s_v.h, v2_p[g])
        ins = sync.dma_start(
            out=epi["xout"][tt * 128:(tt + 1) * 128, ob * 512:(ob + 1) * 512],
            in_=xnew[g % 2][:])
        assert s_st[g % 2].inc(ins, 16) == st_p[g]

    @block.sync
    def _(sync):
        for ob in range(NOB):
            if ob == 0:
                sync.wait_ge(s_trd.h, tr_all)
            if ob >= 2:
                for g in range((ob - 2) * NTT, (ob - 1) * NTT):
                    emit_store(sync, g)
                sync.wait_ge(s_mm.h, mm_p[(ob - 2) * NTT + NTT - 1])
            w_colblock_dma(sync, ob, wcb[ob % 2], s_lw[ob % 2])
            assert s_lw[ob % 2].v == lw_p[ob]
        for g in range((NOB - 2) * NTT, NOB * NTT):
            emit_store(sync, g)

    def emit_xo(s, g):
        ob, tt = groups[g]
        if g == 0:
            s.wait_ge(s_trd.h, tr_all)
        if g >= 4:
            s.wait_ge(s_v.h, v2_p[g - 4])
        ins = s.dma_start(
            out=xo[g % 4][:],
            in_=epi["xold"][tt * 128:(tt + 1) * 128, ob * 512:(ob + 1) * 512])
        assert s_lxo[g % 4].inc(ins, 16) == lxo_p[g]

    def emit_gelu(s, g):
        s.wait_ge(s_v.h, v1_p[g])
        ins = s.activation(tmp[g % 2][:], tmp[g % 2][:], AF.Gelu)
        assert s_s.inc(ins, 1) == sact_p[g]

    @block.scalar
    def _(s):
        if act == "gelu":
            for g in range(4):
                emit_xo(s, g)
            for g in range(NG):
                emit_gelu(s, g)
                if g + 4 < NG:
                    emit_xo(s, g + 4)
        else:
            for g in range(NG):
                emit_xo(s, g)

    @block.tensor
    def _(t):
        t.wait_ge(s_trd.h, tr_all)
        if extra_ready:
            t.wait_ge(s_v.h, extra_ready)
        for g, (ob, tt) in enumerate(groups):
            if tt == 0:
                t.wait_ge(s_lw[ob % 2].h, lw_p[ob])
            if g >= 2:
                t.wait_ge(s_v.h, v1_p[g - 2])
            pg = ps[g % 2]
            for dt in range(nk):
                last = (dt == nk - 1) and (extra_mm is None)
                ins = t.matmul(pg[:],
                               xlnT[:, dt * TOK + tt * 128: dt * TOK + (tt + 1) * 128],
                               wcb[ob % 2][:, dt * 512:(dt + 1) * 512],
                               start=(dt == 0), stop=last)
            if extra_mm is not None:
                ins = extra_mm(t, ob, tt, pg)
            assert s_mm.inc(ins, 1) == mm_p[g]

    @block.vector
    def _(v):
        if epi.get("bias_ready"):
            v.wait_ge(s_v.h, epi["bias_ready"])
        for g, (ob, tt) in enumerate(groups):
            v.wait_ge(s_mm.h, mm_p[g])
            pg = ps[g % 2]
            if bias_b is not None:
                ins = v.scalar_tensor_tensor(
                    tmp[g % 2][:], pg[:], scale,
                    bias_b[:, ob * 512:(ob + 1) * 512], ALU.mult, ALU.add)
            else:
                ins = v.tensor_scalar(tmp[g % 2][:], pg[:], scale, None, ALU.mult)
            assert s_v.inc(ins, 1) == v1_p[g]
            v.wait_ge(s_v.h, v1_p[g])
            if act == "gelu":
                v.wait_ge(s_s.h, sact_p[g])
            v.wait_ge(s_lxo[g % 4].h, lxo_p[g])
            if g >= 2:
                v.wait_ge(s_st[g % 2].h, st_p[g - 2])
            ins = v.tensor_tensor(xnew[g % 2][:], tmp[g % 2][:], xo[g % 4][:],
                                  ALU.add)
            assert s_v.inc(ins, 1) == v2_p[g]

    return [(s_st[0], st_p[-2]), (s_st[1], st_p[-1])] if NG % 2 == 0 else None



def emit_step1(p, block, io, tag="s1", prev_done=None, share=None):
    nc = p.nc
    cores = p.cores
    OSL = D // cores

    s_gp = p.sem("s_gp")
    s_trw = p.sem("s_trw")
    s_shw = p.sem("s_shw")
    s_mm = p.sem("s_mm")
    s_v = p.sem("s_v")
    s_cc = p.sem("s_cc")
    s_trd = p.sem("s_trd")

    es = ExitStack()
    with es:
        xlnT = p.sb(f"{tag}_xlnT", [128, NDT * TOK], BF16, es)
        A_T = p.sb(f"{tag}_AT", [128, NDT * 128], BF16, es)
        B_T = p.sb(f"{tag}_BT", [128, D], BF16, es)
        hT = p.sb(f"{tag}_hT", [128, TOK], BF16, es)
        bias_h = p.sb(f"{tag}_biash", [128, D], F32, es)
        coef = p.sb(f"{tag}_coef", [128, 1], F32, es)
        ones1 = p.sb(f"{tag}_ones1", [1, 128], F32, es)
        lw_sb = p.sb(f"{tag}_lwsb", [128, 1], F32, es)
        ps_h = p.psum(f"{tag}_psh", [128, 512], F32, es=es)

        hold = {}

        @block.vector
        def _(v):
            ins = v.memset(ones1[:], 1.0)
            hold["ones"] = s_v.inc(ins, 1)

        s_gpw = p.sem("s_gpw")
        s_gpl = p.sem("s_gpl")
        s_gplw = p.sem("s_gplw")

        if share is not None and "bar" in share:
            hold["casts"] = share["casts"]
            hold["bar"] = share["bar"]
        else:
            @block.gpsimd
            def _(gp):
                ins = gp.dma_start(out=io["WTsl_bf"], in_=io["WTsl"])
                hold["wcast"] = s_gpw.inc(ins, 16)
                ins = gp.dma_start(out=io["AT_bf"], in_=io["loraAT"])
                s_gpl.inc(ins, 16)
                ins = gp.dma_start(out=io["BT_bf"], in_=io["loraBT"])
                hold["casts"] = s_gpl.inc(ins, 16)

            hold["bar"] = emit_allgather(p, block, io["WTsl_bf"], io["sh_WT"],
                                         wait_gp=(s_gpw, hold["wcast"]))
            if share is not None:
                share["casts"] = hold["casts"]
                share["bar"] = hold["bar"]

        @block.sync
        def _(sync):
            sync.wait_ge(p.sem("s_gpl").h, hold["casts"])
            ins = sync.dma_start(
                out=A_T[:].rearrange("p (dt r) -> p dt r", r=128),
                in_=io["AT_bf"].rearrange("(dt p) r -> p dt r", p=128))
            s_trw.inc(ins, 16)
            ins = sync.dma_start(out=B_T[:], in_=io["BT_bf"])
            hold["lora_T"] = s_trw.inc(ins, 16)

        @block.gpsimd
        def _(gp):
            if prev_done:
                for _s, _val in prev_done:
                    gp.wait_ge(_s.h, _val)
            ins = gp.dma_start(out=lw_sb[:], in_=io["lora_w"])
            hold["lw"] = s_gplw.inc(ins, 16)

        @block.vector
        def _(v):
            v.wait_ge(s_gplw.h, hold["lw"])
            ins = v.tensor_scalar(coef[:], lw_sb[:], 2.0, None, ALU.mult)
            hold["coef"] = s_v.inc(ins, 1)

        bias_ready = emit_bcast_row(p, block, es, io["bias"], D, bias_h, 0.5,
                                    f"{tag}b", ones1, hold["ones"])

        with ExitStack() as es_ln:
            tr_all, _ = emit_ln(p, block, es_ln, io["x_in"], io["xln_tm"], xlnT,
                                f"{tag}ln", x_ready=prev_done)

        hT_mm_p = [s_mm.v + 1, s_mm.v + 2]
        hT_v_p = [s_v.v + 1, s_v.v + 2]

        @block.tensor
        def _(t):
            t.wait_ge(s_trw.h, hold["lora_T"])
            t.wait_ge(s_trd.h, tr_all)
            for th in range(2):
                if th == 1:
                    t.wait_ge(s_v.h, hT_v_p[0])
                for dt in range(NDT):
                    ins = t.matmul(
                        ps_h[:],
                        A_T[:, dt * 128:(dt + 1) * 128],
                        xlnT[:, dt * TOK + th * 512: dt * TOK + (th + 1) * 512],
                        start=(dt == 0), stop=(dt == NDT - 1))
                assert s_mm.inc(ins, 1) == hT_mm_p[th]

        @block.vector
        def _(v):
            v.wait_ge(s_v.h, hold["coef"])
            for th in range(2):
                v.wait_ge(s_mm.h, hT_mm_p[th])
                ins = v.tensor_scalar(hT[:, th * 512:(th + 1) * 512], ps_h[:],
                                      coef[:, 0:1], None, ALU.mult)
                assert s_v.inc(ins, 1) == hT_v_p[th]

        def w_colblock(sync, ob, dest, sem):
            if ob == 0:
                sync.wait_ge(s_cc.h, hold["bar"])
            srcap = io["sh_WT"].rearrange("(dt p) o -> p dt o", p=128)[:, :, ob * 512:(ob + 1) * 512]
            ins = sync.dma_start(out=dest[:].rearrange("p (dt j) -> p dt j", j=512),
                                 in_=srcap)
            sem.inc(ins, 16)

        def lora_mm(t, ob, tt, pg):
            return t.matmul(pg[:], hT[:, tt * 128:(tt + 1) * 128],
                            B_T[:, ob * 512:(ob + 1) * 512],
                            start=False, stop=True)

        st_last = emit_mm_main(
            p, block, es, xlnT, tr_all, w_colblock, D,
            extra_mm=lora_mm, extra_ready=hT_v_p[-1], tag=tag,
            epi=dict(scale=0.5, bias_b=bias_h, act=None,
                     xold=io["x_in"], xout=io["x_out"],
                     bias_ready=bias_ready))

        @block.sync
        def _(sync):
            for sem, val in st_last:
                sync.wait_ge(sem.h, val)

    return st_last




def emit_step5(p, block, io, tag="s5", prev_done=None):
    nc = p.nc
    cores = p.cores
    TSL = D // cores

    s_gp = p.sem("s_gp")
    s_gpw = p.sem("s_gpw")
    s_gpl = p.sem("s_gpl")
    s_mm = p.sem("s_mm")
    s_v = p.sem("s_v")
    s_cc = p.sem("s_cc")
    s_trd = p.sem("s_trd")

    es = ExitStack()
    with es:
        xlnT = p.sb(f"{tag}_xlnT", [128, NDT * TOK], BF16, es)
        bias_b = p.sb(f"{tag}_biasb", [128, D], F32, es)
        ones1 = p.sb(f"{tag}_ones1", [1, 128], F32, es)
        hold = {}

        @block.vector
        def _(v):
            ins = v.memset(ones1[:], 1.0)
            hold["ones"] = s_v.inc(ins, 1)

        @block.gpsimd
        def _(gp):
            ins = gp.dma_start(out=io["w1sl_bf"], in_=io["w1sl"])
            hold["w1cast"] = s_gpw.inc(ins, 16)
            ins = gp.dma_start(out=io["w2Tsl_bf"], in_=io["w2Tsl"])
            hold["w2cast"] = s_gpl.inc(ins, 16)

        hold["agw1"] = emit_allgather(p, block, io["w1sl_bf"], io["sh_w1"],
                                      wait_gp=(s_gpw, hold["w1cast"]))

        bias_ready = emit_bcast_row(p, block, es, io["b2"], D, bias_b, 1.0,
                                    f"{tag}b", ones1, hold["ones"])

        NST = D // 128
        s_lb = [p.sem("s_lb0"), p.sem("s_lb1")]
        s_msk = [p.sem("s_msk0"), p.sem("s_msk1")]
        s_w2r = p.sem("s_w2r")
        s_asw = [p.sem("s_asw0"), p.sem("s_asw1")]

        es_b = ExitStack()
        w2T_sb = p.sb(f"{tag}_w2T", [128, NDT * TSL], BF16, es_b)
        w1cb = [p.sb(f"{tag}_w1cb{i}", [128, NDT * 128], BF16, es_b)
                for i in range(2)]
        msk = [p.sb(f"{tag}_msk{i}", [128, TSL], F32, es_b) for i in range(2)]
        asl = [p.sb(f"{tag}_asl{i}", [128, TSL], BF16, es_b) for i in range(2)]
        ps_b = [p.psum(f"{tag}_psb{i}", [128, 512], F32, es=es_b)
                for i in range(2)]

        with ExitStack() as es_ln:
            tr_all, _ = emit_ln(p, block, es_ln, io["x_in"], io["xln_tm"], xlnT,
                                f"{tag}ln", x_ready=prev_done)

        if True:

            mm0b, v0b = s_mm.v, s_v.v
            mmb_p = [mm0b + st * (TSL // 512) + c + 1
                     for st in range(NST) for c in range(TSL // 512)]
            vb_p = [v0b + st + 1 for st in range(NST)]
            lb_p = [s_lb[st % 2].v + 16 * (st // 2 + 1) for st in range(NST)]
            msk_p = [s_msk[st % 2].v + 16 * (st // 2 + 1) for st in range(NST)]
            asw_p = [s_asw[st % 2].v + 16 * (st // 2 + 1) for st in range(NST)]

            @block.sync
            def _(sync):
                if prev_done:
                    for sem, val in prev_done:
                        sync.wait_ge(sem.h, val)
                sync.wait_ge(s_gpl.h, hold["w2cast"])
                ins = sync.dma_start(
                    out=w2T_sb[:].rearrange("p (dt j) -> p dt j", j=TSL),
                    in_=io["w2Tsl_bf"].rearrange("(dt p) j -> p dt j", p=128))
                hold["w2r"] = s_w2r.inc(ins, 16)
                sync.wait_ge(s_cc.h, hold["agw1"])
                for st in range(NST):
                    if st >= 2:
                        sync.wait_ge(s_mm.h, mmb_p[(st - 2) * (TSL // 512) + TSL // 512 - 1])
                        sync.wait_ge(s_v.h, vb_p[st - 2])
                        ins = sync.dma_start(
                            out=io["attn_sl"][(st - 2) * 128:(st - 1) * 128, :],
                            in_=asl[st % 2][:])
                        assert s_asw[st % 2].inc(ins, 16) == asw_p[st - 2]
                    ins = sync.dma_start(
                        out=w1cb[st % 2][:].rearrange("p (dt j) -> p dt j", j=128),
                        in_=io["sh_w1"].rearrange("(dt p) s -> p dt s", p=128)[:, :, st * 128:(st + 1) * 128])
                    assert s_lb[st % 2].inc(ins, 16) == lb_p[st]
                    ins = sync.dma_start(out=msk[st % 2][:],
                                         in_=io["mask_sl"][st * 128:(st + 1) * 128, :])
                    assert s_msk[st % 2].inc(ins, 16) == msk_p[st]
                for st in (NST - 2, NST - 1):
                    sync.wait_ge(s_mm.h, mmb_p[st * (TSL // 512) + TSL // 512 - 1])
                    sync.wait_ge(s_v.h, vb_p[st])
                    ins = sync.dma_start(
                        out=io["attn_sl"][st * 128:(st + 1) * 128, :],
                        in_=asl[st % 2][:])
                    assert s_asw[st % 2].inc(ins, 16) == asw_p[st]

            @block.tensor
            def _(t):
                t.wait_ge(s_w2r.h, hold["w2r"])
                for st in range(NST):
                    t.wait_ge(s_lb[st % 2].h, lb_p[st])
                    if st >= 2:
                        t.wait_ge(s_v.h, vb_p[st - 2])
                    for c in range(TSL // 512):
                        for dt in range(NDT):
                            ins = t.matmul(
                                ps_b[st % 2][:],
                                w1cb[st % 2][:, dt * 128:(dt + 1) * 128],
                                w2T_sb[:, dt * TSL + c * 512: dt * TSL + (c + 1) * 512],
                                start=(dt == 0), stop=(dt == NDT - 1))
                        assert s_mm.inc(ins, 1) == mmb_p[st * (TSL // 512) + c]

            @block.vector
            def _(v):
                for st in range(NST):
                    v.wait_ge(s_mm.h, mmb_p[st * (TSL // 512) + TSL // 512 - 1])
                    v.wait_ge(s_msk[st % 2].h, msk_p[st])
                    if st >= 2:
                        v.wait_ge(s_asw[st % 2].h, asw_p[st - 2])
                    for c in range(TSL // 512):
                        ins = v.tensor_tensor(
                            asl[st % 2][:, c * 512:(c + 1) * 512],
                            ps_b[st % 2][:, 0:512] if TSL == 512 else ps_b[st % 2][:],
                            msk[st % 2][:, c * 512:(c + 1) * 512], ALU.mult)
                    assert s_v.inc(ins, 1) == vb_p[st]

            @block.gpsimd
            def _(gp):
                gp.wait_ge(s_asw[0].h, asw_p[NST - 2])
                gp.wait_ge(s_asw[1].h, asw_p[NST - 1])
            hold["agattn"] = emit_allgather(p, block, io["attn_sl"], io["sh_attn"])

        es_b.close()

        def w_colblock(sync, ob, dest, sem):
            if ob == 0:
                sync.wait_ge(s_cc.h, hold["agattn"])
            srcap = io["sh_attn"].rearrange("(b dt p) j -> b p dt j", b=cores, p=128)[ob]
            ins = sync.dma_start(out=dest[:].rearrange("p (dt j) -> p dt j", j=512),
                                 in_=srcap)
            sem.inc(ins, 16)

        st_last = emit_mm_main(
            p, block, es, xlnT, tr_all, w_colblock, D,
            tag=tag,
            epi=dict(scale=1.0, bias_b=bias_b, act="gelu",
                     xold=io["x_in"], xout=io["x_out"],
                     bias_ready=bias_ready))

        @block.sync
        def _(sync):
            for sem, val in st_last:
                sync.wait_ge(sem.h, val)

    return st_last


def build_step5_nc(cores=8):
    nc = bass.Bass("TRN2", target_bir_lowering=False)
    TSL = D // cores
    io = {}
    io["x_in"] = nc.dram_tensor("x_in", [TOK, D], F32, kind="ExternalInput").ap()
    io["x_out"] = nc.dram_tensor("x_out", [TOK, D], F32, kind="ExternalOutput").ap()
    io["w1sl"] = nc.dram_tensor("w1sl", [TSL, D], F32, kind="ExternalInput").ap()
    io["w2Tsl"] = nc.dram_tensor("w2Tsl", [D, TSL], F32, kind="ExternalInput").ap()
    io["mask_sl"] = nc.dram_tensor("mask_sl", [D, TSL], F32, kind="ExternalInput").ap()
    io["b2"] = nc.dram_tensor("b2", [1, D], F32, kind="ExternalInput").ap()

    io["w1sl_bf"] = nc.dram_tensor("w1sl_bf", [TSL, D], BF16).ap()
    io["w2Tsl_bf"] = nc.dram_tensor("w2Tsl_bf", [D, TSL], BF16).ap()
    io["attn_sl"] = nc.dram_tensor("attn_sl", [D, TSL], BF16).ap()
    io["xln_tm"] = nc.dram_tensor("xln_tm", [TOK, D], BF16).ap()
    io["sh_w1"] = nc.dram_tensor("sh_w1", [D, D], BF16, addr_space="Shared").ap()
    io["sh_attn"] = nc.dram_tensor("sh_attn", [cores * D, TSL], BF16,
                                   addr_space="Shared").ap()

    with ExitStack() as es:
        p = P(nc, es, cores)
        block = es.enter_context(nc.Block())
        emit_step5(p, block, io)
    return nc


def step5_in_maps(inputs, x_full, cores=8):
    w1 = np.asarray(inputs["L1_attn_w1"], np.float32)
    w2 = np.asarray(inputs["L1_attn_w2"], np.float32)
    mask = np.asarray(inputs["L1_attn_sparse_mask"], np.float32)
    w2T = np.ascontiguousarray(w2.T)
    TSL = D // cores
    maps = []
    for k in range(cores):
        maps.append(dict(
            x_in=np.ascontiguousarray(x_full[k * TOK:(k + 1) * TOK]),
            w1sl=np.ascontiguousarray(w1[k * TSL:(k + 1) * TSL]),
            w2Tsl=np.ascontiguousarray(w2T[:, k * TSL:(k + 1) * TSL]),
            mask_sl=np.ascontiguousarray(mask[:, k * TSL:(k + 1) * TSL]),
            b2=np.ascontiguousarray(
                np.asarray(inputs["L1_attn_b2"], np.float32).reshape(1, D)),
        ))
    return maps




def emit_step4(p, block, io, tag="s4", prev_done=None, share=None):
    nc = p.nc
    cores = p.cores
    NH = H_HEADS

    s_gp = p.sem("s_gp")
    s_gpw = p.sem("s_gpw")
    s_gpl = p.sem("s_gpl")
    s_gpp = p.sem("s_gpp")
    s_mm = p.sem("s_mm")
    s_v = p.sem("s_v")
    s_s = p.sem("s_s")
    s_cc = p.sem("s_cc")
    s_trd = p.sem("s_trd")
    s_gx = [p.sem("s_gx0"), p.sem("s_gx1")]
    s_lwu = [p.sem("s_lwu0"), p.sem("s_lwu1")]
    s_gs = [p.sem("s_gs0"), p.sem("s_gs1")]
    s_dl = [p.sem("s_dl0"), p.sem("s_dl1"), p.sem("s_dl2"), p.sem("s_dl3")]
    s_axo = [p.sem("s_axo0"), p.sem("s_axo1")]
    s_ast = [p.sem("s_ast0"), p.sem("s_ast1")]
    s_gpc = p.sem("s_gpc")

    es = ExitStack()
    with es:
        xlnT = p.sb(f"{tag}_xlnT", [128, NDT * TOK], BF16, es)
        sperm = p.sb(f"{tag}_perm", [128, 32], I32, es)
        ident = p.sb(f"{tag}_ident", [128, 128], F32, es)
        xg = [[p.sb(f"{tag}_xg{i}{j}", [128, TOK], BF16, es) for j in range(2)]
              for i in range(2)]
        ub = [p.sb(f"{tag}_ub{i}", [128, 2 * H_M], BF16, es) for i in range(2)]
        db = [p.sb(f"{tag}_db{i}", [128, 8 * H_DH], BF16, es) for i in range(2)]
        ug = [p.sb(f"{tag}_ug{i}", [128, 8 * TOK], BF16, es) for i in range(2)]
        dnT = [[p.sb(f"{tag}_dn{i}{j}", [128, TOK], F32, es) for j in range(2)]
               for i in range(2)]
        ps_u = [p.psum(f"{tag}_psu{i}", [128, 512], F32, es=es) for i in range(2)]
        ps_d = [p.psum(f"{tag}_psd{i}", [128, 512], F32, es=es) for i in range(2)]
        ps_t = [p.psum(f"{tag}_pst{i}", [128, 128], F32, es=es) for i in range(2)]
        hold = {}

        @block.vector
        def _(v):
            ins = v.memset(ident[:], 1.0)
            hold["id1"] = s_v.inc(ins, 1)

        if share is not None and "agd" in share:
            @block.gpsimd
            def _(gp):
                gp.wait_ge(s_v.h, hold["id1"])
                ins = gp.affine_select(ident[:], ident[:], [[-1, 128]],
                                       ALU.is_equal, 0.0, base=0,
                                       channel_multiplier=1)
                hold["ident"] = s_gpc.inc(ins, 1)
                if prev_done:
                    for _s, _val in prev_done:
                        gp.wait_ge(_s.h, _val)
                ins = gp.dma_start(out=sperm[:], in_=io["perm_t"])
                hold["perm"] = s_gpp.inc(ins, 16)
            hold["agu"] = share["agu"]
            hold["agd"] = share["agd"]
        else:
            @block.gpsimd
            def _(gp):
                ins = gp.dma_start(out=io["upsl_bf"], in_=io["upsl"])
                hold["upcast"] = s_gpw.inc(ins, 16)
                ins = gp.dma_start(out=io["dnsl_bf"], in_=io["dnsl"])
                hold["dncast"] = s_gpl.inc(ins, 16)
                gp.wait_ge(s_v.h, hold["id1"])
                ins = gp.affine_select(ident[:], ident[:], [[-1, 128]],
                                       ALU.is_equal, 0.0, base=0,
                                       channel_multiplier=1)
                hold["ident"] = s_gpc.inc(ins, 1)
                if prev_done:
                    for _s, _val in prev_done:
                        gp.wait_ge(_s.h, _val)
                ins = gp.dma_start(out=sperm[:], in_=io["perm_t"])
                hold["perm"] = s_gpp.inc(ins, 16)

            hold["agu"] = emit_allgather(p, block, io["upsl_bf"], io["sh_up"],
                                         wait_gp=(s_gpw, hold["upcast"]))
            hold["agd"] = emit_allgather(p, block, io["dnsl_bf"], io["sh_dn"],
                                         wait_gp=(s_gpl, hold["dncast"]))
            if share is not None:
                share["agu"] = hold["agu"]
                share["agd"] = hold["agd"]

        with ExitStack() as es_ln:
            tr_all, _ = emit_ln(p, block, es_ln, io["x_in"], io["xln_tm"], xlnT,
                                f"{tag}ln", fm_dram=io["xln_fm"], x_ready=prev_done)

        mm0, v0, s0 = s_mm.v, s_v.v, s_s.v
        gx0 = [s.v for s in s_gx]
        lwu0 = [s.v for s in s_lwu]
        gs0 = [s.v for s in s_gs]
        up_p = {}
        dn_p = {}
        gelu_p = {}
        dnv_p = {}
        for h in range(NH):
            for k in range(16):
                up_p[(h, k)] = mm0 + 20 * h + k + 1
            for j in range(4):
                dn_p[(h, j)] = mm0 + 20 * h + 16 + j + 1
            for k in range(16):
                gelu_p[(h, k)] = s0 + 16 * h + k + 1
            for j in range(4):
                dnv_p[(h, j)] = v0 + 4 * h + j + 1
        gx_p = [gx0[h % 2] + 32 * (h // 2 + 1) for h in range(NH)]
        lwu_p = [lwu0[h % 2] + 32 * (h // 2 + 1) for h in range(NH)]
        gs_p = [gs0[h % 2] + 32 * (h // 2 + 1) for h in range(NH)]

        @block.gpsimd
        def _(gp):
            gp.wait_ge(s_gpp.h, hold["perm"])
            gp.wait_ge(s_trd.h, tr_all)
            for h in range(NH):
                if h >= 2:
                    gp.wait_ge(s_mm.h, up_p[(h - 2, 15)])
                for i in range(2):
                    ins = gp.indirect_dma_start(
                        xg[h % 2][i][:], None, io["xln_fm"],
                        IndirectOffsetOnAxis(sperm[:, 2 * h + i:2 * h + i + 1], 0))
                    s_gx[h % 2].inc(ins, 16)
                assert s_gx[h % 2].v == gx_p[h]
                if h >= 1:
                    hh = h - 1
                    gp.wait_ge(s_v.h, dnv_p[(hh, 3)])
                    for i in range(2):
                        ins = gp.indirect_dma_start(
                            io["deltaT"],
                            IndirectOffsetOnAxis(sperm[:, 2 * hh + i:2 * hh + i + 1], 0),
                            dnT[hh % 2][i][:], None)
                        s_gs[hh % 2].inc(ins, 16)
                    assert s_gs[hh % 2].v == gs_p[hh]
            hh = NH - 1
            gp.wait_ge(s_v.h, dnv_p[(hh, 3)])
            for i in range(2):
                ins = gp.indirect_dma_start(
                    io["deltaT"],
                    IndirectOffsetOnAxis(sperm[:, 2 * hh + i:2 * hh + i + 1], 0),
                    dnT[hh % 2][i][:], None)
                s_gs[hh % 2].inc(ins, 16)
            assert s_gs[hh % 2].v == gs_p[hh]

        @block.sync
        def _(sync):
            sync.wait_ge(s_cc.h, hold["agd"])
            for h in range(NH):
                if h >= 2:
                    sync.wait_ge(s_mm.h, dn_p[(h - 2, 3)])
                ins = sync.dma_start(
                    out=ub[h % 2][:].rearrange("p (dt m) -> p dt m", m=H_M),
                    in_=io["sh_up"].rearrange("(hh dt p) m -> hh p dt m", hh=NH, p=128)[h])
                s_lwu[h % 2].inc(ins, 16)
                ins = sync.dma_start(
                    out=db[h % 2][:].rearrange("p (mt d) -> p mt d", d=H_DH),
                    in_=io["sh_dn"].rearrange("(hh mt p) d -> hh p mt d", hh=NH, p=128)[h])
                s_lwu[h % 2].inc(ins, 16)
                assert s_lwu[h % 2].v == lwu_p[h]

        @block.tensor
        def _(t):
            for h in range(NH):
                t.wait_ge(s_gx[h % 2].h, gx_p[h])
                t.wait_ge(s_lwu[h % 2].h, lwu_p[h])
                for mt in range(8):
                    for th in range(2):
                        k = mt * 2 + th
                        if 20 * h + k + 1 > 2:
                            pass
                        if k >= 2:
                            t.wait_ge(s_s.h, gelu_p[(h, k - 2)])
                        elif h >= 1:
                            t.wait_ge(s_s.h, gelu_p[(h - 1, 14 + k)])
                        for dt in range(2):
                            ins = t.matmul(
                                ps_u[k % 2][:],
                                ub[h % 2][:, dt * H_M + mt * 128: dt * H_M + (mt + 1) * 128],
                                xg[h % 2][dt][:, th * 512:(th + 1) * 512],
                                start=(dt == 0), stop=(dt == 1))
                        assert s_mm.inc(ins, 1) == up_p[(h, k)]
                t.wait_ge(s_s.h, gelu_p[(h, 15)])
                for dtile in range(2):
                    for th in range(2):
                        j = dtile * 2 + th
                        if j >= 2:
                            t.wait_ge(s_v.h, dnv_p[(h, j - 2)])
                        elif h >= 1:
                            t.wait_ge(s_v.h, dnv_p[(h - 1, 2 + j)])
                        for mt in range(8):
                            ins = t.matmul(
                                ps_d[j % 2][:],
                                db[h % 2][:, mt * H_DH + dtile * 128: mt * H_DH + (dtile + 1) * 128],
                                ug[h % 2][:, mt * TOK + th * 512: mt * TOK + (th + 1) * 512],
                                start=(mt == 0), stop=(mt == 7))
                        assert s_mm.inc(ins, 1) == dn_p[(h, j)]

        @block.scalar
        def _(s):
            for h in range(NH):
                for mt in range(8):
                    for th in range(2):
                        k = mt * 2 + th
                        s.wait_ge(s_mm.h, up_p[(h, k)])
                        if k == 0 and h >= 2:
                            s.wait_ge(s_mm.h, dn_p[(h - 2, 3)])
                        ins = s.activation(
                            ug[h % 2][:, mt * TOK + th * 512: mt * TOK + (th + 1) * 512],
                            ps_u[k % 2][:], AF.Gelu)
                        assert s_s.inc(ins, 1) == gelu_p[(h, k)]

        @block.vector
        def _(v):
            for h in range(NH):
                for dtile in range(2):
                    for th in range(2):
                        j = dtile * 2 + th
                        v.wait_ge(s_mm.h, dn_p[(h, j)])
                        if j == 0 and h >= 2:
                            v.wait_ge(s_gs[h % 2].h, gs_p[h - 2])
                        ins = v.tensor_scalar(
                            dnT[h % 2][dtile][:, th * 512:(th + 1) * 512],
                            ps_d[j % 2][:], 0.5, None, ALU.mult)
                        assert s_v.inc(ins, 1) == dnv_p[(h, j)]

        TBT = 2
        NTB = NTT // TBT
        dld = [p.sb(f"{tag}_dld{i}", [128, TBT * 128], F32, es) for i in range(4)]
        xo_t = [p.sb(f"{tag}_xot{i}", [128, D], F32, es) for i in range(2)]

        mmA0, vA0 = s_mm.v, s_v.v
        dl0 = [s.v for s in s_dl]
        axo0 = [s.v for s in s_axo]
        ast0 = [s.v for s in s_ast]
        trp = {}
        vap = {}
        for tb in range(NTB):
            for vt in range(NDT):
                for i in range(TBT):
                    idx = tb * NDT * TBT + vt * TBT + i
                    trp[(tb, vt, i)] = mmA0 + idx + 1
                    vap[(tb, vt, i)] = vA0 + idx + 1
        dl_p = {}
        for tb in range(NTB):
            for vt in range(NDT):
                dl_p[(tb, vt)] = dl0[vt % 4] + 16 * (tb * (NDT // 4) + vt // 4 + 1)
        axo_p = {}
        ast_p = {}
        for tb in range(NTB):
            for i in range(TBT):
                axo_p[(tb, i)] = axo0[i] + 16 * (tb + 1)
                ast_p[(tb, i)] = ast0[i] + 16 * (tb + 1)

        @block.sync
        def _(sync):
            sync.wait_ge(s_gs[0].h, gs_p[NH - 2])
            sync.wait_ge(s_gs[1].h, gs_p[NH - 1])
            for tb in range(NTB):
                for vt in range(0, NDT, 2):
                    idx4 = tb * NDT + vt
                    if idx4 >= 4:
                        ptb, pv = divmod(idx4 - 4, NDT)
                        sync.wait_ge(s_mm.h, trp[(ptb, pv, TBT - 1)])
                    ins = sync.dma_start(
                        out=dld[vt % 4][:],
                        in_=io["deltaT"][vt * 128:(vt + 1) * 128,
                                         tb * TBT * 128:(tb + 1) * TBT * 128])
                    assert s_dl[vt % 4].inc(ins, 16) == dl_p[(tb, vt)]
                for i in range(TBT):
                    sync.wait_ge(s_v.h, vap[(tb, NDT - 1, i)])
                    tt = tb * TBT + i
                    ins = sync.dma_start(
                        out=io["x_out"][tt * 128:(tt + 1) * 128, :],
                        in_=xo_t[i][:])
                    assert s_ast[i].inc(ins, 16) == ast_p[(tb, i)]

        @block.scalar
        def _(s):
            s.wait_ge(s_gs[0].h, gs_p[NH - 2])
            s.wait_ge(s_gs[1].h, gs_p[NH - 1])
            for tb in range(NTB):
                for i in range(TBT):
                    if tb >= 1:
                        s.wait_ge(s_ast[i].h, ast_p[(tb - 1, i)])
                    tt = tb * TBT + i
                    ins = s.dma_start(out=xo_t[i][:],
                                      in_=io["x_in"][tt * 128:(tt + 1) * 128, :])
                    assert s_axo[i].inc(ins, 16) == axo_p[(tb, i)]
                for vt in range(1, NDT, 2):
                    idx4 = tb * NDT + vt
                    if idx4 >= 4:
                        ptb, pv = divmod(idx4 - 4, NDT)
                        s.wait_ge(s_mm.h, trp[(ptb, pv, TBT - 1)])
                    ins = s.dma_start(
                        out=dld[vt % 4][:],
                        in_=io["deltaT"][vt * 128:(vt + 1) * 128,
                                         tb * TBT * 128:(tb + 1) * TBT * 128])
                    assert s_dl[vt % 4].inc(ins, 16) == dl_p[(tb, vt)]

        @block.tensor
        def _(t):
            t.wait_ge(s_gpc.h, hold["ident"])
            for tb in range(NTB):
                for vt in range(NDT):
                    t.wait_ge(s_dl[vt % 4].h, dl_p[(tb, vt)])
                    for i in range(TBT):
                        idx = tb * NDT * TBT + vt * TBT + i
                        if idx >= 2:
                            pidx = idx - 2
                            ptb, r = divmod(pidx, NDT * TBT)
                            pv, pi = divmod(r, TBT)
                            t.wait_ge(s_v.h, vap[(ptb, pv, pi)])
                        ins = t.transpose(ps_t[idx % 2][:],
                                          dld[vt % 4][:, i * 128:(i + 1) * 128],
                                          ident[:])
                        assert s_mm.inc(ins, 1) == trp[(tb, vt, i)]

        @block.vector
        def _(v):
            for tb in range(NTB):
                for vt in range(NDT):
                    for i in range(TBT):
                        idx = tb * NDT * TBT + vt * TBT + i
                        v.wait_ge(s_mm.h, trp[(tb, vt, i)])
                        if vt == 0:
                            v.wait_ge(s_axo[i].h, axo_p[(tb, i)])
                        ins = v.tensor_tensor(
                            xo_t[i][:, vt * 128:(vt + 1) * 128],
                            ps_t[idx % 2][:],
                            xo_t[i][:, vt * 128:(vt + 1) * 128], ALU.add)
                        assert s_v.inc(ins, 1) == vap[(tb, vt, i)]

        @block.sync
        def _(sync):
            sync.wait_ge(s_ast[0].h, ast_p[(NTB - 1, 0)])
            sync.wait_ge(s_ast[1].h, ast_p[(NTB - 1, 1)])

    return [(s_ast[0], ast_p[(NTB - 1, 0)]), (s_ast[1], ast_p[(NTB - 1, 1)])]


def build_step4_nc(cores=8):
    nc = bass.Bass("TRN2", target_bir_lowering=False)
    HSL = H_HEADS // cores
    io = {}
    io["x_in"] = nc.dram_tensor("x_in", [TOK, D], F32, kind="ExternalInput").ap()
    io["x_out"] = nc.dram_tensor("x_out", [TOK, D], F32, kind="ExternalOutput").ap()
    io["upsl"] = nc.dram_tensor("upsl", [HSL * H_DH, H_M], F32, kind="ExternalInput").ap()
    io["dnsl"] = nc.dram_tensor("dnsl", [HSL * H_M, H_DH], F32, kind="ExternalInput").ap()
    io["perm_t"] = nc.dram_tensor("perm_t", [128, 32], I32, kind="ExternalInput").ap()

    io["upsl_bf"] = nc.dram_tensor("upsl_bf", [HSL * H_DH, H_M], BF16).ap()
    io["dnsl_bf"] = nc.dram_tensor("dnsl_bf", [HSL * H_M, H_DH], BF16).ap()
    io["xln_tm"] = nc.dram_tensor("xln_tm", [TOK, D], BF16).ap()
    io["xln_fm"] = nc.dram_tensor("xln_fm", [D, TOK], BF16).ap()
    io["deltaT"] = nc.dram_tensor("deltaT", [D, TOK], F32).ap()
    io["sh_up"] = nc.dram_tensor("sh_up", [H_HEADS * H_DH, H_M], BF16,
                                 addr_space="Shared").ap()
    io["sh_dn"] = nc.dram_tensor("sh_dn", [H_HEADS * H_M, H_DH], BF16,
                                 addr_space="Shared").ap()

    with ExitStack() as es:
        p = P(nc, es, cores)
        block = es.enter_context(nc.Block())
        emit_step4(p, block, io)
    return nc


def step4_in_maps(inputs, x_full, cores=8):
    up = np.asarray(inputs["L1_ffn_up"], np.float32)
    dn = np.asarray(inputs["L1_ffn_down"], np.float32)
    perm = np.asarray(inputs["L1_ffn_perm"], np.int32)
    perm_t = np.ascontiguousarray(perm.reshape(32, 128).T.astype(np.int32))
    HSL = H_HEADS // cores
    maps = []
    for k in range(cores):
        maps.append(dict(
            x_in=np.ascontiguousarray(x_full[k * TOK:(k + 1) * TOK]),
            upsl=np.ascontiguousarray(
                up[k * HSL:(k + 1) * HSL].reshape(HSL * H_DH, H_M)),
            dnsl=np.ascontiguousarray(
                dn[k * HSL:(k + 1) * HSL].reshape(HSL * H_M, H_DH)),
            perm_t=perm_t,
        ))
    return maps



def emit_step2(p, block, io, tag="s2", prev_done=None):
    nc = p.nc
    cores = p.cores
    AD = A_DIM
    NCT = AD // 128
    NFT = A_DFF // 128

    s_gpw = p.sem("s_gpw")
    s_gpl = p.sem("s_gpl")
    s_gpt = p.sem("s_gpt")
    s_gpz = p.sem("s_gpz")
    s_gpc = p.sem("s_gpc")
    s_mm = p.sem("s_mm")
    s_v = p.sem("s_v")
    s_s = p.sem("s_s")
    s_cc = p.sem("s_cc")
    s_trd = p.sem("s_trd")
    s_lwa = p.sem("s_lwa")
    s_ytm = [p.sem("s_ytm0"), p.sem("s_ytm1")]
    s_ytr = p.sem("s_ytr")
    s_tab = p.sem("s_tab")
    s_el = p.sem("s_el")
    s_sml = p.sem("s_sml")

    def vedge(v, ins):
        v.wait_ge(s_v.h, s_v.inc(ins, 1))

    def sedge(s, ins):
        s.wait_ge(s_s.h, s_s.inc(ins, 1))

    es = ExitStack()
    with es:
        h = [p.sb(f"{tag}_h{t}", [128, AD], F32, es) for t in range(NTT)]
        ident = p.sb(f"{tag}_ident", [128, 128], F32, es)
        ones1 = p.sb(f"{tag}_ones1", [1, 128], F32, es)
        onesw = p.sb(f"{tag}_onesw", [128, TOK], F32, es)
        zbre = p.sb(f"{tag}_zbre", [128, 2], F32, es)
        zbim = p.sb(f"{tag}_zbim", [128, 2], F32, es)
        zre_t = p.sb(f"{tag}_zret", [128, 16], F32, es)
        zim_t = p.sb(f"{tag}_zimt", [128, 16], F32, es)
        lc_t = p.sb(f"{tag}_lct", [128, 32], F32, es)
        scl_t = p.sb(f"{tag}_sclt", [128, 16], F32, es)
        b1_t = p.sb(f"{tag}_b1t", [128, 64], F32, es)
        qsel = p.sb(f"{tag}_qsel", [128, 4], F32, es)
        c_t1 = p.sb(f"{tag}_ct1", [128, 16], F32, es)
        c_t2 = p.sb(f"{tag}_ct2", [128, 16], F32, es)
        zabs = p.sb(f"{tag}_zabs", [128, 16], F32, es)
        rmagA = p.sb(f"{tag}_rmagA", [128, 16], F32, es)
        urA = p.sb(f"{tag}_urA", [128, 16], F32, es)
        uiA = p.sb(f"{tag}_uiA", [128, 16], F32, es)
        bin_b = p.sb(f"{tag}_binb", [128, AD], F32, es)
        lnlw_b = p.sb(f"{tag}_lnlwb", [128, AD], F32, es)
        lnlb_b = p.sb(f"{tag}_lnlbb", [128, AD], F32, es)
        vrow = p.sb(f"{tag}_vrow", [1, AD], F32, es)
        hfT = p.sb(f"{tag}_hfT", [128, NCT * TOK], BF16, es)
        yscr = [p.sb(f"{tag}_yscr{i}", [128, AD], F32, es) for i in range(2)]
        y_tm = [p.sb(f"{tag}_ytmb{i}", [128, AD], BF16, es) for i in range(2)]
        ident_bf = p.sb(f"{tag}_idbf", [128, 128], BF16, es)
        yT = [p.sb(f"{tag}_yT{i}", [128, TOK], BF16, es) for i in range(4)]
        stat6 = p.sb(f"{tag}_st6", [128, 6], F32, es)
        agg2 = p.sb(f"{tag}_agg2", [128, 2 * NTT], F32, es)
        vep = p.sb(f"{tag}_vep", [128, NTT], F32, es)
        sdv = p.sb(f"{tag}_sdv", [128, NTT], F32, es)
        rsv = p.sb(f"{tag}_rsv", [128, NTT], F32, es)
        nbv = p.sb(f"{tag}_nbv", [128, NTT], F32, es)
        tscr = p.sb(f"{tag}_tscr", [128, 8], F32, es)
        pb = [p.psum(f"{tag}_bps{i}", [128, 512], F32, es=es) for i in range(2)]
        hold = {}

        @block.vector
        def _(v):
            v.memset(ones1[:], 1.0)
            v.memset(onesw[:], 1.0)
            ins = v.memset(ident[:], 1.0)
            hold["id1"] = s_v.inc(ins, 1)

        @block.gpsimd
        def _(gp):
            ins = gp.dma_start(out=io["WinTsl_bf"], in_=io["WinTsl"])
            hold["wincast"] = s_gpw.inc(ins, 16)
            ins = gp.dma_start(out=io["W1Tsl_bf"], in_=io["W1Tsl"])
            s_gpl.inc(ins, 16)
            ins = gp.dma_start(out=io["W2Tsl_bf"], in_=io["W2Tsl"])
            hold["wcast"] = s_gpl.inc(ins, 16)
            ins = gp.dma_start(out=io["WoutTsl_bf"], in_=io["WoutTsl"])
            hold["woutcast"] = s_gpt.inc(ins, 16)
            gp.wait_ge(s_v.h, hold["id1"])
            ins = gp.affine_select(ident[:], ident[:], [[-1, 128]], ALU.is_equal,
                                   0.0, base=0, channel_multiplier=1)
            hold["ident"] = s_gpc.inc(ins, 1)
            gp.wait_ge(s_gpc.h, hold["ident"])
            ins = gp.tensor_copy(ident_bf[:], ident[:])
            hold["ident_bf"] = s_gpc.inc(ins, 1)
            if prev_done:
                for _s, _val in prev_done:
                    gp.wait_ge(_s.h, _val)
            for name, dst in [("zbre", zbre), ("zbim", zbim), ("zre_t", zre_t),
                              ("zim_t", zim_t), ("lc_t", lc_t), ("scl_t", scl_t),
                              ("b1_t", b1_t), ("qsel", qsel)]:
                ins = gp.dma_start(out=dst[:], in_=io[name])
                s_gpz.inc(ins, 16)
            hold["smalls"] = s_gpz.v

        hold["ag_win"] = emit_allgather(p, block, io["WinTsl_bf"], io["sh_WinT"],
                                        wait_gp=(s_gpw, hold["wincast"]))
        hold["ag_w"] = emit_allgather(p, block, io["W1Tsl_bf"], io["sh_W1T"],
                                      wait_gp=(s_gpl, hold["wcast"]))
        hold["ag_w2"] = emit_allgather(p, block, io["W2Tsl_bf"], io["sh_W2T"])
        hold["ag_wout"] = emit_allgather(p, block, io["WoutTsl_bf"], io["sh_WoutT"],
                                         wait_gp=(s_gpt, hold["woutcast"]))

        @block.vector
        def _(v):
            v.wait_ge(s_gpz.h, hold["smalls"])
            ins = v.tensor_tensor(c_t1[:], zre_t[:], zre_t[:], ALU.mult)
            vedge(v, ins)
            ins = v.tensor_tensor(c_t2[:], zim_t[:], zim_t[:], ALU.mult)
            vedge(v, ins)
            ins = v.tensor_tensor(c_t1[:], c_t1[:], c_t2[:], ALU.add)
            hold["zn2"] = s_v.inc(ins, 1)

        @block.scalar
        def _(s):
            s.wait_ge(s_v.h, hold["zn2"])
            ins = s.activation(zabs[:], c_t1[:], AF.Sqrt)
            sedge(s, ins)
            ins = s.activation(rmagA[:], zabs[:], AF.Tanh)
            hold["rmag"] = s_s.inc(ins, 1)

        @block.vector
        def _(v):
            v.wait_ge(s_s.h, hold["rmag"])
            ins = v.reciprocal(c_t2[:], zabs[:])
            vedge(v, ins)
            ins = v.tensor_tensor(urA[:], zre_t[:], c_t2[:], ALU.mult)
            vedge(v, ins)
            ins = v.tensor_tensor(uiA[:], zim_t[:], c_t2[:], ALU.mult)
            hold["uvec"] = s_v.inc(ins, 1)

        def bcast_row(t_sec, v_sec, src_row, dst, n, scale, idx):
            raise RuntimeError("inline only")

        bc_jobs0 = [(io["bin_r"], bin_b, AD), (io["lnlw_r"], lnlw_b, AD),
                    (io["lnlb_r"], lnlb_b, AD)]
        bc0_mm = [s_mm.v + ji + 1 for ji in range(3)]
        bc0_v = [s_v.v + ji + 1 for ji in range(3)]
        bc0_l = []

        @block.sync
        def _(sync):
            for ji, (src, dst, n) in enumerate(bc_jobs0):
                if ji >= 1:
                    sync.wait_ge(s_mm.h, bc0_mm[ji - 1])
                ins = sync.dma_start(out=vrow[:, 0:n], in_=src)
                bc0_l.append(s_sml.inc(ins, 16))

        @block.tensor
        def _(t):
            t.wait_ge(s_v.h, hold["id1"])
            for ji, (src, dst, n) in enumerate(bc_jobs0):
                t.wait_ge(s_sml.h, bc0_l[ji])
                if ji >= 2:
                    t.wait_ge(s_v.h, bc0_v[ji - 2])
                ins = t.matmul(pb[ji % 2][:], ones1[:, 0:128], vrow[:, 0:n],
                               start=True, stop=True)
                assert s_mm.inc(ins, 1) == bc0_mm[ji]

        @block.vector
        def _(v):
            for ji, (src, dst, n) in enumerate(bc_jobs0):
                v.wait_ge(s_mm.h, bc0_mm[ji])
                ins = v.tensor_copy(dst[:], pb[ji % 2][:])
                assert s_v.inc(ins, 1) == bc0_v[ji]

        es_e = ExitStack()
        xlnT = p.sb(f"{tag}_xlnT", [128, NDT * TOK], BF16, es_e)
        wincb = p.sb(f"{tag}_wincb", [128, NDT * 512], BF16, es_e)
        with ExitStack() as es_ln:
            tr_all, _ = emit_ln(p, block, es_ln, io["x_in"], io["xln_tm"],
                                xlnT, f"{tag}ln", x_ready=prev_done)

        ent_mm_p = [s_mm.v + g + 1 for g in range(NTT)]
        ent_v_p = [s_v.v + g + 1 for g in range(NTT)]
        hold["win_l"] = s_lwa.v + 16

        @block.sync
        def _(sync):
            sync.wait_ge(s_cc.h, hold["ag_win"])
            sync.wait_ge(s_trd.h, tr_all)
            ins = sync.dma_start(
                out=wincb[:].rearrange("p (dt j) -> p dt j", j=512),
                in_=io["sh_WinT"].rearrange("(dt p) j -> p dt j", p=128))
            assert s_lwa.inc(ins, 16) == hold["win_l"]

        ps_a = [p.psum(f"{tag}_psa{i}", [128, 512], F32, es=es) for i in range(2)]
        ps_t = [p.psum(f"{tag}_pst{i}", [128, 128], F32, es=es) for i in range(2)]

        @block.tensor
        def _(t):
            t.wait_ge(s_lwa.h, hold["win_l"])
            for g in range(NTT):
                if g >= 2:
                    t.wait_ge(s_v.h, ent_v_p[g - 2])
                for dt in range(NDT):
                    ins = t.matmul(
                        ps_a[g % 2][:],
                        xlnT[:, dt * TOK + g * 128: dt * TOK + (g + 1) * 128],
                        wincb[:, dt * 512:(dt + 1) * 512],
                        start=(dt == 0), stop=(dt == NDT - 1))
                assert s_mm.inc(ins, 1) == ent_mm_p[g]
        ent_done = ent_mm_p[-1]

        @block.vector
        def _(v):
            v.wait_ge(s_v.h, bc0_v[0])
            for g in range(NTT):
                v.wait_ge(s_mm.h, ent_mm_p[g])
                ins = v.tensor_tensor(h[g][:], ps_a[g % 2][:], bin_b[:], ALU.add)
                assert s_v.inc(ins, 1) == ent_v_p[g]
        hold["h_ready"] = list(ent_v_p)
        es_e.close()

        es_L = ExitStack()
        ps_ln = [p.psum(f"{tag}_psln{i}", [128, 128], BF16, es=es_L)
                 for i in range(2)]
        lnst = {"pse": [0, 0], "ytm": [0, 0]}
        bc_stage = p.sb(f"{tag}_bcst", [128, 5 * AD], F32, es_L)
        Tm_r = p.sb(f"{tag}_Tmr", [128, 4], F32, es_L)
        Tm_i = p.sb(f"{tag}_Tmi", [128, 4], F32, es_L)
        Cr = p.sb(f"{tag}_Cr", [128, 4], F32, es_L)
        Ci = p.sb(f"{tag}_Ci", [128, 4], F32, es_L)
        p1r = p.sb(f"{tag}_p1r", [128, 4], F32, es_L)
        p1i = p.sb(f"{tag}_p1i", [128, 4], F32, es_L)
        pcr = p.sb(f"{tag}_pcr", [128, 4], F32, es_L)
        npci = p.sb(f"{tag}_npci", [128, 4], F32, es_L)
        e_sb = p.sb(f"{tag}_esb", [128, 8], F32, es_L)
        e_all = p.sb(f"{tag}_eall", [128, 32], F32, es_L)
        scanAB = p.sb(f"{tag}_scanAB", [128, 8 * TOK], F32, es_L)
        scw = p.sb(f"{tag}_scw", [128, 2 * TOK], F32, es_L)
        cvw = [p.sb(f"{tag}_cvw{i}", [128, TOK], F32, es_L) for i in range(2)]
        tabs = p.sb(f"{tag}_tabs", [128, 3 * TOK], F32, es_L)
        rconst = p.sb(f"{tag}_rconst", [128, TOK], F32, es_L)
        tb1 = p.sb(f"{tag}_tb1", [128, 512], F32, es_L)
        tb2 = p.sb(f"{tag}_tb2", [128, 512], F32, es_L)
        y3T = p.sb(f"{tag}_y3T", [128, NFT * TOK], BF16, es_L)
        wbuf = p.sb(f"{tag}_wbuf", [128, NCT * A_DFF], BF16, es_L)
        m1w = scw[:, 0:TOK]
        m2w = scw[:, TOK:2 * TOK]
        tabu = scanAB[:, 0:6 * TOK]

        hold["tb_s0"] = s_s.v
        @block.vector
        def _(v):
            v.wait_ge(s_s.h, hold["rmag"])
            v.wait_ge(s_v.h, hold["uvec"])
            v.wait_ge(s_v.h, ent_v_p[-1])
            v.wait_ge(s_trd.h, tr_all)
            for u in range(2):
                base = u * 3 * TOK
                co = tabu[:, base:base + TOK]
                si = tabu[:, base + TOK:base + 2 * TOK]
                rp = tabu[:, base + 2 * TOK:base + 3 * TOK]
                ins = v.tensor_tensor(tscr[:, 0:1], zbre[:, u:u + 1],
                                      zbre[:, u:u + 1], ALU.mult)
                vedge(v, ins)
                ins = v.tensor_tensor(tscr[:, 1:2], zbim[:, u:u + 1],
                                      zbim[:, u:u + 1], ALU.mult)
                vedge(v, ins)
                ins = v.tensor_tensor(tscr[:, 0:1], tscr[:, 0:1], tscr[:, 1:2],
                                      ALU.add)
                hold[f"tbz_{u}"] = s_v.inc(ins, 1)
                v.wait_ge(s_s.h, hold["tb_s0"] + 2 * (u + 1))
                ins = v.reciprocal(tscr[:, 4:5], tscr[:, 2:3])
                vedge(v, ins)
                ins = v.tensor_tensor(tscr[:, 5:6], zbre[:, u:u + 1],
                                      tscr[:, 4:5], ALU.mult)
                vedge(v, ins)
                ins = v.tensor_tensor(tscr[:, 6:7], zbim[:, u:u + 1],
                                      tscr[:, 4:5], ALU.mult)
                vedge(v, ins)
                ins = v.memset(co[:, 0:1], 1.0)
                vedge(v, ins)
                ins = v.memset(si[:, 0:1], 0.0)
                vedge(v, ins)
                ins = v.memset(rp[:, 0:1], 1.0)
                vedge(v, ins)
                ins = v.tensor_copy(co[:, 1:2], tscr[:, 5:6])
                vedge(v, ins)
                ins = v.tensor_copy(si[:, 1:2], tscr[:, 6:7])
                vedge(v, ins)
                ins = v.tensor_copy(rp[:, 1:2], tscr[:, 3:4])
                vedge(v, ins)
                w = 2
                while w < TOK:
                    ins = v.tensor_tensor(tscr[:, 0:1], co[:, w - 1:w],
                                          co[:, 1:2], ALU.mult)
                    vedge(v, ins)
                    ins = v.tensor_tensor(tscr[:, 1:2], si[:, w - 1:w],
                                          si[:, 1:2], ALU.mult)
                    vedge(v, ins)
                    ins = v.tensor_tensor(co[:, w:w + 1], tscr[:, 0:1],
                                          tscr[:, 1:2], ALU.subtract)
                    vedge(v, ins)
                    ins = v.tensor_tensor(tscr[:, 0:1], si[:, w - 1:w],
                                          co[:, 1:2], ALU.mult)
                    vedge(v, ins)
                    ins = v.tensor_tensor(tscr[:, 1:2], co[:, w - 1:w],
                                          si[:, 1:2], ALU.mult)
                    vedge(v, ins)
                    ins = v.tensor_tensor(si[:, w:w + 1], tscr[:, 0:1],
                                          tscr[:, 1:2], ALU.add)
                    vedge(v, ins)
                    ins = v.tensor_tensor(rp[:, w:w + 1], rp[:, w - 1:w],
                                          rp[:, 1:2], ALU.mult)
                    vedge(v, ins)
                    nb = w - 1
                    if nb > 0:
                        ins = v.tensor_scalar(tb1[:, 0:nb], co[:, 1:w],
                                              co[:, w:w + 1], None, ALU.mult)
                        vedge(v, ins)
                        ins = v.tensor_scalar(tb2[:, 0:nb], si[:, 1:w],
                                              si[:, w:w + 1], None, ALU.mult)
                        vedge(v, ins)
                        ins = v.tensor_tensor(co[:, w + 1:2 * w], tb1[:, 0:nb],
                                              tb2[:, 0:nb], ALU.subtract)
                        vedge(v, ins)
                        ins = v.tensor_scalar(tb1[:, 0:nb], si[:, 1:w],
                                              co[:, w:w + 1], None, ALU.mult)
                        vedge(v, ins)
                        ins = v.tensor_scalar(tb2[:, 0:nb], co[:, 1:w],
                                              si[:, w:w + 1], None, ALU.mult)
                        vedge(v, ins)
                        ins = v.tensor_tensor(si[:, w + 1:2 * w], tb1[:, 0:nb],
                                              tb2[:, 0:nb], ALU.add)
                        vedge(v, ins)
                        ins = v.tensor_scalar(rp[:, w + 1:2 * w], rp[:, 1:w],
                                              rp[:, w:w + 1], None, ALU.mult)
                        vedge(v, ins)
                    w *= 2
            ins = v.tensor_copy(tscr[:, 7:8], tscr[:, 6:7])
            hold["tab_done"] = s_v.inc(ins, 1)

        @block.scalar
        def _(s):
            assert s_s.v == hold["tb_s0"]
            for u in range(2):
                s.wait_ge(s_v.h, hold[f"tbz_{u}"])
                ins = s.activation(tscr[:, 2:3], tscr[:, 0:1], AF.Sqrt)
                sedge(s, ins)
                ins = s.activation(tscr[:, 3:4], tscr[:, 2:3], AF.Tanh)
                s_s.inc(ins, 1)

        tabw_p = s_el.v + 16

        @block.sync
        def _(sync):
            sync.wait_ge(s_v.h, hold["tab_done"])
            ins = sync.dma_start(
                out=io["tabloc"].rearrange("(ut p) j -> p ut j", p=128),
                in_=tabu[:].rearrange("p (ut j) -> p ut j", j=TOK))
            assert s_el.inc(ins, 16) == tabw_p

        @block.gpsimd
        def _(gp):
            gp.wait_ge(s_el.h, tabw_p)
        hold["ag_tab"] = emit_allgather(p, block, io["tabloc"], io["sh_tab"])
        def inner_ln(wb, bb, do_silu, h_ready):
            v0, s0 = s_v.v, s_s.v
            stats_p = [v0 + 3 * tt + 3 for tt in range(NTT)]
            sqrt_p = s0 + 1
            rs_p = v0 + 3 * NTT + 3
            norm_p = [s0 + 1 + 2 * tt + 1 for tt in range(NTT)]
            act_p = [s0 + 1 + 2 * tt + 2 for tt in range(NTT)]
            aff_p = [rs_p + 2 * tt + 4 * max(tt - 1, 0) + 2 for tt in range(NTT)]
            mm0t = s_mm.v
            tr_p = {(tt, ct): mm0t + NCT * tt + ct + 1
                    for tt in range(NTT) for ct in range(NCT)}
            ev_p = {}
            for e in range(NTT):
                base = aff_p[e + 1] if e < NTT - 1 else aff_p[NTT - 1] + 4
                for ct in range(NCT):
                    ev_p[(e, ct)] = base + ct + 1
            ev_flat = [ev_p[(i // NCT, i % NCT)] for i in range(NTT * NCT)]

            @block.vector
            def _(v):
                for tt in range(NTT):
                    v.wait_ge(s_v.h, h_ready[tt])
                    ins = v.bn_stats(stat6[:], h[tt][:])
                    vedge(v, ins)
                    ins = v.bn_aggr(agg2[:, 2 * tt:2 * tt + 2], stat6[:])
                    vedge(v, ins)
                    ins = v.tensor_scalar(vep[:, tt:tt + 1],
                                          agg2[:, 2 * tt + 1:2 * tt + 2],
                                          EPS, None, ALU.add)
                    assert s_v.inc(ins, 1) == stats_p[tt]
                v.wait_ge(s_s.h, sqrt_p)
                ins = v.reciprocal(rsv[:], sdv[:])
                vedge(v, ins)
                means = agg2[:].rearrange("p (t two) -> p two t", two=2)[:, 0, :]
                ins = v.tensor_tensor(nbv[:], means, rsv[:], ALU.mult)
                vedge(v, ins)
                ins = v.tensor_scalar(nbv[:], nbv[:], -1.0, None, ALU.mult)
                assert s_v.inc(ins, 1) == rs_p
                def evac_group(v, e):
                    for ct in range(NCT):
                        v.wait_ge(s_mm.h, tr_p[(e, ct)])
                        ins = v.tensor_copy(yT[ct][:, e * 128:(e + 1) * 128],
                                            ps_ln[(NCT * e + ct) % 2][:])
                        assert s_v.inc(ins, 1) == ev_p[(e, ct)]

                for tt in range(NTT):
                    v.wait_ge(s_s.h, norm_p[tt])
                    ins = v.tensor_tensor(yscr[tt % 2][:], yscr[tt % 2][:],
                                          wb, ALU.mult)
                    vedge(v, ins)
                    ins = v.tensor_tensor(yscr[tt % 2][:], yscr[tt % 2][:],
                                          bb, ALU.add)
                    assert s_v.inc(ins, 1) == aff_p[tt]
                    if tt >= 1:
                        evac_group(v, tt - 1)
                evac_group(v, NTT - 1)

            @block.scalar
            def _(s):
                s.wait_ge(s_v.h, stats_p[-1])
                ins = s.activation(sdv[:], vep[:], AF.Sqrt)
                assert s_s.inc(ins, 1) == sqrt_p
                for tt in range(NTT):
                    if tt == 0:
                        s.wait_ge(s_v.h, rs_p)
                        s.wait_ge(s_v.h, h_ready[-1])
                    if tt >= 2:
                        s.wait_ge(s_s.h, act_p[tt - 2])
                        s.wait_ge(s_mm.h, tr_p[(tt - 2, NCT - 1)])
                    else:
                        s.wait_ge(s_mm.h, lnst["ytm"][tt % 2])
                    ins = s.activation(yscr[tt % 2][:], h[tt][:], AF.Identity,
                                       bias=nbv[:, tt:tt + 1],
                                       scale=rsv[:, tt:tt + 1])
                    assert s_s.inc(ins, 1) == norm_p[tt]
                    s.wait_ge(s_v.h, aff_p[tt])
                    ins = s.activation(y_tm[tt % 2][:], yscr[tt % 2][:],
                                       AF.Silu if do_silu else AF.Identity)
                    assert s_s.inc(ins, 1) == act_p[tt]

            @block.tensor
            def _(t):
                t.wait_ge(s_gpc.h, hold["ident_bf"])
                for tt in range(NTT):
                    t.wait_ge(s_s.h, act_p[tt])
                    for ct in range(NCT):
                        idx = NCT * tt + ct
                        if idx >= 2:
                            t.wait_ge(s_v.h, ev_flat[idx - 2])
                        else:
                            t.wait_ge(s_v.h, lnst["pse"][idx % 2])
                        ins = t.transpose(
                            ps_ln[idx % 2][:],
                            y_tm[tt % 2][:, ct * 128:(ct + 1) * 128], ident_bf[:])
                        assert s_mm.inc(ins, 1) == tr_p[(tt, ct)]

            lnst["pse"] = [ev_p[(NTT - 1, NCT - 2)], ev_p[(NTT - 1, NCT - 1)]]
            lnst["ytm"] = [tr_p[(NTT - 2, NCT - 1)], tr_p[(NTT - 1, NCT - 1)]]
            return ev_p[(NTT - 1, NCT - 1)]

        def bcast5(l, rows_done_guard):
            srcs = [io["ln1w"], io["ln1b"], io["ln2w"], io["ln2b"], io["b2r"]]
            mm_c = [s_mm.v + ji + 1 for ji in range(5)]
            v_c = [s_v.v + ji + 1 for ji in range(5)]
            l_c = []

            @block.sync
            def _(sync):
                for ji in range(5):
                    if ji >= 1:
                        sync.wait_ge(s_mm.h, mm_c[ji - 1])
                    else:
                        sync.wait_ge(s_mm.h, vrow_guard[0])
                        if rows_done_guard is not None:
                            sync.wait_ge(s_v.h, rows_done_guard)
                    ins = sync.dma_start(out=vrow[:, 0:AD],
                                         in_=srcs[ji][l:l + 1, :])
                    l_c.append(s_sml.inc(ins, 16))

            @block.tensor
            def _(t):
                for ji in range(5):
                    t.wait_ge(s_sml.h, l_c[ji])
                    if ji >= 2:
                        t.wait_ge(s_v.h, v_c[ji - 2])
                    elif rows_done_guard is not None:
                        t.wait_ge(s_v.h, rows_done_guard)
                    ins = t.matmul(pb[ji % 2][:], ones1[:, 0:128],
                                   vrow[:, 0:AD], start=True, stop=True)
                    assert s_mm.inc(ins, 1) == mm_c[ji]

            @block.vector
            def _(v):
                for ji in range(5):
                    v.wait_ge(s_mm.h, mm_c[ji])
                    ins = v.tensor_copy(bc_stage[:, ji * AD:(ji + 1) * AD],
                                        pb[ji % 2][:])
                    assert s_v.inc(ins, 1) == v_c[ji]

            vrow_guard[0] = mm_c[-1]
            return v_c[-1]

        prev_ffn_done = ent_done
        prev_bc_guard = ent_v_p[-1]
        hup2_guard = [ent_v_p[-1]]
        y3T_guard = [ent_done]
        vrow_guard = [max(bc0_mm[-1], ent_done)]
        for l in range(4):
            bc_done = bcast5(l, prev_bc_guard)
            l1w_b = bc_stage[:, 0 * AD:1 * AD]
            l1b_b = bc_stage[:, 1 * AD:2 * AD]
            l2w_b = bc_stage[:, 2 * AD:3 * AD]
            l2b_b = bc_stage[:, 3 * AD:4 * AD]
            b2_b = bc_stage[:, 4 * AD:5 * AD]

            @block.vector
            def _(v):
                v.wait_ge(s_v.h, bc_done)

            ytr_p = inner_ln(l1w_b, l1b_b, True, hold["h_ready"])

            tab0 = s_tab.v
            tabl_p = [tab0 + 16 * (ct + 1) for ct in range(NCT)]
            v0 = s_v.v
            scan_p = {}
            for ct in range(NCT):
                for k in range(11):
                    scan_p[(ct, k)] = v0 + 11 * ct + k + 1
            escan_done = v0 + 44

            @block.sync
            def _(sync):
                for ct in range(NCT):
                    u = l * 4 + ct
                    if ct == 0:
                        sync.wait_ge(s_cc.h, hold["ag_tab"])
                        if l > 0:
                            sync.wait_ge(s_v.h, hold[f"fix_done_{l-1}"])
                        else:
                            sync.wait_ge(s_v.h, hold["tab_done"])
                    else:
                        sync.wait_ge(s_v.h, scan_p[(ct - 1, 10)])
                    src = io["sh_tab"][(u // 2) * 768 + (u % 2) * 384:
                                       (u // 2) * 768 + (u % 2) * 384 + 384, :]
                    ins = sync.dma_start(
                        out=tabs[:].rearrange("p (t j) -> p t j", j=TOK),
                        in_=src.rearrange("(t p) j -> p t j", p=128))
                    assert s_tab.inc(ins, 16) == tabl_p[ct]

            @block.vector
            def _(v):
                for ct in range(NCT):
                    u = l * 4 + ct
                    co = tabs[:, 0:TOK]
                    si = tabs[:, TOK:2 * TOK]
                    A = scanAB[:, (2 * ct) * TOK:(2 * ct + 1) * TOK]
                    B = scanAB[:, (2 * ct + 1) * TOK:(2 * ct + 2) * TOK]
                    v.wait_ge(s_tab.h, tabl_p[ct])
                    v.wait_ge(s_v.h, ytr_p)
                    ins = v.tensor_scalar(rconst[:], onesw[:],
                                          rmagA[:, u:u + 1], None, ALU.mult)
                    assert s_v.inc(ins, 1) == scan_p[(ct, 0)]
                    v.wait_ge(s_v.h, scan_p[(ct, 0)])
                    ins = v.tensor_tensor(scw[:, 0:TOK], yT[ct][:], co, ALU.mult)
                    assert s_v.inc(ins, 1) == scan_p[(ct, 1)]
                    ins = v.tensor_tensor(scw[:, TOK:2 * TOK], yT[ct][:], si,
                                          ALU.mult)
                    v.wait_ge(s_v.h, s_v.inc(ins, 1))
                    ins = v.tensor_tensor_scan(A, rconst[:], scw[:, 0:TOK],
                                               0.0, ALU.mult, ALU.add)
                    assert s_v.inc(ins, 1) == scan_p[(ct, 3)]
                    ins = v.tensor_tensor_scan(B, rconst[:], scw[:, TOK:2 * TOK],
                                               0.0, ALU.mult, ALU.add)
                    v.wait_ge(s_v.h, s_v.inc(ins, 1))
                    ins = v.tensor_tensor(tscr[:, 0:1], co[:, TOK - 1:TOK],
                                          A[:, TOK - 1:TOK], ALU.mult)
                    vedge(v, ins)
                    ins = v.tensor_tensor(tscr[:, 1:2], si[:, TOK - 1:TOK],
                                          B[:, TOK - 1:TOK], ALU.mult)
                    vedge(v, ins)
                    ins = v.tensor_tensor(e_sb[:, 2 * ct:2 * ct + 1],
                                          tscr[:, 0:1], tscr[:, 1:2], ALU.add)
                    vedge(v, ins)
                    ins = v.tensor_tensor(tscr[:, 0:1], si[:, TOK - 1:TOK],
                                          A[:, TOK - 1:TOK], ALU.mult)
                    vedge(v, ins)
                    ins = v.tensor_tensor(tscr[:, 1:2], co[:, TOK - 1:TOK],
                                          B[:, TOK - 1:TOK], ALU.mult)
                    vedge(v, ins)
                    ins = v.tensor_tensor(e_sb[:, 2 * ct + 1:2 * ct + 2],
                                          tscr[:, 0:1], tscr[:, 1:2],
                                          ALU.subtract)
                    assert s_v.inc(ins, 1) == scan_p[(ct, 10)]
            assert s_v.v == escan_done

            el_w = s_el.v + 16

            @block.sync
            def _(sync):
                sync.wait_ge(s_v.h, escan_done)
                ins = sync.dma_start(out=io[f"eloc{l}"], in_=e_sb[:])
                assert s_el.inc(ins, 16) == el_w

            @block.gpsimd
            def _(gp):
                gp.wait_ge(s_el.h, el_w)
                ins = gp.collective_compute(
                    "AllGather", ALU.bypass,
                    replica_groups=[[0, 1, 2, 3], [4, 5, 6, 7]],
                    ins=[io[f"eloc{l}"]], outs=[io[f"egath{l}"]])
                s_cc.inc(ins, 1)
            eg_done = s_cc.v
            el_r = s_el.v + 16

            @block.sync
            def _(sync):
                sync.wait_ge(s_cc.h, eg_done)
                ins = sync.dma_start(
                    out=e_all[:],
                    in_=io[f"egath{l}"].rearrange("(m p) c -> p m c", p=128))
                assert s_el.inc(ins, 16) == el_r

            @block.vector
            def _(v):
                v.wait_ge(s_el.h, el_r)
                ins = v.tensor_tensor(p1r[:], urA[:, 4 * l:4 * l + 4],
                                      rmagA[:, 4 * l:4 * l + 4], ALU.mult)
                vedge(v, ins)
                ins = v.tensor_tensor(p1i[:], uiA[:, 4 * l:4 * l + 4],
                                      rmagA[:, 4 * l:4 * l + 4], ALU.mult)
                vedge(v, ins)
                for _ in range(10):
                    ins = v.tensor_tensor(Tm_r[:], p1r[:], p1r[:], ALU.mult)
                    vedge(v, ins)
                    ins = v.tensor_tensor(Tm_i[:], p1i[:], p1i[:], ALU.mult)
                    vedge(v, ins)
                    ins = v.tensor_tensor(Cr[:], p1r[:], p1i[:], ALU.mult)
                    vedge(v, ins)
                    ins = v.tensor_tensor(p1r[:], Tm_r[:], Tm_i[:], ALU.subtract)
                    vedge(v, ins)
                    ins = v.tensor_tensor(p1i[:], Cr[:], Cr[:], ALU.add)
                    vedge(v, ins)
                lcr = lc_t[:].rearrange("p (l ct two) -> p l ct two", l=4, two=2)
                ins = v.tensor_copy(Tm_r[:], lcr[:, l, :, 0])
                vedge(v, ins)
                ins = v.tensor_copy(Tm_i[:], lcr[:, l, :, 1])
                vedge(v, ins)
                ins = v.memset(Cr[:], 0.0)
                vedge(v, ins)
                ins = v.memset(Ci[:], 0.0)
                vedge(v, ins)
                eview = e_all[:].rearrange("p (m ct two) -> p m ct two",
                                           m=4, two=2)
                for m in range(4):
                    ins = v.tensor_scalar(tscr[:, 0:4], Tm_r[:],
                                          qsel[:, m:m + 1], None, ALU.mult)
                    vedge(v, ins)
                    ins = v.tensor_tensor(Cr[:], Cr[:], tscr[:, 0:4], ALU.add)
                    vedge(v, ins)
                    ins = v.tensor_scalar(tscr[:, 0:4], Tm_i[:],
                                          qsel[:, m:m + 1], None, ALU.mult)
                    vedge(v, ins)
                    ins = v.tensor_tensor(Ci[:], Ci[:], tscr[:, 0:4], ALU.add)
                    vedge(v, ins)
                    if m < 3:
                        ins = v.tensor_tensor(tscr[:, 0:4], p1r[:], Tm_r[:],
                                              ALU.mult)
                        vedge(v, ins)
                        ins = v.tensor_tensor(tscr[:, 4:8], p1i[:], Tm_i[:],
                                              ALU.mult)
                        vedge(v, ins)
                        ins = v.tensor_tensor(tscr[:, 0:4], tscr[:, 0:4],
                                              tscr[:, 4:8], ALU.subtract)
                        vedge(v, ins)
                        ins = v.tensor_tensor(tscr[:, 4:8], p1r[:], Tm_i[:],
                                              ALU.mult)
                        vedge(v, ins)
                        ins = v.tensor_tensor(Tm_i[:], p1i[:], Tm_r[:], ALU.mult)
                        vedge(v, ins)
                        ins = v.tensor_tensor(Tm_i[:], Tm_i[:], tscr[:, 4:8],
                                              ALU.add)
                        vedge(v, ins)
                        ins = v.tensor_tensor(Tm_r[:], tscr[:, 0:4],
                                              eview[:, m, :, 0], ALU.add)
                        vedge(v, ins)
                        ins = v.tensor_tensor(Tm_i[:], Tm_i[:],
                                              eview[:, m, :, 1], ALU.add)
                        vedge(v, ins)
                ins = v.tensor_tensor(tscr[:, 0:4], urA[:, 4 * l:4 * l + 4],
                                      rmagA[:, 4 * l:4 * l + 4], ALU.mult)
                vedge(v, ins)
                ins = v.tensor_tensor(tscr[:, 4:8], uiA[:, 4 * l:4 * l + 4],
                                      rmagA[:, 4 * l:4 * l + 4], ALU.mult)
                vedge(v, ins)
                ins = v.tensor_tensor(pcr[:], tscr[:, 0:4], Cr[:], ALU.mult)
                vedge(v, ins)
                ins = v.tensor_tensor(npci[:], tscr[:, 4:8], Ci[:], ALU.mult)
                vedge(v, ins)
                ins = v.tensor_tensor(pcr[:], pcr[:], npci[:], ALU.subtract)
                vedge(v, ins)
                ins = v.tensor_tensor(npci[:], tscr[:, 0:4], Ci[:], ALU.mult)
                vedge(v, ins)
                ins = v.tensor_tensor(Tm_r[:], tscr[:, 4:8], Cr[:], ALU.mult)
                vedge(v, ins)
                ins = v.tensor_tensor(npci[:], npci[:], Tm_r[:], ALU.add)
                vedge(v, ins)
                ins = v.tensor_scalar(npci[:], npci[:], -1.0, None, ALU.mult)
                hold[f"carry_{l}"] = s_v.inc(ins, 1)

            tabr_p = [s_tab.v + 16 * (ct + 1) for ct in range(NCT)]
            v0f = s_v.v
            mm0c = s_mm.v
            trc_p = {}
            for ct in range(NCT):
                for tt in range(NTT):
                    trc_p[(ct, tt)] = mm0c + NTT * ct + tt + 1
            vorder = []
            for ct in range(NCT):
                vorder.append(("fix", ct))
                if ct >= 1:
                    vorder.append(("upd", ct - 1))
            vorder.append(("upd", NCT - 1))
            vbase = v0f
            fix_p = {}
            hupd_p = {}
            for kind, ct in vorder:
                if kind == "fix":
                    for k in range(6):
                        fix_p[(ct, k)] = vbase + k + 1
                    vbase += 6
                else:
                    for tt in range(NTT):
                        hupd_p[(ct, tt)] = vbase + tt + 1
                    vbase += NTT
            hold[f"fix_done_{l}"] = fix_p[(NCT - 1, 5)]

            @block.sync
            def _(sync):
                for ct in range(NCT):
                    u = l * 4 + ct
                    sync.wait_ge(s_v.h, hold[f"carry_{l}"])
                    if ct >= 1:
                        sync.wait_ge(s_v.h, fix_p[(ct - 1, 3)])
                    src = io["sh_tab"][(u // 2) * 768 + (u % 2) * 384:
                                       (u // 2) * 768 + (u % 2) * 384 + 384, :]
                    ins = sync.dma_start(
                        out=tabs[:].rearrange("p (t j) -> p t j", j=TOK),
                        in_=src.rearrange("(t p) j -> p t j", p=128))
                    assert s_tab.inc(ins, 16) == tabr_p[ct]

            @block.vector
            def _(v):
                for kind, ct in vorder:
                    if kind == "fix":
                        co = tabs[:, 0:TOK]
                        si = tabs[:, TOK:2 * TOK]
                        rp = tabs[:, 2 * TOK:3 * TOK]
                        A = scanAB[:, (2 * ct) * TOK:(2 * ct + 1) * TOK]
                        B = scanAB[:, (2 * ct + 1) * TOK:(2 * ct + 2) * TOK]
                        u = l * 4 + ct
                        v.wait_ge(s_tab.h, tabr_p[ct])
                        ins = v.scalar_tensor_tensor(A, rp, pcr[:, ct:ct + 1],
                                                     A, ALU.mult, ALU.add)
                        assert s_v.inc(ins, 1) == fix_p[(ct, 0)]
                        v.wait_ge(s_v.h, fix_p[(ct, 0)])
                        ins = v.scalar_tensor_tensor(B, rp, npci[:, ct:ct + 1],
                                                     B, ALU.mult, ALU.add)
                        v.wait_ge(s_v.h, s_v.inc(ins, 1))
                        ins = v.tensor_tensor(m1w, co, A, ALU.mult)
                        assert s_v.inc(ins, 1) == fix_p[(ct, 2)]
                        ins = v.tensor_tensor(m2w, si, B, ALU.mult)
                        v.wait_ge(s_v.h, s_v.inc(ins, 1))
                        if ct >= 2:
                            v.wait_ge(s_mm.h, trc_p[(ct - 2, NTT - 1)])
                        ins = v.tensor_tensor(cvw[ct % 2][:], m1w, m2w, ALU.add)
                        vedge(v, ins)
                        ins = v.tensor_scalar(cvw[ct % 2][:], cvw[ct % 2][:],
                                              scl_t[:, u:u + 1], None, ALU.mult)
                        assert s_v.inc(ins, 1) == fix_p[(ct, 5)]
                    else:
                        for tt in range(NTT):
                            v.wait_ge(s_mm.h, trc_p[(ct, tt)])
                            ins = v.tensor_tensor(
                                h[tt][:, ct * 128:(ct + 1) * 128],
                                ps_t[(NTT * ct + tt) % 2][:],
                                h[tt][:, ct * 128:(ct + 1) * 128], ALU.add)
                            assert s_v.inc(ins, 1) == hupd_p[(ct, tt)]

            @block.tensor
            def _(t):
                t.wait_ge(s_gpc.h, hold["ident"])
                for ct in range(NCT):
                    t.wait_ge(s_v.h, fix_p[(ct, 5)])
                    for tt in range(NTT):
                        idx = NTT * ct + tt
                        if idx >= 2:
                            pct, ptt = divmod(idx - 2, NTT)
                            t.wait_ge(s_v.h, hupd_p[(pct, ptt)])
                        ins = t.transpose(ps_t[idx % 2][:],
                                          cvw[ct % 2][:, tt * 128:(tt + 1) * 128],
                                          ident[:])
                        assert s_mm.inc(ins, 1) == trc_p[(ct, tt)]

            h_ready2 = [hupd_p[(NCT - 1, tt)] for tt in range(NTT)]
            hold["h_ready"] = h_ready2
            ytr2_p = inner_ln(l2w_b, l2b_b, False, h_ready2)

            w1l_p = s_lwa.v + 16
            w2l_p = s_lwa.v + 32
            mm0f = s_mm.v
            f1_p = {}
            for ft in range(NFT):
                for th in range(2):
                    f1_p[(ft, th)] = mm0f + 2 * ft + th + 1
            s0f = s_s.v
            sil_p = {}
            for ft in range(NFT):
                for th in range(2):
                    sil_p[(ft, th)] = s0f + 2 * ft + th + 1
            mm0g = mm0f + 32
            f2_p = [mm0g + tt + 1 for tt in range(NTT)]
            v0g = s_v.v
            hup2_p = [v0g + 2 * tt + 2 for tt in range(NTT)]

            @block.sync
            def _(sync):
                sync.wait_ge(s_cc.h, hold["ag_w"])
                sync.wait_ge(s_cc.h, hold["ag_w2"])
                sync.wait_ge(s_mm.h, prev_ffn_done)
                ins = sync.dma_start(
                    out=wbuf[:].rearrange("p (dt j) -> p dt j", j=A_DFF),
                    in_=io["sh_W1T"][l * AD:(l + 1) * AD, :].rearrange(
                        "(dt pp) m -> pp dt m", pp=128))
                assert s_lwa.inc(ins, 16) == w1l_p
                sync.wait_ge(s_mm.h, f1_p[(NFT - 1, 1)])
                ins = sync.dma_start(
                    out=wbuf[:, 0:NFT * AD].rearrange("p (ft j) -> p ft j", j=AD),
                    in_=io["sh_W2T"][l * A_DFF:(l + 1) * A_DFF, :].rearrange(
                        "(ft pp) m -> pp ft m", pp=128))
                assert s_lwa.inc(ins, 16) == w2l_p

            @block.tensor
            def _(t):
                t.wait_ge(s_lwa.h, w1l_p)
                t.wait_ge(s_v.h, ytr2_p)
                for ft in range(NFT):
                    for th in range(2):
                        k = 2 * ft + th
                        if k >= 2:
                            t.wait_ge(s_s.h, sil_p[((k - 2) // 2, (k - 2) % 2)])
                        else:
                            t.wait_ge(s_v.h, hup2_guard[0])
                        for dt in range(NCT):
                            ins = t.matmul(
                                ps_a[k % 2][:],
                                wbuf[:, dt * A_DFF + ft * 128:
                                     dt * A_DFF + (ft + 1) * 128],
                                yT[dt][:, th * 512:(th + 1) * 512],
                                start=(dt == 0), stop=(dt == NCT - 1))
                        assert s_mm.inc(ins, 1) == f1_p[(ft, th)]
                t.wait_ge(s_lwa.h, w2l_p)
                t.wait_ge(s_s.h, sil_p[(NFT - 1, 1)])
                for tt in range(NTT):
                    if tt >= 2:
                        t.wait_ge(s_v.h, hup2_p[tt - 2])
                    for ft in range(NFT):
                        ins = t.matmul(
                            ps_a[tt % 2][:],
                            y3T[:, ft * TOK + tt * 128: ft * TOK + (tt + 1) * 128],
                            wbuf[:, ft * AD:(ft + 1) * AD],
                            start=(ft == 0), stop=(ft == NFT - 1))
                    assert s_mm.inc(ins, 1) == f2_p[tt]
            prev_ffn_done = f2_p[-1]

            @block.scalar
            def _(s):
                for ft in range(NFT):
                    for th in range(2):
                        s.wait_ge(s_mm.h, f1_p[(ft, th)])
                        if ft == 0 and th == 0:
                            s.wait_ge(s_mm.h, y3T_guard[0])
                        ins = s.activation(
                            y3T[:, ft * TOK + th * 512: ft * TOK + (th + 1) * 512],
                            ps_a[(2 * ft + th) % 2][:], AF.Silu,
                            bias=b1_t[:, l * 16 + ft:l * 16 + ft + 1], scale=1.0)
                        assert s_s.inc(ins, 1) == sil_p[(ft, th)]

            @block.vector
            def _(v):
                for tt in range(NTT):
                    v.wait_ge(s_mm.h, f2_p[tt])
                    ins = v.tensor_tensor(yscr[tt % 2][:], ps_a[tt % 2][:],
                                          b2_b, ALU.add)
                    vedge(v, ins)
                    ins = v.tensor_tensor(h[tt][:], h[tt][:], yscr[tt % 2][:],
                                          ALU.add)
                    assert s_v.inc(ins, 1) == hup2_p[tt]
            hold["h_ready"] = list(hup2_p)
            prev_bc_guard = hup2_p[-1]
            hup2_guard = [hup2_p[-1]]
            y3T_guard = [prev_ffn_done]

        ytrf_p = inner_ln(lnlw_b[:], lnlb_b[:], False, hold["h_ready"])
        v0x = s_v.v
        hft_p = [v0x + ct + 1 for ct in range(NCT)]

        @block.vector
        def _(v):
            v.wait_ge(s_v.h, ytrf_p)
            for ct in range(NCT):
                ins = v.tensor_copy(hfT[:, ct * TOK:(ct + 1) * TOK], yT[ct][:])
                assert s_v.inc(ins, 1) == hft_p[ct]

        es_L.close()

        bout_b = p.sb(f"{tag}_boutb", [128, D], F32, es)
        NBC = D // 512
        bo_mm = [s_mm.v + c + 1 for c in range(NBC)]
        bo_v = [s_v.v + c + 1 for c in range(NBC)]
        bo_l = [s_sml.v + 16 * (c + 1) for c in range(NBC)]

        @block.sync
        def _(sync):
            for c in range(NBC):
                if c == 0:
                    sync.wait_ge(s_v.h, hft_p[-1])
                    sync.wait_ge(s_mm.h, vrow_guard[0])
                else:
                    sync.wait_ge(s_mm.h, bo_mm[c - 1])
                ins = sync.dma_start(out=vrow[:, 0:512],
                                     in_=io["bout_r"][0:1, c * 512:(c + 1) * 512])
                assert s_sml.inc(ins, 16) == bo_l[c]

        @block.tensor
        def _(t):
            t.wait_ge(s_v.h, hft_p[-1])
            for c in range(NBC):
                t.wait_ge(s_sml.h, bo_l[c])
                if c >= 2:
                    t.wait_ge(s_v.h, bo_v[c - 2])
                ins = t.matmul(pb[c % 2][:], ones1[:, 0:128], vrow[:, 0:512],
                               start=True, stop=True)
                assert s_mm.inc(ins, 1) == bo_mm[c]

        @block.vector
        def _(v):
            for c in range(NBC):
                v.wait_ge(s_mm.h, bo_mm[c])
                ins = v.tensor_copy(bout_b[:, c * 512:(c + 1) * 512],
                                    pb[c % 2][:])
                assert s_v.inc(ins, 1) == bo_v[c]

        def wout_colblock(sync, ob, dest, sem):
            if ob == 0:
                sync.wait_ge(s_cc.h, hold["ag_wout"])
                sync.wait_ge(s_v.h, hft_p[-1])
            srcap = io["sh_WoutT"].rearrange("(dt pp) o -> pp dt o", pp=128)[:, :, ob * 512:(ob + 1) * 512]
            ins = sync.dma_start(out=dest[:].rearrange("p (dt j) -> p dt j", j=512),
                                 in_=srcap)
            sem.inc(ins, 16)

        st_last = emit_mm_main(
            p, block, es, hfT, s_trd.v, wout_colblock, D,
            tag=f"{tag}o", nk=NCT,
            epi=dict(scale=1.0, bias_b=bout_b, act=None,
                     xold=io["x_in"], xout=io["x_out"],
                     bias_ready=bo_v[-1]))

        @block.sync
        def _(sync):
            for sem, val in st_last:
                sync.wait_ge(sem.h, val)

    return st_last


def build_step2_nc(cores=8):
    nc = bass.Bass("TRN2", target_bir_lowering=False)
    io = {}
    AD = A_DIM
    io["x_in"] = nc.dram_tensor("x_in", [TOK, D], F32, kind="ExternalInput").ap()
    io["x_out"] = nc.dram_tensor("x_out", [TOK, D], F32, kind="ExternalOutput").ap()
    io["WinTsl"] = nc.dram_tensor("WinTsl", [D // cores, AD], F32,
                                  kind="ExternalInput").ap()
    io["W1Tsl"] = nc.dram_tensor("W1Tsl", [4 * (AD // cores), A_DFF], F32,
                                 kind="ExternalInput").ap()
    io["W2Tsl"] = nc.dram_tensor("W2Tsl", [4 * (A_DFF // cores), AD], F32,
                                 kind="ExternalInput").ap()
    io["WoutTsl"] = nc.dram_tensor("WoutTsl", [AD // cores, D], F32,
                                   kind="ExternalInput").ap()
    for name, shp in [("zbre", [128, 2]), ("zbim", [128, 2]),
                      ("zre_t", [128, 16]), ("zim_t", [128, 16]),
                      ("lc_t", [128, 32]), ("scl_t", [128, 16]),
                      ("b1_t", [128, 64]), ("qsel", [128, 4])]:
        io[name] = nc.dram_tensor(name, shp, F32, kind="ExternalInput").ap()
    for name in ["ln1w", "ln1b", "ln2w", "ln2b", "b2r"]:
        io[name] = nc.dram_tensor(name, [4, AD], F32, kind="ExternalInput").ap()
    io["bin_r"] = nc.dram_tensor("bin_r", [1, AD], F32, kind="ExternalInput").ap()
    io["lnlw_r"] = nc.dram_tensor("lnlw_r", [1, AD], F32, kind="ExternalInput").ap()
    io["lnlb_r"] = nc.dram_tensor("lnlb_r", [1, AD], F32, kind="ExternalInput").ap()
    io["bout_r"] = nc.dram_tensor("bout_r", [1, D], F32, kind="ExternalInput").ap()

    io["WinTsl_bf"] = nc.dram_tensor("WinTsl_bf", [D // cores, AD], BF16).ap()
    io["W1Tsl_bf"] = nc.dram_tensor("W1Tsl_bf", [4 * (AD // cores), A_DFF], BF16).ap()
    io["W2Tsl_bf"] = nc.dram_tensor("W2Tsl_bf", [4 * (A_DFF // cores), AD], BF16).ap()
    io["WoutTsl_bf"] = nc.dram_tensor("WoutTsl_bf", [AD // cores, D], BF16).ap()
    io["xln_tm"] = nc.dram_tensor("xln_tm", [TOK, D], BF16).ap()
    io["y_dram"] = nc.dram_tensor("y_dram", [TOK, AD], BF16).ap()
    io["tabloc"] = nc.dram_tensor("tabloc", [768, TOK], F32).ap()
    for l in range(4):
        io[f"eloc{l}"] = nc.dram_tensor(f"eloc{l}", [128, 8], F32).ap()
        io[f"egath{l}"] = nc.dram_tensor(f"egath{l}", [512, 8], F32).ap()
    io["sh_WinT"] = nc.dram_tensor("sh_WinT", [D, AD], BF16, addr_space="Shared").ap()
    io["sh_W1T"] = nc.dram_tensor("sh_W1T", [4 * AD, A_DFF], BF16,
                                  addr_space="Shared").ap()
    io["sh_W2T"] = nc.dram_tensor("sh_W2T", [4 * A_DFF, AD], BF16,
                                  addr_space="Shared").ap()
    io["sh_WoutT"] = nc.dram_tensor("sh_WoutT", [AD, D], BF16,
                                    addr_space="Shared").ap()
    io["sh_tab"] = nc.dram_tensor("sh_tab", [cores * 768, TOK], F32,
                                  addr_space="Shared").ap()

    with ExitStack() as es:
        p = P(nc, es, cores)
        block = es.enter_context(nc.Block())
        emit_step2(p, block, io)
    return nc


def step2_in_maps(inputs, x_full, cores=8):
    AD = A_DIM
    Win = np.asarray(inputs["L0_attn_Win"], np.float32)
    W1 = np.asarray(inputs["L0_attn_W1"], np.float32)
    W2 = np.asarray(inputs["L0_attn_W2"], np.float32)
    Wout = np.asarray(inputs["L0_attn_Wout"], np.float32)
    WinT = np.ascontiguousarray(Win.T)
    W1T = np.ascontiguousarray(W1.transpose(0, 2, 1))
    W2T = np.ascontiguousarray(W2.transpose(0, 2, 1))
    WoutT = np.ascontiguousarray(Wout.T)
    phr = np.asarray(inputs["L0_attn_phz_re"], np.float32)
    phi = np.asarray(inputs["L0_attn_phz_im"], np.float32)
    lcr = np.asarray(inputs["L0_attn_lc_re"], np.float32)
    lci = np.asarray(inputs["L0_attn_lc_im"], np.float32)
    scl = np.asarray(inputs["L0_attn_sc_lin"], np.float32)
    b1 = np.asarray(inputs["L0_attn_b1"], np.float32)
    zre_t = np.ascontiguousarray(phr.reshape(4, 4, 128).transpose(2, 0, 1)
                                 .reshape(128, 16))
    zim_t = np.ascontiguousarray(phi.reshape(4, 4, 128).transpose(2, 0, 1)
                                 .reshape(128, 16))
    lc_t = np.ascontiguousarray(
        np.stack([lcr, lci], -1).reshape(4, 4, 128, 2).transpose(2, 0, 1, 3)
        .reshape(128, 32))
    scl_t = np.ascontiguousarray(scl.reshape(4, 4, 128).transpose(2, 0, 1)
                                 .reshape(128, 16))
    b1_t = np.ascontiguousarray(b1.reshape(4, 16, 128).transpose(2, 0, 1)
                                .reshape(128, 64))
    WSL, FSL, OSL = AD // cores, A_DFF // cores, D // cores
    maps = []
    for k in range(cores):
        q = k % 4
        qs = np.zeros((128, 4), np.float32)
        qs[:, q] = 1.0
        u0 = 2 * k
        maps.append(dict(
            x_in=np.ascontiguousarray(x_full[k * TOK:(k + 1) * TOK]),
            WinTsl=np.ascontiguousarray(WinT[k * OSL:(k + 1) * OSL]),
            W1Tsl=np.ascontiguousarray(
                W1T.reshape(4 * AD, A_DFF)[k * 4 * WSL:(k + 1) * 4 * WSL]),
            W2Tsl=np.ascontiguousarray(
                W2T.reshape(4 * A_DFF, AD)[k * 4 * FSL:(k + 1) * 4 * FSL]),
            WoutTsl=np.ascontiguousarray(WoutT[k * WSL:(k + 1) * WSL]),
            zbre=np.ascontiguousarray(zre_t[:, u0:u0 + 2]),
            zbim=np.ascontiguousarray(zim_t[:, u0:u0 + 2]),
            zre_t=zre_t, zim_t=zim_t, lc_t=lc_t, scl_t=scl_t, b1_t=b1_t,
            qsel=qs,
            ln1w=np.ascontiguousarray(np.asarray(inputs["L0_attn_ln1_w"], np.float32)),
            ln1b=np.ascontiguousarray(np.asarray(inputs["L0_attn_ln1_b"], np.float32)),
            ln2w=np.ascontiguousarray(np.asarray(inputs["L0_attn_ln2_w"], np.float32)),
            ln2b=np.ascontiguousarray(np.asarray(inputs["L0_attn_ln2_b"], np.float32)),
            b2r=np.ascontiguousarray(np.asarray(inputs["L0_attn_b2"], np.float32)),
            bin_r=np.ascontiguousarray(
                np.asarray(inputs["L0_attn_bin"], np.float32).reshape(1, AD)),
            lnlw_r=np.ascontiguousarray(
                np.asarray(inputs["L0_attn_lnl_w"], np.float32).reshape(1, AD)),
            lnlb_r=np.ascontiguousarray(
                np.asarray(inputs["L0_attn_lnl_b"], np.float32).reshape(1, AD)),
            bout_r=np.ascontiguousarray(
                np.asarray(inputs["L0_attn_bout"], np.float32).reshape(1, D)),
        ))
    return maps



def build_step1_nc(cores=8):
    nc = bass.Bass("TRN2", target_bir_lowering=False)
    OSL = D // cores
    io = {}
    io["x_in"] = nc.dram_tensor("x_in", [TOK, D], F32, kind="ExternalInput").ap()
    io["x_out"] = nc.dram_tensor("x_out", [TOK, D], F32, kind="ExternalOutput").ap()
    io["WTsl"] = nc.dram_tensor("WTsl", [OSL, D], F32, kind="ExternalInput").ap()
    io["loraAT"] = nc.dram_tensor("loraAT", [D, 128], F32, kind="ExternalInput").ap()
    io["loraBT"] = nc.dram_tensor("loraBT", [128, D], F32, kind="ExternalInput").ap()
    io["lora_w"] = nc.dram_tensor("lora_w", [128, 1], F32, kind="ExternalInput").ap()
    io["bias"] = nc.dram_tensor("bias", [1, D], F32, kind="ExternalInput").ap()
    io["WTsl_bf"] = nc.dram_tensor("WTsl_bf", [OSL, D], BF16).ap()
    io["AT_bf"] = nc.dram_tensor("AT_bf", [D, 128], BF16).ap()
    io["BT_bf"] = nc.dram_tensor("BT_bf", [128, D], BF16).ap()
    io["xln_tm"] = nc.dram_tensor("xln_tm", [TOK, D], BF16).ap()

    io["sh_WT"] = nc.dram_tensor("sh_WT", [D, D], BF16, addr_space="Shared").ap()

    with ExitStack() as es:
        p = P(nc, es, cores)
        block = es.enter_context(nc.Block())
        emit_step1(p, block, io)
    return nc


def step1_in_maps(inputs, x_full, cores=8):
    W = np.asarray(inputs["L0_ffn_W"], np.float32)
    WT = np.ascontiguousarray(W.T)
    AT = np.ascontiguousarray(
        np.asarray(inputs["L0_ffn_lora_A"], np.float32).reshape(128, D).T)
    BT = np.ascontiguousarray(
        np.asarray(inputs["L0_ffn_lora_B"], np.float32).transpose(0, 2, 1).reshape(128, D))
    OSL = D // cores
    maps = []
    for k in range(cores):
        maps.append(dict(
            x_in=np.ascontiguousarray(x_full[k * TOK:(k + 1) * TOK]),
            WTsl=np.ascontiguousarray(WT[k * OSL:(k + 1) * OSL]),
            loraAT=AT,
            loraBT=BT,
            lora_w=np.ascontiguousarray(np.repeat(
                np.asarray(inputs["L0_ffn_lora_weights"], np.float32), 16).reshape(128, 1)),
            bias=np.ascontiguousarray(
                np.asarray(inputs["L0_ffn_bias"], np.float32).reshape(1, D)),
        ))
    return maps





def _decl_step1_io(nc, pre, cores, x_in, x_out):
    OSL = D // cores
    io = {"x_in": x_in, "x_out": x_out}
    io["WTsl"] = nc.dram_tensor(pre + "WTsl", [OSL, D], F32, kind="ExternalInput").ap()
    io["loraAT"] = nc.dram_tensor(pre + "loraAT", [D, 128], F32, kind="ExternalInput").ap()
    io["loraBT"] = nc.dram_tensor(pre + "loraBT", [128, D], F32, kind="ExternalInput").ap()
    io["lora_w"] = nc.dram_tensor(pre + "lora_w", [128, 1], F32, kind="ExternalInput").ap()
    io["bias"] = nc.dram_tensor(pre + "bias", [1, D], F32, kind="ExternalInput").ap()
    io["WTsl_bf"] = nc.dram_tensor(pre + "WTsl_bf", [OSL, D], BF16).ap()
    io["AT_bf"] = nc.dram_tensor(pre + "AT_bf", [D, 128], BF16).ap()
    io["BT_bf"] = nc.dram_tensor(pre + "BT_bf", [128, D], BF16).ap()
    io["xln_tm"] = nc.dram_tensor(pre + "xln_tm", [TOK, D], BF16).ap()
    io["sh_WT"] = nc.dram_tensor(pre + "sh_WT", [D, D], BF16, addr_space="Shared").ap()
    return io


def _decl_step2_io(nc, pre, cores, x_in, x_out):
    AD = A_DIM
    io = {"x_in": x_in, "x_out": x_out}
    io["WinTsl"] = nc.dram_tensor(pre + "WinTsl", [D // cores, AD], F32, kind="ExternalInput").ap()
    io["W1Tsl"] = nc.dram_tensor(pre + "W1Tsl", [4 * (AD // cores), A_DFF], F32, kind="ExternalInput").ap()
    io["W2Tsl"] = nc.dram_tensor(pre + "W2Tsl", [4 * (A_DFF // cores), AD], F32, kind="ExternalInput").ap()
    io["WoutTsl"] = nc.dram_tensor(pre + "WoutTsl", [AD // cores, D], F32, kind="ExternalInput").ap()
    for name, shp in [("zbre", [128, 2]), ("zbim", [128, 2]),
                      ("zre_t", [128, 16]), ("zim_t", [128, 16]),
                      ("lc_t", [128, 32]), ("scl_t", [128, 16]),
                      ("b1_t", [128, 64]), ("qsel", [128, 4])]:
        io[name] = nc.dram_tensor(pre + name, shp, F32, kind="ExternalInput").ap()
    for name in ["ln1w", "ln1b", "ln2w", "ln2b", "b2r"]:
        io[name] = nc.dram_tensor(pre + name, [4, AD], F32, kind="ExternalInput").ap()
    io["bin_r"] = nc.dram_tensor(pre + "bin_r", [1, AD], F32, kind="ExternalInput").ap()
    io["lnlw_r"] = nc.dram_tensor(pre + "lnlw_r", [1, AD], F32, kind="ExternalInput").ap()
    io["lnlb_r"] = nc.dram_tensor(pre + "lnlb_r", [1, AD], F32, kind="ExternalInput").ap()
    io["bout_r"] = nc.dram_tensor(pre + "bout_r", [1, D], F32, kind="ExternalInput").ap()
    io["WinTsl_bf"] = nc.dram_tensor(pre + "WinTsl_bf", [D // cores, AD], BF16).ap()
    io["W1Tsl_bf"] = nc.dram_tensor(pre + "W1Tsl_bf", [4 * (AD // cores), A_DFF], BF16).ap()
    io["W2Tsl_bf"] = nc.dram_tensor(pre + "W2Tsl_bf", [4 * (A_DFF // cores), AD], BF16).ap()
    io["WoutTsl_bf"] = nc.dram_tensor(pre + "WoutTsl_bf", [AD // cores, D], BF16).ap()
    io["xln_tm"] = nc.dram_tensor(pre + "xln_tm", [TOK, D], BF16).ap()
    io["y_dram"] = nc.dram_tensor(pre + "y_dram", [TOK, AD], BF16).ap()
    io["tabloc"] = nc.dram_tensor(pre + "tabloc", [768, TOK], F32).ap()
    for l in range(4):
        io[f"eloc{l}"] = nc.dram_tensor(pre + f"eloc{l}", [128, 8], F32).ap()
        io[f"egath{l}"] = nc.dram_tensor(pre + f"egath{l}", [512, 8], F32).ap()
    io["sh_WinT"] = nc.dram_tensor(pre + "sh_WinT", [D, AD], BF16, addr_space="Shared").ap()
    io["sh_W1T"] = nc.dram_tensor(pre + "sh_W1T", [4 * AD, A_DFF], BF16, addr_space="Shared").ap()
    io["sh_W2T"] = nc.dram_tensor(pre + "sh_W2T", [4 * A_DFF, AD], BF16, addr_space="Shared").ap()
    io["sh_WoutT"] = nc.dram_tensor(pre + "sh_WoutT", [AD, D], BF16, addr_space="Shared").ap()
    io["sh_tab"] = nc.dram_tensor(pre + "sh_tab", [cores * 768, TOK], F32, addr_space="Shared").ap()
    return io


def _decl_step4_io(nc, pre, cores, x_in, x_out, weights=None):
    HSL = H_HEADS // cores
    io = {"x_in": x_in, "x_out": x_out}
    if weights is None:
        io["upsl"] = nc.dram_tensor(pre + "upsl", [HSL * H_DH, H_M], F32, kind="ExternalInput").ap()
        io["dnsl"] = nc.dram_tensor(pre + "dnsl", [HSL * H_M, H_DH], F32, kind="ExternalInput").ap()
        io["perm_t"] = nc.dram_tensor(pre + "perm_t", [128, 32], I32, kind="ExternalInput").ap()
        io["upsl_bf"] = nc.dram_tensor(pre + "upsl_bf", [HSL * H_DH, H_M], BF16).ap()
        io["dnsl_bf"] = nc.dram_tensor(pre + "dnsl_bf", [HSL * H_M, H_DH], BF16).ap()
        io["sh_up"] = nc.dram_tensor(pre + "sh_up", [H_HEADS * H_DH, H_M], BF16, addr_space="Shared").ap()
        io["sh_dn"] = nc.dram_tensor(pre + "sh_dn", [H_HEADS * H_M, H_DH], BF16, addr_space="Shared").ap()
    else:
        for k in ["upsl", "dnsl", "perm_t", "upsl_bf", "dnsl_bf", "sh_up", "sh_dn"]:
            io[k] = weights[k]
    io["xln_tm"] = nc.dram_tensor(pre + "xln_tm", [TOK, D], BF16).ap()
    io["xln_fm"] = nc.dram_tensor(pre + "xln_fm", [D, TOK], BF16).ap()
    io["deltaT"] = nc.dram_tensor(pre + "deltaT", [D, TOK], F32).ap()
    return io


def _decl_step5_io(nc, pre, cores, x_in, x_out):
    TSL = D // cores
    io = {"x_in": x_in, "x_out": x_out}
    io["w1sl"] = nc.dram_tensor(pre + "w1sl", [TSL, D], F32, kind="ExternalInput").ap()
    io["w2Tsl"] = nc.dram_tensor(pre + "w2Tsl", [D, TSL], F32, kind="ExternalInput").ap()
    io["mask_sl"] = nc.dram_tensor(pre + "mask_sl", [D, TSL], F32, kind="ExternalInput").ap()
    io["b2"] = nc.dram_tensor(pre + "b2", [1, D], F32, kind="ExternalInput").ap()
    io["w1sl_bf"] = nc.dram_tensor(pre + "w1sl_bf", [TSL, D], BF16).ap()
    io["w2Tsl_bf"] = nc.dram_tensor(pre + "w2Tsl_bf", [D, TSL], BF16).ap()
    io["attn_sl"] = nc.dram_tensor(pre + "attn_sl", [D, TSL], BF16).ap()
    io["xln_tm"] = nc.dram_tensor(pre + "xln_tm", [TOK, D], BF16).ap()
    io["sh_w1"] = nc.dram_tensor(pre + "sh_w1", [D, D], BF16, addr_space="Shared").ap()
    io["sh_attn"] = nc.dram_tensor(pre + "sh_attn", [cores * D, TSL], BF16, addr_space="Shared").ap()
    return io


def emit_gate(p, block, prev_done):
    if prev_done:
        @block.vector
        def _(v):
            for s, val in prev_done:
                v.wait_ge(s.h, val)

        @block.scalar
        def _(sc):
            for s, val in prev_done:
                sc.wait_ge(s.h, val)

        @block.tensor
        def _(t):
            for s, val in prev_done:
                t.wait_ge(s.h, val)

        @block.sync
        def _(sy):
            for s, val in prev_done:
                sy.wait_ge(s.h, val)

    gp_sems = ["s_gp", "s_gpw", "s_gpl", "s_gpt", "s_gpz", "s_gplw",
               "s_gpp", "s_gx0", "s_gx1", "s_gs0", "s_gs1"]
    targets = [(p.sems[n], p.sems[n].v) for n in gp_sems if n in p.sems]
    if targets:
        @block.gpsimd
        def _(gp):
            for s, tgt in targets:
                gp.wait_ge(s.h, tgt)


def build_fused_nc(cores=8, nsteps=6):
    nc = bass.Bass("TRN2", target_bir_lowering=False)
    x0 = nc.dram_tensor("x0", [TOK, D], F32, kind="ExternalInput").ap()
    xf = nc.dram_tensor("x_out", [TOK, D], F32, kind="ExternalOutput").ap()
    xA = nc.dram_tensor("xA", [TOK, D], F32).ap()
    xB = nc.dram_tensor("xB", [TOK, D], F32).ap()

    outs = [xA, xB, xA, xB, xA, xf]
    outs[nsteps - 1] = xf
    io1 = _decl_step1_io(nc, "p1_", cores, x0, outs[0])
    io2 = _decl_step2_io(nc, "p2_", cores, outs[0], outs[1])
    io3 = dict(io1)
    io3.update({"x_in": outs[1], "x_out": outs[2],
                "xln_tm": nc.dram_tensor("p3_xln_tm", [TOK, D], BF16).ap()})
    io4 = _decl_step4_io(nc, "p4_", cores, outs[2], outs[3])
    io5 = _decl_step5_io(nc, "p5_", cores, outs[3], outs[4])
    io6 = _decl_step4_io(nc, "p6_", cores, outs[4], outs[5], weights=io4)

    with ExitStack() as es:
        p = P(nc, es, cores)
        block = es.enter_context(nc.Block())
        sh1 = {}
        sh4 = {}
        d1 = emit_step1(p, block, io1, tag="s1", share=sh1)
        if nsteps >= 2:
            emit_gate(p, block, d1)
            d2 = emit_step2(p, block, io2, tag="s2", prev_done=d1)
        if nsteps >= 3:
            emit_gate(p, block, d2)
            d3 = emit_step1(p, block, io3, tag="s3", prev_done=d2, share=sh1)
        if nsteps >= 4:
            emit_gate(p, block, d3)
            d4 = emit_step4(p, block, io4, tag="s4", prev_done=d3, share=sh4)
        if nsteps >= 5:
            emit_gate(p, block, d4)
            d5 = emit_step5(p, block, io5, tag="s5", prev_done=d4)
        if nsteps >= 6:
            emit_gate(p, block, d5)
            d6 = emit_step4(p, block, io6, tag="s6", prev_done=d5, share=sh4)
    return nc


def fused_in_maps(inputs, cores=8):
    x = np.ascontiguousarray(
        np.asarray(inputs["L0_attn_x"], np.float32).reshape(8192, D))
    m1 = step1_in_maps(inputs, x, cores)
    m2 = step2_in_maps(inputs, x, cores)
    m4 = step4_in_maps(inputs, x, cores)
    m5 = step5_in_maps(inputs, x, cores)
    maps = []
    for k in range(cores):
        m = {}
        m["x0"] = m1[k].pop("x_in")
        m2[k].pop("x_in"); m4[k].pop("x_in"); m5[k].pop("x_in")
        for pre, mm in [("p1_", m1[k]), ("p2_", m2[k]), ("p4_", m4[k]),
                        ("p5_", m5[k])]:
            for name, arr in mm.items():
                m[pre + name] = arr
        maps.append(m)
    return maps


_NC_CACHE = {}


def _get_nc(name, builder):
    if name not in _NC_CACHE:
        _NC_CACHE[name] = builder(cores=8)
    return _NC_CACHE[name]


def _launch(nc, maps):
    res = run_bass_kernel_spmd(nc, maps, core_ids=list(range(8)))
    return np.concatenate([res.results[c]["x_out"] for c in range(8)], 0)


def kernel_unfused(**inputs):
    x = np.ascontiguousarray(
        np.asarray(inputs["L0_attn_x"], np.float32).reshape(8192, D))
    nc1 = _get_nc("s1", build_step1_nc)
    nc2 = _get_nc("s2", build_step2_nc)
    nc4 = _get_nc("s4", build_step4_nc)
    nc5 = _get_nc("s5", build_step5_nc)
    x = _launch(nc1, step1_in_maps(inputs, x))
    x = _launch(nc2, step2_in_maps(inputs, x))
    x = _launch(nc1, step1_in_maps(inputs, x))
    x = _launch(nc4, step4_in_maps(inputs, x))
    x = _launch(nc5, step5_in_maps(inputs, x))
    x = _launch(nc4, step4_in_maps(inputs, x))
    return x.reshape(2, 4096, D)


def kernel(**inputs):
    nc = _get_nc("fused", build_fused_nc)
    maps = fused_in_maps(inputs, cores=8)
    res = run_bass_kernel_spmd(nc, maps, core_ids=list(range(8)))
    x = np.concatenate([res.results[c]["x_out"] for c in range(8)], 0)
    return x.reshape(2, 4096, D)


# version 06
# speedup vs baseline: 1.1285x; 1.0042x over previous
import numpy as np
from contextlib import ExitStack

import concourse.bass as bass
import concourse.mybir as mybir
from concourse.bass_utils import run_bass_kernel_spmd
from concourse.bass import IndirectOffsetOnAxis

F32 = mybir.dt.float32
BF16 = mybir.dt.bfloat16
I32 = mybir.dt.int32
AF = mybir.ActivationFunctionType
ALU = mybir.AluOpType
AX = mybir.AxisListType

TOK = 1024
NTT = TOK // 128
D = 4096
NDT = D // 128
EPS = 1e-5

A_DEPTH, A_DIM, A_DFF = 4, 512, 2048
H_HEADS, H_DH, H_M = 16, 256, 1024


class Sem:
    __slots__ = ("h", "v", "_cm")

    def __init__(self, nc, name):
        self._cm = nc.semaphore(name)
        self.h = self._cm.__enter__()
        self.v = 0

    def inc(self, instr, n):
        instr.then_inc(self.h, n)
        self.v += n
        return self.v


class P:
    def __init__(self, nc, es, cores):
        self.nc = nc
        self.es = es
        self.cores = cores
        self.sems = {}
        self.n_bar = 0
        self.zrow = self.sb("p_zrow", [1, 8], F32)

    def sem(self, name):
        if name not in self.sems:
            self.sems[name] = Sem(self.nc, name)
        return self.sems[name]

    def sb(self, name, shape, dtype, es=None):
        return (es or self.es).enter_context(self.nc.sbuf_tensor(name, shape, dtype))

    def psum(self, name, shape, dtype=F32, es=None):
        return (es or self.es).enter_context(self.nc.psum_tensor(name, shape, dtype))


def emit_barrier(p, block):
    nc = p.nc
    s_cc = p.sem("s_cc")
    i = p.n_bar
    p.n_bar += 1
    bar_in = nc.dram_tensor(f"bar_in{i}", [1, 8], F32)
    bar_out = nc.dram_tensor(f"bar_out{i}", [p.cores, 8], F32, addr_space="Shared")
    s_gp = p.sem("s_gp")

    s_gpc = p.sem("s_gpc")

    @block.gpsimd
    def _(gp):
        ins = gp.memset(p.zrow[:], 0.0)
        zc = s_gpc.inc(ins, 1)
        gp.wait_ge(s_gpc.h, zc)
        ins = gp.dma_start(out=bar_in.ap(), in_=p.zrow[:])
        zv = s_gp.inc(ins, 16)
        gp.wait_ge(s_gp.h, zv)
        ins = gp.collective_compute(
            "AllGather", ALU.bypass, replica_groups=[list(range(p.cores))],
            ins=[bar_in.ap()], outs=[bar_out.ap()],
        )
        s_cc.inc(ins, 1)

    return s_cc.v


def emit_allgather(p, block, local_in_ap, shared_out_ap, wait_gp=None):
    s_cc = p.sem("s_cc")

    @block.gpsimd
    def _(gp):
        if wait_gp is not None:
            gp.wait_ge(wait_gp[0].h, wait_gp[1])
        ins = gp.collective_compute(
            "AllGather", ALU.bypass, replica_groups=[list(range(p.cores))],
            ins=[local_in_ap], outs=[shared_out_ap],
        )
        s_cc.inc(ins, 1)

    return s_cc.v



def emit_ln(p, block, es, x_dram, xln_tm, xlnT, tag, fm_dram=None,
            x_ready=None):
    s_lx = [p.sem("s_lx0"), p.sem("s_lx1")]
    s_v = p.sem("s_v")
    s_s = p.sem("s_s")
    s_wm = [p.sem("s_wm0"), p.sem("s_wm1")]
    s_trd = p.sem("s_trd")

    NC5 = D // 512
    xt = [p.sb(f"{tag}_xt{i}", [128, D], F32, es) for i in range(2)]
    stats = p.sb(f"{tag}_bst", [128, 6 * NC5], F32, es)
    agg = p.sb(f"{tag}_agg", [128, 2 * NTT], F32, es)
    vep = p.sb(f"{tag}_vep", [128, NTT], F32, es)
    sd = p.sb(f"{tag}_sd", [128, NTT], F32, es)
    rs = p.sb(f"{tag}_rs", [128, NTT], F32, es)
    nb = p.sb(f"{tag}_nb", [128, NTT], F32, es)
    xlno = [p.sb(f"{tag}_xo{i}", [128, D], BF16, es) for i in range(2)]

    v0, s0 = s_v.v, s_s.v
    lx0 = [s_lx[0].v, s_lx[1].v]
    wm0 = [s_wm[0].v, s_wm[1].v]
    trd0 = s_trd.v
    lx1_p = [lx0[tt % 2] + 16 * (tt // 2 + 1) for tt in range(NTT)]
    lx2_p = [lx0[tt % 2] + 16 * (NTT // 2 + tt // 2 + 1) for tt in range(NTT)]
    v1_p = [v0 + tt + 1 for tt in range(NTT)]
    sqrt_p = s0 + 1
    rs_p = v0 + NTT + 1
    s2_p = [s0 + 1 + tt + 1 for tt in range(NTT)]
    wm_p = [wm0[tt % 2] + 16 * (tt // 2 + 1) for tt in range(NTT)]
    n_tr = NDT + (NDT if fm_dram is not None else 0)
    tr_all = trd0 + 16 * n_tr

    @block.sync
    def _(sync):
        for tt in range(NTT):
            if tt == 0 and x_ready:
                for sem, val in x_ready:
                    sync.wait_ge(sem.h, val)
            if tt >= 2:
                sync.wait_ge(s_v.h, v1_p[tt - 2])
            ins = sync.dma_start(out=xt[tt % 2][:],
                                 in_=x_dram[tt * 128:(tt + 1) * 128, :])
            assert s_lx[tt % 2].inc(ins, 16) == lx1_p[tt]
        for tt in range(NTT):
            if tt < 2:
                sync.wait_ge(s_v.h, v1_p[-1])
            ins = sync.dma_start(out=xt[tt % 2][:],
                                 in_=x_dram[tt * 128:(tt + 1) * 128, :])
            assert s_lx[tt % 2].inc(ins, 16) == lx2_p[tt]
            if tt >= 1:
                w = tt - 1
                sync.wait_ge(s_s.h, s2_p[w])
                ins = sync.dma_start(out=xln_tm[w * 128:(w + 1) * 128, :],
                                     in_=xlno[w % 2][:])
                assert s_wm[w % 2].inc(ins, 16) == wm_p[w]
        for w in (NTT - 1,):
            sync.wait_ge(s_s.h, s2_p[w])
            ins = sync.dma_start(out=xln_tm[w * 128:(w + 1) * 128, :],
                                 in_=xlno[w % 2][:])
            assert s_wm[w % 2].inc(ins, 16) == wm_p[w]

    @block.vector
    def _(v):
        s_ve = p.sem("s_ve")
        for tt in range(NTT):
            v.wait_ge(s_lx[tt % 2].h, lx1_p[tt])
            for c in range(NC5):
                ins = v.bn_stats(stats[:, 6 * c:6 * c + 6],
                                 xt[tt % 2][:, 512 * c:512 * (c + 1)])
            v.wait_ge(s_ve.h, s_ve.inc(ins, 1))
            ins = v.bn_aggr(agg[:, 2 * tt:2 * tt + 2], stats[:])
            v.wait_ge(s_ve.h, s_ve.inc(ins, 1))
            ins = v.tensor_scalar(vep[:, tt:tt + 1], agg[:, 2 * tt + 1:2 * tt + 2],
                                  EPS, None, ALU.add)
            assert s_v.inc(ins, 1) == v1_p[tt]
        v.wait_ge(s_s.h, sqrt_p)
        ins = v.reciprocal(rs[:], sd[:])
        v.wait_ge(s_ve.h, s_ve.inc(ins, 1))
        means = agg[:].rearrange("p (t two) -> p two t", two=2)[:, 0, :]
        ins = v.tensor_tensor(nb[:], means, rs[:], ALU.mult)
        v.wait_ge(s_ve.h, s_ve.inc(ins, 1))
        ins = v.tensor_scalar(nb[:], nb[:], -1.0, None, ALU.mult)
        assert s_v.inc(ins, 1) == rs_p

    @block.scalar
    def _(s):
        s.wait_ge(s_v.h, v1_p[-1])
        ins = s.activation(sd[:], vep[:], AF.Sqrt)
        assert s_s.inc(ins, 1) == sqrt_p
        for tt in range(NTT):
            s.wait_ge(s_lx[tt % 2].h, lx2_p[tt])
            if tt == 0:
                s.wait_ge(s_v.h, rs_p)
            if tt >= 2:
                s.wait_ge(s_wm[tt % 2].h, wm_p[tt - 2])
            ins = s.activation(xlno[tt % 2][:], xt[tt % 2][:], AF.Identity,
                               bias=nb[:, tt:tt + 1], scale=rs[:, tt:tt + 1])
            assert s_s.inc(ins, 1) == s2_p[tt]
        for i, w in enumerate(wm_p[-2:]):
            s.wait_ge(s_wm[(NTT - 2 + i) % 2].h, w)
        for dt in range(NDT):
            ins = s.dma_start(out=xlnT[:, dt * TOK:(dt + 1) * TOK],
                              in_=xln_tm[:, dt * 128:(dt + 1) * 128],
                              transpose=True)
            s_trd.inc(ins, 16)
        if fm_dram is not None:
            s.wait_ge(s_trd.h, trd0 + 16 * NDT)
            for dt in range(NDT):
                ins = s.dma_start(out=fm_dram[dt * 128:(dt + 1) * 128, :],
                                  in_=xlnT[:, dt * TOK:(dt + 1) * TOK])
                s_trd.inc(ins, 16)
    assert s_trd.v == tr_all

    return tr_all, tr_all



def emit_bcast_row(p, block, es, vec_dram_row, n, out_sb, scale, tag, ones1,
                   ones_ready):
    s_lb = p.sem("s_lx")
    s_mm = p.sem("s_mm")
    s_v = p.sem("s_v")
    vrow = p.sb(f"{tag}_vrow", [1, n], F32, es)
    pb = [p.psum(f"{tag}_bps{i}", [128, 512], F32, es=es) for i in range(2)]
    hold = {}
    mm_c, v_c = [], []
    nch = (n + 511) // 512

    @block.sync
    def _(sync):
        ins = sync.dma_start(out=vrow[:], in_=vec_dram_row)
        hold["ld"] = s_lb.inc(ins, 16)

    v0 = s_v.v
    v_pred = [v0 + c + 1 for c in range(nch)]

    @block.tensor
    def _(t):
        t.wait_ge(s_lb.h, hold["ld"])
        t.wait_ge(s_v.h, ones_ready)
        for c in range(nch):
            w = min(512, n - 512 * c)
            if c >= 2:
                t.wait_ge(s_v.h, v_pred[c - 2])
            ins = t.matmul(pb[c % 2][:, 0:w], ones1[:, 0:128],
                           vrow[:, 512 * c:512 * c + w], start=True, stop=True)
            mm_c.append(s_mm.inc(ins, 1))

    @block.vector
    def _(v):
        for c in range(nch):
            w = min(512, n - 512 * c)
            v.wait_ge(s_mm.h, mm_c[c])
            ins = v.tensor_scalar(out_sb[:, 512 * c:512 * c + w], pb[c % 2][:, 0:w],
                                  scale, None, ALU.mult)
            v_c.append(s_v.inc(ins, 1))

    assert v_c == v_pred, (v_c, v_pred)
    return v_c[-1]



def emit_mm_main(p, block, es, xlnT, tr_all, w_colblock_dma, n_out,
                 extra_mm=None, extra_ready=0, tag="mm", epi=None, nk=NDT):
    s_lw = [p.sem("s_lw0"), p.sem("s_lw1")]
    s_lxo = [p.sem(f"s_lxo{i}") for i in range(4)]
    s_mm = p.sem("s_mm")
    s_v = p.sem("s_v")
    s_s = p.sem("s_s")
    s_st = [p.sem("s_st0"), p.sem("s_st1")]
    s_trd = p.sem("s_trd")

    NOB = n_out // 512
    wcb = [p.sb(f"{tag}_wcb{i}", [128, nk * 512], BF16, es) for i in range(2)]
    ps = [p.psum(f"{tag}_ps{i}", [128, 512], F32, es=es) for i in range(2)]
    tmp = [p.sb(f"{tag}_tmp{i}", [128, 512], F32, es) for i in range(2)]
    xnew = [p.sb(f"{tag}_xn{i}", [128, 512], F32, es) for i in range(2)]
    xo = [p.sb(f"{tag}_xo{i}", [128, 512], F32, es) for i in range(4)]

    groups = [(ob, tt) for ob in range(NOB) for tt in range(NTT)]
    NG = len(groups)
    scale, bias_b, act = epi["scale"], epi.get("bias_b"), epi.get("act")
    assert act in (None, "gelu")

    mm0, v0, s0 = s_mm.v, s_v.v, s_s.v
    lw0 = [s.v for s in s_lw]
    lxo0 = [s.v for s in s_lxo]
    st0 = [s.v for s in s_st]
    mm_p = [mm0 + g + 1 for g in range(NG)]
    v1_p = [v0 + 2 * g + 1 for g in range(NG)]
    v2_p = [v0 + 2 * g + 2 for g in range(NG)]
    sact_p = [s0 + g + 1 for g in range(NG)]
    lw_p = [lw0[ob % 2] + 16 * (ob // 2 + 1) for ob in range(NOB)]
    lxo_p = [lxo0[g % 4] + 16 * (g // 4 + 1) for g in range(NG)]
    st_p = [st0[g % 2] + 16 * (g // 2 + 1) for g in range(NG)]

    def emit_store(sync, g):
        ob, tt = groups[g]
        sync.wait_ge(s_v.h, v2_p[g])
        ins = sync.dma_start(
            out=epi["xout"][tt * 128:(tt + 1) * 128, ob * 512:(ob + 1) * 512],
            in_=xnew[g % 2][:])
        assert s_st[g % 2].inc(ins, 16) == st_p[g]

    @block.sync
    def _(sync):
        for ob in range(NOB):
            if ob == 0:
                sync.wait_ge(s_trd.h, tr_all)
            if ob >= 2:
                for g in range((ob - 2) * NTT, (ob - 1) * NTT):
                    emit_store(sync, g)
                sync.wait_ge(s_mm.h, mm_p[(ob - 2) * NTT + NTT - 1])
            w_colblock_dma(sync, ob, wcb[ob % 2], s_lw[ob % 2])
            assert s_lw[ob % 2].v == lw_p[ob]
        for g in range((NOB - 2) * NTT, NOB * NTT):
            emit_store(sync, g)

    def emit_xo(s, g):
        ob, tt = groups[g]
        if g == 0:
            s.wait_ge(s_trd.h, tr_all)
        if g >= 4:
            s.wait_ge(s_v.h, v2_p[g - 4])
        ins = s.dma_start(
            out=xo[g % 4][:],
            in_=epi["xold"][tt * 128:(tt + 1) * 128, ob * 512:(ob + 1) * 512])
        assert s_lxo[g % 4].inc(ins, 16) == lxo_p[g]

    def emit_gelu(s, g):
        s.wait_ge(s_v.h, v1_p[g])
        ins = s.activation(tmp[g % 2][:], tmp[g % 2][:], AF.Gelu)
        assert s_s.inc(ins, 1) == sact_p[g]

    @block.scalar
    def _(s):
        if act == "gelu":
            for g in range(4):
                emit_xo(s, g)
            for g in range(NG):
                emit_gelu(s, g)
                if g + 4 < NG:
                    emit_xo(s, g + 4)
        else:
            for g in range(NG):
                emit_xo(s, g)

    @block.tensor
    def _(t):
        t.wait_ge(s_trd.h, tr_all)
        if extra_ready:
            t.wait_ge(s_v.h, extra_ready)
        for g, (ob, tt) in enumerate(groups):
            if tt == 0:
                t.wait_ge(s_lw[ob % 2].h, lw_p[ob])
            if g >= 2:
                t.wait_ge(s_v.h, v1_p[g - 2])
            pg = ps[g % 2]
            for dt in range(nk):
                last = (dt == nk - 1) and (extra_mm is None)
                ins = t.matmul(pg[:],
                               xlnT[:, dt * TOK + tt * 128: dt * TOK + (tt + 1) * 128],
                               wcb[ob % 2][:, dt * 512:(dt + 1) * 512],
                               start=(dt == 0), stop=last)
            if extra_mm is not None:
                ins = extra_mm(t, ob, tt, pg)
            assert s_mm.inc(ins, 1) == mm_p[g]

    @block.vector
    def _(v):
        if epi.get("bias_ready"):
            v.wait_ge(s_v.h, epi["bias_ready"])
        for g, (ob, tt) in enumerate(groups):
            v.wait_ge(s_mm.h, mm_p[g])
            pg = ps[g % 2]
            if bias_b is not None:
                ins = v.scalar_tensor_tensor(
                    tmp[g % 2][:], pg[:], scale,
                    bias_b[:, ob * 512:(ob + 1) * 512], ALU.mult, ALU.add)
            else:
                ins = v.tensor_scalar(tmp[g % 2][:], pg[:], scale, None, ALU.mult)
            assert s_v.inc(ins, 1) == v1_p[g]
            v.wait_ge(s_v.h, v1_p[g])
            if act == "gelu":
                v.wait_ge(s_s.h, sact_p[g])
            v.wait_ge(s_lxo[g % 4].h, lxo_p[g])
            if g >= 2:
                v.wait_ge(s_st[g % 2].h, st_p[g - 2])
            ins = v.tensor_tensor(xnew[g % 2][:], tmp[g % 2][:], xo[g % 4][:],
                                  ALU.add)
            assert s_v.inc(ins, 1) == v2_p[g]

    return [(s_st[0], st_p[-2]), (s_st[1], st_p[-1])] if NG % 2 == 0 else None



def emit_step1(p, block, io, tag="s1", prev_done=None, share=None):
    nc = p.nc
    cores = p.cores
    OSL = D // cores

    s_gp = p.sem("s_gp")
    s_trw = p.sem("s_trw")
    s_shw = p.sem("s_shw")
    s_mm = p.sem("s_mm")
    s_v = p.sem("s_v")
    s_cc = p.sem("s_cc")
    s_trd = p.sem("s_trd")

    es = ExitStack()
    with es:
        xlnT = p.sb(f"{tag}_xlnT", [128, NDT * TOK], BF16, es)
        A_T = p.sb(f"{tag}_AT", [128, NDT * 128], BF16, es)
        B_T = p.sb(f"{tag}_BT", [128, D], BF16, es)
        hT = p.sb(f"{tag}_hT", [128, TOK], BF16, es)
        bias_h = p.sb(f"{tag}_biash", [128, D], F32, es)
        coef = p.sb(f"{tag}_coef", [128, 1], F32, es)
        ones1 = p.sb(f"{tag}_ones1", [1, 128], F32, es)
        lw_sb = p.sb(f"{tag}_lwsb", [128, 1], F32, es)
        ps_h = p.psum(f"{tag}_psh", [128, 512], F32, es=es)

        hold = {}

        @block.vector
        def _(v):
            ins = v.memset(ones1[:], 1.0)
            hold["ones"] = s_v.inc(ins, 1)

        s_gpw = p.sem("s_gpw")
        s_gpl = p.sem("s_gpl")
        s_gplw = p.sem("s_gplw")

        if share is not None and "bar" in share:
            hold["casts"] = share["casts"]
            hold["bar"] = share["bar"]
        else:
            @block.gpsimd
            def _(gp):
                ins = gp.dma_start(out=io["WTsl_bf"], in_=io["WTsl"])
                hold["wcast"] = s_gpw.inc(ins, 16)
                ins = gp.dma_start(out=io["AT_bf"], in_=io["loraAT"])
                s_gpl.inc(ins, 16)
                ins = gp.dma_start(out=io["BT_bf"], in_=io["loraBT"])
                hold["casts"] = s_gpl.inc(ins, 16)

            hold["bar"] = emit_allgather(p, block, io["WTsl_bf"], io["sh_WT"],
                                         wait_gp=(s_gpw, hold["wcast"]))
            if share is not None:
                share["casts"] = hold["casts"]
                share["bar"] = hold["bar"]

        @block.sync
        def _(sync):
            sync.wait_ge(p.sem("s_gpl").h, hold["casts"])
            ins = sync.dma_start(
                out=A_T[:].rearrange("p (dt r) -> p dt r", r=128),
                in_=io["AT_bf"].rearrange("(dt p) r -> p dt r", p=128))
            s_trw.inc(ins, 16)
            ins = sync.dma_start(out=B_T[:], in_=io["BT_bf"])
            hold["lora_T"] = s_trw.inc(ins, 16)

        @block.gpsimd
        def _(gp):
            if prev_done:
                for _s, _val in prev_done:
                    gp.wait_ge(_s.h, _val)
            ins = gp.dma_start(out=lw_sb[:], in_=io["lora_w"])
            hold["lw"] = s_gplw.inc(ins, 16)

        @block.vector
        def _(v):
            v.wait_ge(s_gplw.h, hold["lw"])
            ins = v.tensor_scalar(coef[:], lw_sb[:], 2.0, None, ALU.mult)
            hold["coef"] = s_v.inc(ins, 1)

        bias_ready = emit_bcast_row(p, block, es, io["bias"], D, bias_h, 0.5,
                                    f"{tag}b", ones1, hold["ones"])

        with ExitStack() as es_ln:
            tr_all, _ = emit_ln(p, block, es_ln, io["x_in"], io["xln_tm"], xlnT,
                                f"{tag}ln", x_ready=prev_done)

        hT_mm_p = [s_mm.v + 1, s_mm.v + 2]
        hT_v_p = [s_v.v + 1, s_v.v + 2]

        @block.tensor
        def _(t):
            t.wait_ge(s_trw.h, hold["lora_T"])
            t.wait_ge(s_trd.h, tr_all)
            for th in range(2):
                if th == 1:
                    t.wait_ge(s_v.h, hT_v_p[0])
                for dt in range(NDT):
                    ins = t.matmul(
                        ps_h[:],
                        A_T[:, dt * 128:(dt + 1) * 128],
                        xlnT[:, dt * TOK + th * 512: dt * TOK + (th + 1) * 512],
                        start=(dt == 0), stop=(dt == NDT - 1))
                assert s_mm.inc(ins, 1) == hT_mm_p[th]

        @block.vector
        def _(v):
            v.wait_ge(s_v.h, hold["coef"])
            for th in range(2):
                v.wait_ge(s_mm.h, hT_mm_p[th])
                ins = v.tensor_scalar(hT[:, th * 512:(th + 1) * 512], ps_h[:],
                                      coef[:, 0:1], None, ALU.mult)
                assert s_v.inc(ins, 1) == hT_v_p[th]

        def w_colblock(sync, ob, dest, sem):
            if ob == 0:
                sync.wait_ge(s_cc.h, hold["bar"])
            srcap = io["sh_WT"].rearrange("(dt p) o -> p dt o", p=128)[:, :, ob * 512:(ob + 1) * 512]
            ins = sync.dma_start(out=dest[:].rearrange("p (dt j) -> p dt j", j=512),
                                 in_=srcap)
            sem.inc(ins, 16)

        def lora_mm(t, ob, tt, pg):
            return t.matmul(pg[:], hT[:, tt * 128:(tt + 1) * 128],
                            B_T[:, ob * 512:(ob + 1) * 512],
                            start=False, stop=True)

        st_last = emit_mm_main(
            p, block, es, xlnT, tr_all, w_colblock, D,
            extra_mm=lora_mm, extra_ready=hT_v_p[-1], tag=tag,
            epi=dict(scale=0.5, bias_b=bias_h, act=None,
                     xold=io["x_in"], xout=io["x_out"],
                     bias_ready=bias_ready))

        @block.sync
        def _(sync):
            for sem, val in st_last:
                sync.wait_ge(sem.h, val)

    return st_last




def emit_step5(p, block, io, tag="s5", prev_done=None):
    nc = p.nc
    cores = p.cores
    TSL = D // cores

    s_gp = p.sem("s_gp")
    s_gpw = p.sem("s_gpw")
    s_gpl = p.sem("s_gpl")
    s_mm = p.sem("s_mm")
    s_v = p.sem("s_v")
    s_cc = p.sem("s_cc")
    s_trd = p.sem("s_trd")

    es = ExitStack()
    with es:
        xlnT = p.sb(f"{tag}_xlnT", [128, NDT * TOK], BF16, es)
        bias_b = p.sb(f"{tag}_biasb", [128, D], F32, es)
        ones1 = p.sb(f"{tag}_ones1", [1, 128], F32, es)
        hold = {}

        @block.vector
        def _(v):
            ins = v.memset(ones1[:], 1.0)
            hold["ones"] = s_v.inc(ins, 1)

        @block.gpsimd
        def _(gp):
            ins = gp.dma_start(out=io["w1sl_bf"], in_=io["w1sl"])
            hold["w1cast"] = s_gpw.inc(ins, 16)
            ins = gp.dma_start(out=io["w2Tsl_bf"], in_=io["w2Tsl"])
            hold["w2cast"] = s_gpl.inc(ins, 16)

        hold["agw1"] = emit_allgather(p, block, io["w1sl_bf"], io["sh_w1"],
                                      wait_gp=(s_gpw, hold["w1cast"]))

        bias_ready = emit_bcast_row(p, block, es, io["b2"], D, bias_b, 1.0,
                                    f"{tag}b", ones1, hold["ones"])

        NST = D // 128
        s_lb = [p.sem("s_lb0"), p.sem("s_lb1")]
        s_msk = [p.sem("s_msk0"), p.sem("s_msk1")]
        s_w2r = p.sem("s_w2r")
        s_asw = [p.sem("s_asw0"), p.sem("s_asw1")]

        es_b = ExitStack()
        w2T_sb = p.sb(f"{tag}_w2T", [128, NDT * TSL], BF16, es_b)
        w1cb = [p.sb(f"{tag}_w1cb{i}", [128, NDT * 128], BF16, es_b)
                for i in range(2)]
        msk = [p.sb(f"{tag}_msk{i}", [128, TSL], F32, es_b) for i in range(2)]
        asl = [p.sb(f"{tag}_asl{i}", [128, TSL], BF16, es_b) for i in range(2)]
        ps_b = [p.psum(f"{tag}_psb{i}", [128, 512], F32, es=es_b)
                for i in range(2)]

        with ExitStack() as es_ln:
            tr_all, _ = emit_ln(p, block, es_ln, io["x_in"], io["xln_tm"], xlnT,
                                f"{tag}ln", x_ready=prev_done)

        if True:

            mm0b, v0b = s_mm.v, s_v.v
            mmb_p = [mm0b + st * (TSL // 512) + c + 1
                     for st in range(NST) for c in range(TSL // 512)]
            vb_p = [v0b + st + 1 for st in range(NST)]
            lb_p = [s_lb[st % 2].v + 16 * (st // 2 + 1) for st in range(NST)]
            msk_p = [s_msk[st % 2].v + 16 * (st // 2 + 1) for st in range(NST)]
            asw_p = [s_asw[st % 2].v + 16 * (st // 2 + 1) for st in range(NST)]

            @block.sync
            def _(sync):
                if prev_done:
                    for sem, val in prev_done:
                        sync.wait_ge(sem.h, val)
                sync.wait_ge(s_gpl.h, hold["w2cast"])
                ins = sync.dma_start(
                    out=w2T_sb[:].rearrange("p (dt j) -> p dt j", j=TSL),
                    in_=io["w2Tsl_bf"].rearrange("(dt p) j -> p dt j", p=128))
                hold["w2r"] = s_w2r.inc(ins, 16)
                sync.wait_ge(s_cc.h, hold["agw1"])
                for st in range(NST):
                    if st >= 2:
                        sync.wait_ge(s_mm.h, mmb_p[(st - 2) * (TSL // 512) + TSL // 512 - 1])
                        sync.wait_ge(s_v.h, vb_p[st - 2])
                        ins = sync.dma_start(
                            out=io["attn_sl"][(st - 2) * 128:(st - 1) * 128, :],
                            in_=asl[st % 2][:])
                        assert s_asw[st % 2].inc(ins, 16) == asw_p[st - 2]
                    ins = sync.dma_start(
                        out=w1cb[st % 2][:].rearrange("p (dt j) -> p dt j", j=128),
                        in_=io["sh_w1"].rearrange("(dt p) s -> p dt s", p=128)[:, :, st * 128:(st + 1) * 128])
                    assert s_lb[st % 2].inc(ins, 16) == lb_p[st]
                    ins = sync.dma_start(out=msk[st % 2][:],
                                         in_=io["mask_sl"][st * 128:(st + 1) * 128, :])
                    assert s_msk[st % 2].inc(ins, 16) == msk_p[st]
                for st in (NST - 2, NST - 1):
                    sync.wait_ge(s_mm.h, mmb_p[st * (TSL // 512) + TSL // 512 - 1])
                    sync.wait_ge(s_v.h, vb_p[st])
                    ins = sync.dma_start(
                        out=io["attn_sl"][st * 128:(st + 1) * 128, :],
                        in_=asl[st % 2][:])
                    assert s_asw[st % 2].inc(ins, 16) == asw_p[st]

            @block.tensor
            def _(t):
                t.wait_ge(s_w2r.h, hold["w2r"])
                for st in range(NST):
                    t.wait_ge(s_lb[st % 2].h, lb_p[st])
                    if st >= 2:
                        t.wait_ge(s_v.h, vb_p[st - 2])
                    for c in range(TSL // 512):
                        for dt in range(NDT):
                            ins = t.matmul(
                                ps_b[st % 2][:],
                                w1cb[st % 2][:, dt * 128:(dt + 1) * 128],
                                w2T_sb[:, dt * TSL + c * 512: dt * TSL + (c + 1) * 512],
                                start=(dt == 0), stop=(dt == NDT - 1))
                        assert s_mm.inc(ins, 1) == mmb_p[st * (TSL // 512) + c]

            @block.vector
            def _(v):
                for st in range(NST):
                    v.wait_ge(s_mm.h, mmb_p[st * (TSL // 512) + TSL // 512 - 1])
                    v.wait_ge(s_msk[st % 2].h, msk_p[st])
                    if st >= 2:
                        v.wait_ge(s_asw[st % 2].h, asw_p[st - 2])
                    for c in range(TSL // 512):
                        ins = v.tensor_tensor(
                            asl[st % 2][:, c * 512:(c + 1) * 512],
                            ps_b[st % 2][:, 0:512] if TSL == 512 else ps_b[st % 2][:],
                            msk[st % 2][:, c * 512:(c + 1) * 512], ALU.mult)
                    assert s_v.inc(ins, 1) == vb_p[st]

            @block.gpsimd
            def _(gp):
                gp.wait_ge(s_asw[0].h, asw_p[NST - 2])
                gp.wait_ge(s_asw[1].h, asw_p[NST - 1])
            hold["agattn"] = emit_allgather(p, block, io["attn_sl"], io["sh_attn"])

        es_b.close()

        def w_colblock(sync, ob, dest, sem):
            if ob == 0:
                sync.wait_ge(s_cc.h, hold["agattn"])
            srcap = io["sh_attn"].rearrange("(b dt p) j -> b p dt j", b=cores, p=128)[ob]
            ins = sync.dma_start(out=dest[:].rearrange("p (dt j) -> p dt j", j=512),
                                 in_=srcap)
            sem.inc(ins, 16)

        st_last = emit_mm_main(
            p, block, es, xlnT, tr_all, w_colblock, D,
            tag=tag,
            epi=dict(scale=1.0, bias_b=bias_b, act="gelu",
                     xold=io["x_in"], xout=io["x_out"],
                     bias_ready=bias_ready))

        @block.sync
        def _(sync):
            for sem, val in st_last:
                sync.wait_ge(sem.h, val)

    return st_last


def build_step5_nc(cores=8):
    nc = bass.Bass("TRN2", target_bir_lowering=False)
    TSL = D // cores
    io = {}
    io["x_in"] = nc.dram_tensor("x_in", [TOK, D], F32, kind="ExternalInput").ap()
    io["x_out"] = nc.dram_tensor("x_out", [TOK, D], F32, kind="ExternalOutput").ap()
    io["w1sl"] = nc.dram_tensor("w1sl", [TSL, D], F32, kind="ExternalInput").ap()
    io["w2Tsl"] = nc.dram_tensor("w2Tsl", [D, TSL], F32, kind="ExternalInput").ap()
    io["mask_sl"] = nc.dram_tensor("mask_sl", [D, TSL], F32, kind="ExternalInput").ap()
    io["b2"] = nc.dram_tensor("b2", [1, D], F32, kind="ExternalInput").ap()

    io["w1sl_bf"] = nc.dram_tensor("w1sl_bf", [TSL, D], BF16).ap()
    io["w2Tsl_bf"] = nc.dram_tensor("w2Tsl_bf", [D, TSL], BF16).ap()
    io["attn_sl"] = nc.dram_tensor("attn_sl", [D, TSL], BF16).ap()
    io["xln_tm"] = nc.dram_tensor("xln_tm", [TOK, D], BF16).ap()
    io["sh_w1"] = nc.dram_tensor("sh_w1", [D, D], BF16, addr_space="Shared").ap()
    io["sh_attn"] = nc.dram_tensor("sh_attn", [cores * D, TSL], BF16,
                                   addr_space="Shared").ap()

    with ExitStack() as es:
        p = P(nc, es, cores)
        block = es.enter_context(nc.Block())
        emit_step5(p, block, io)
    return nc


def step5_in_maps(inputs, x_full, cores=8):
    w1 = np.asarray(inputs["L1_attn_w1"], np.float32)
    w2 = np.asarray(inputs["L1_attn_w2"], np.float32)
    mask = np.asarray(inputs["L1_attn_sparse_mask"], np.float32)
    w2T = np.ascontiguousarray(w2.T)
    TSL = D // cores
    maps = []
    for k in range(cores):
        maps.append(dict(
            x_in=np.ascontiguousarray(x_full[k * TOK:(k + 1) * TOK]),
            w1sl=np.ascontiguousarray(w1[k * TSL:(k + 1) * TSL]),
            w2Tsl=np.ascontiguousarray(w2T[:, k * TSL:(k + 1) * TSL]),
            mask_sl=np.ascontiguousarray(mask[:, k * TSL:(k + 1) * TSL]),
            b2=np.ascontiguousarray(
                np.asarray(inputs["L1_attn_b2"], np.float32).reshape(1, D)),
        ))
    return maps




def emit_step4(p, block, io, tag="s4", prev_done=None, share=None):
    nc = p.nc
    cores = p.cores
    NH = H_HEADS

    s_gp = p.sem("s_gp")
    s_gpw = p.sem("s_gpw")
    s_gpl = p.sem("s_gpl")
    s_gpp = p.sem("s_gpp")
    s_mm = p.sem("s_mm")
    s_v = p.sem("s_v")
    s_s = p.sem("s_s")
    s_cc = p.sem("s_cc")
    s_trd = p.sem("s_trd")
    s_gx = [p.sem("s_gx0"), p.sem("s_gx1")]
    s_lwu = [p.sem("s_lwu0"), p.sem("s_lwu1")]
    s_gs = [p.sem("s_gs0"), p.sem("s_gs1")]
    s_dl = [p.sem("s_dl0"), p.sem("s_dl1"), p.sem("s_dl2"), p.sem("s_dl3")]
    s_axo = [p.sem("s_axo0"), p.sem("s_axo1")]
    s_ast = [p.sem("s_ast0"), p.sem("s_ast1")]
    s_gpc = p.sem("s_gpc")

    es = ExitStack()
    with es:
        xlnT = p.sb(f"{tag}_xlnT", [128, NDT * TOK], BF16, es)
        sperm = p.sb(f"{tag}_perm", [128, 32], I32, es)
        ident = p.sb(f"{tag}_ident", [128, 128], F32, es)
        xg = [[p.sb(f"{tag}_xg{i}{j}", [128, TOK], BF16, es) for j in range(2)]
              for i in range(2)]
        ub = [p.sb(f"{tag}_ub{i}", [128, 2 * H_M], BF16, es) for i in range(2)]
        db = [p.sb(f"{tag}_db{i}", [128, 8 * H_DH], BF16, es) for i in range(2)]
        ug = [p.sb(f"{tag}_ug{i}", [128, 8 * TOK], BF16, es) for i in range(2)]
        dnT = [[p.sb(f"{tag}_dn{i}{j}", [128, TOK], F32, es) for j in range(2)]
               for i in range(2)]
        ps_u = [p.psum(f"{tag}_psu{i}", [128, 512], F32, es=es) for i in range(2)]
        ps_d = [p.psum(f"{tag}_psd{i}", [128, 512], F32, es=es) for i in range(2)]
        ps_t = [p.psum(f"{tag}_pst{i}", [128, 128], F32, es=es) for i in range(2)]
        hold = {}

        @block.vector
        def _(v):
            ins = v.memset(ident[:], 1.0)
            hold["id1"] = s_v.inc(ins, 1)

        if share is not None and "agd" in share:
            @block.gpsimd
            def _(gp):
                gp.wait_ge(s_v.h, hold["id1"])
                ins = gp.affine_select(ident[:], ident[:], [[-1, 128]],
                                       ALU.is_equal, 0.0, base=0,
                                       channel_multiplier=1)
                hold["ident"] = s_gpc.inc(ins, 1)
                if prev_done:
                    for _s, _val in prev_done:
                        gp.wait_ge(_s.h, _val)
                ins = gp.dma_start(out=sperm[:], in_=io["perm_t"])
                hold["perm"] = s_gpp.inc(ins, 16)
            hold["agu"] = share["agu"]
            hold["agd"] = share["agd"]
        else:
            @block.gpsimd
            def _(gp):
                ins = gp.dma_start(out=io["upsl_bf"], in_=io["upsl"])
                hold["upcast"] = s_gpw.inc(ins, 16)
                ins = gp.dma_start(out=io["dnsl_bf"], in_=io["dnsl"])
                hold["dncast"] = s_gpl.inc(ins, 16)
                gp.wait_ge(s_v.h, hold["id1"])
                ins = gp.affine_select(ident[:], ident[:], [[-1, 128]],
                                       ALU.is_equal, 0.0, base=0,
                                       channel_multiplier=1)
                hold["ident"] = s_gpc.inc(ins, 1)
                if prev_done:
                    for _s, _val in prev_done:
                        gp.wait_ge(_s.h, _val)
                ins = gp.dma_start(out=sperm[:], in_=io["perm_t"])
                hold["perm"] = s_gpp.inc(ins, 16)

            hold["agu"] = emit_allgather(p, block, io["upsl_bf"], io["sh_up"],
                                         wait_gp=(s_gpw, hold["upcast"]))
            hold["agd"] = emit_allgather(p, block, io["dnsl_bf"], io["sh_dn"],
                                         wait_gp=(s_gpl, hold["dncast"]))
            if share is not None:
                share["agu"] = hold["agu"]
                share["agd"] = hold["agd"]

        with ExitStack() as es_ln:
            tr_all, _ = emit_ln(p, block, es_ln, io["x_in"], io["xln_tm"], xlnT,
                                f"{tag}ln", fm_dram=io["xln_fm"], x_ready=prev_done)

        mm0, v0, s0 = s_mm.v, s_v.v, s_s.v
        gx0 = [s.v for s in s_gx]
        lwu0 = [s.v for s in s_lwu]
        gs0 = [s.v for s in s_gs]
        up_p = {}
        dn_p = {}
        gelu_p = {}
        dnv_p = {}
        for h in range(NH):
            for k in range(16):
                up_p[(h, k)] = mm0 + 20 * h + k + 1
            for j in range(4):
                dn_p[(h, j)] = mm0 + 20 * h + 16 + j + 1
            for k in range(16):
                gelu_p[(h, k)] = s0 + 16 * h + k + 1
            for j in range(4):
                dnv_p[(h, j)] = v0 + 4 * h + j + 1
        gx_p = [gx0[h % 2] + 32 * (h // 2 + 1) for h in range(NH)]
        lwu_p = [lwu0[h % 2] + 32 * (h // 2 + 1) for h in range(NH)]
        gs_p = [gs0[h % 2] + 32 * (h // 2 + 1) for h in range(NH)]

        @block.gpsimd
        def _(gp):
            gp.wait_ge(s_gpp.h, hold["perm"])
            gp.wait_ge(s_trd.h, tr_all)
            for h in range(NH):
                if h >= 2:
                    gp.wait_ge(s_mm.h, up_p[(h - 2, 15)])
                for i in range(2):
                    ins = gp.indirect_dma_start(
                        xg[h % 2][i][:], None, io["xln_fm"],
                        IndirectOffsetOnAxis(sperm[:, 2 * h + i:2 * h + i + 1], 0))
                    s_gx[h % 2].inc(ins, 16)
                assert s_gx[h % 2].v == gx_p[h]
                if h >= 1:
                    hh = h - 1
                    gp.wait_ge(s_v.h, dnv_p[(hh, 3)])
                    for i in range(2):
                        ins = gp.indirect_dma_start(
                            io["deltaT"],
                            IndirectOffsetOnAxis(sperm[:, 2 * hh + i:2 * hh + i + 1], 0),
                            dnT[hh % 2][i][:], None)
                        s_gs[hh % 2].inc(ins, 16)
                    assert s_gs[hh % 2].v == gs_p[hh]
            hh = NH - 1
            gp.wait_ge(s_v.h, dnv_p[(hh, 3)])
            for i in range(2):
                ins = gp.indirect_dma_start(
                    io["deltaT"],
                    IndirectOffsetOnAxis(sperm[:, 2 * hh + i:2 * hh + i + 1], 0),
                    dnT[hh % 2][i][:], None)
                s_gs[hh % 2].inc(ins, 16)
            assert s_gs[hh % 2].v == gs_p[hh]

        @block.sync
        def _(sync):
            sync.wait_ge(s_cc.h, hold["agd"])
            for h in range(NH):
                if h >= 2:
                    sync.wait_ge(s_mm.h, dn_p[(h - 2, 3)])
                ins = sync.dma_start(
                    out=ub[h % 2][:].rearrange("p (dt m) -> p dt m", m=H_M),
                    in_=io["sh_up"].rearrange("(hh dt p) m -> hh p dt m", hh=NH, p=128)[h])
                s_lwu[h % 2].inc(ins, 16)
                ins = sync.dma_start(
                    out=db[h % 2][:].rearrange("p (mt d) -> p mt d", d=H_DH),
                    in_=io["sh_dn"].rearrange("(hh mt p) d -> hh p mt d", hh=NH, p=128)[h])
                s_lwu[h % 2].inc(ins, 16)
                assert s_lwu[h % 2].v == lwu_p[h]

        @block.tensor
        def _(t):
            for h in range(NH):
                t.wait_ge(s_gx[h % 2].h, gx_p[h])
                t.wait_ge(s_lwu[h % 2].h, lwu_p[h])
                for mt in range(8):
                    for th in range(2):
                        k = mt * 2 + th
                        if 20 * h + k + 1 > 2:
                            pass
                        if k >= 2:
                            t.wait_ge(s_s.h, gelu_p[(h, k - 2)])
                        elif h >= 1:
                            t.wait_ge(s_s.h, gelu_p[(h - 1, 14 + k)])
                        for dt in range(2):
                            ins = t.matmul(
                                ps_u[k % 2][:],
                                ub[h % 2][:, dt * H_M + mt * 128: dt * H_M + (mt + 1) * 128],
                                xg[h % 2][dt][:, th * 512:(th + 1) * 512],
                                start=(dt == 0), stop=(dt == 1))
                        assert s_mm.inc(ins, 1) == up_p[(h, k)]
                t.wait_ge(s_s.h, gelu_p[(h, 15)])
                for dtile in range(2):
                    for th in range(2):
                        j = dtile * 2 + th
                        if j >= 2:
                            t.wait_ge(s_v.h, dnv_p[(h, j - 2)])
                        elif h >= 1:
                            t.wait_ge(s_v.h, dnv_p[(h - 1, 2 + j)])
                        for mt in range(8):
                            ins = t.matmul(
                                ps_d[j % 2][:],
                                db[h % 2][:, mt * H_DH + dtile * 128: mt * H_DH + (dtile + 1) * 128],
                                ug[h % 2][:, mt * TOK + th * 512: mt * TOK + (th + 1) * 512],
                                start=(mt == 0), stop=(mt == 7))
                        assert s_mm.inc(ins, 1) == dn_p[(h, j)]

        @block.scalar
        def _(s):
            for h in range(NH):
                for mt in range(8):
                    for th in range(2):
                        k = mt * 2 + th
                        s.wait_ge(s_mm.h, up_p[(h, k)])
                        if k == 0 and h >= 2:
                            s.wait_ge(s_mm.h, dn_p[(h - 2, 3)])
                        ins = s.activation(
                            ug[h % 2][:, mt * TOK + th * 512: mt * TOK + (th + 1) * 512],
                            ps_u[k % 2][:], AF.Gelu)
                        assert s_s.inc(ins, 1) == gelu_p[(h, k)]

        @block.vector
        def _(v):
            for h in range(NH):
                for dtile in range(2):
                    for th in range(2):
                        j = dtile * 2 + th
                        v.wait_ge(s_mm.h, dn_p[(h, j)])
                        if j == 0 and h >= 2:
                            v.wait_ge(s_gs[h % 2].h, gs_p[h - 2])
                        ins = v.tensor_scalar(
                            dnT[h % 2][dtile][:, th * 512:(th + 1) * 512],
                            ps_d[j % 2][:], 0.5, None, ALU.mult)
                        assert s_v.inc(ins, 1) == dnv_p[(h, j)]

        TBT = 2
        NTB = NTT // TBT
        dld = [p.sb(f"{tag}_dld{i}", [128, TBT * 128], F32, es) for i in range(4)]
        xo_t = [p.sb(f"{tag}_xot{i}", [128, D], F32, es) for i in range(2)]

        mmA0, vA0 = s_mm.v, s_v.v
        dl0 = [s.v for s in s_dl]
        axo0 = [s.v for s in s_axo]
        ast0 = [s.v for s in s_ast]
        trp = {}
        vap = {}
        for tb in range(NTB):
            for vt in range(NDT):
                for i in range(TBT):
                    idx = tb * NDT * TBT + vt * TBT + i
                    trp[(tb, vt, i)] = mmA0 + idx + 1
                    vap[(tb, vt, i)] = vA0 + idx + 1
        dl_p = {}
        for tb in range(NTB):
            for vt in range(NDT):
                dl_p[(tb, vt)] = dl0[vt % 4] + 16 * (tb * (NDT // 4) + vt // 4 + 1)
        axo_p = {}
        ast_p = {}
        for tb in range(NTB):
            for i in range(TBT):
                axo_p[(tb, i)] = axo0[i] + 16 * (tb + 1)
                ast_p[(tb, i)] = ast0[i] + 16 * (tb + 1)

        @block.sync
        def _(sync):
            sync.wait_ge(s_gs[0].h, gs_p[NH - 2])
            sync.wait_ge(s_gs[1].h, gs_p[NH - 1])
            for tb in range(NTB):
                for vt in range(0, NDT, 2):
                    idx4 = tb * NDT + vt
                    if idx4 >= 4:
                        ptb, pv = divmod(idx4 - 4, NDT)
                        sync.wait_ge(s_mm.h, trp[(ptb, pv, TBT - 1)])
                    ins = sync.dma_start(
                        out=dld[vt % 4][:],
                        in_=io["deltaT"][vt * 128:(vt + 1) * 128,
                                         tb * TBT * 128:(tb + 1) * TBT * 128])
                    assert s_dl[vt % 4].inc(ins, 16) == dl_p[(tb, vt)]
                for i in range(TBT):
                    sync.wait_ge(s_v.h, vap[(tb, NDT - 1, i)])
                    tt = tb * TBT + i
                    ins = sync.dma_start(
                        out=io["x_out"][tt * 128:(tt + 1) * 128, :],
                        in_=xo_t[i][:])
                    assert s_ast[i].inc(ins, 16) == ast_p[(tb, i)]

        @block.scalar
        def _(s):
            s.wait_ge(s_gs[0].h, gs_p[NH - 2])
            s.wait_ge(s_gs[1].h, gs_p[NH - 1])
            for tb in range(NTB):
                for i in range(TBT):
                    if tb >= 1:
                        s.wait_ge(s_ast[i].h, ast_p[(tb - 1, i)])
                    tt = tb * TBT + i
                    ins = s.dma_start(out=xo_t[i][:],
                                      in_=io["x_in"][tt * 128:(tt + 1) * 128, :])
                    assert s_axo[i].inc(ins, 16) == axo_p[(tb, i)]
                for vt in range(1, NDT, 2):
                    idx4 = tb * NDT + vt
                    if idx4 >= 4:
                        ptb, pv = divmod(idx4 - 4, NDT)
                        s.wait_ge(s_mm.h, trp[(ptb, pv, TBT - 1)])
                    ins = s.dma_start(
                        out=dld[vt % 4][:],
                        in_=io["deltaT"][vt * 128:(vt + 1) * 128,
                                         tb * TBT * 128:(tb + 1) * TBT * 128])
                    assert s_dl[vt % 4].inc(ins, 16) == dl_p[(tb, vt)]

        @block.tensor
        def _(t):
            t.wait_ge(s_gpc.h, hold["ident"])
            for tb in range(NTB):
                for vt in range(NDT):
                    t.wait_ge(s_dl[vt % 4].h, dl_p[(tb, vt)])
                    for i in range(TBT):
                        idx = tb * NDT * TBT + vt * TBT + i
                        if idx >= 2:
                            pidx = idx - 2
                            ptb, r = divmod(pidx, NDT * TBT)
                            pv, pi = divmod(r, TBT)
                            t.wait_ge(s_v.h, vap[(ptb, pv, pi)])
                        ins = t.transpose(ps_t[idx % 2][:],
                                          dld[vt % 4][:, i * 128:(i + 1) * 128],
                                          ident[:])
                        assert s_mm.inc(ins, 1) == trp[(tb, vt, i)]

        @block.vector
        def _(v):
            for tb in range(NTB):
                for vt in range(NDT):
                    for i in range(TBT):
                        idx = tb * NDT * TBT + vt * TBT + i
                        v.wait_ge(s_mm.h, trp[(tb, vt, i)])
                        if vt == 0:
                            v.wait_ge(s_axo[i].h, axo_p[(tb, i)])
                        ins = v.tensor_tensor(
                            xo_t[i][:, vt * 128:(vt + 1) * 128],
                            ps_t[idx % 2][:],
                            xo_t[i][:, vt * 128:(vt + 1) * 128], ALU.add)
                        assert s_v.inc(ins, 1) == vap[(tb, vt, i)]

        @block.sync
        def _(sync):
            sync.wait_ge(s_ast[0].h, ast_p[(NTB - 1, 0)])
            sync.wait_ge(s_ast[1].h, ast_p[(NTB - 1, 1)])

    return [(s_ast[0], ast_p[(NTB - 1, 0)]), (s_ast[1], ast_p[(NTB - 1, 1)])]


def build_step4_nc(cores=8):
    nc = bass.Bass("TRN2", target_bir_lowering=False)
    HSL = H_HEADS // cores
    io = {}
    io["x_in"] = nc.dram_tensor("x_in", [TOK, D], F32, kind="ExternalInput").ap()
    io["x_out"] = nc.dram_tensor("x_out", [TOK, D], F32, kind="ExternalOutput").ap()
    io["upsl"] = nc.dram_tensor("upsl", [HSL * H_DH, H_M], F32, kind="ExternalInput").ap()
    io["dnsl"] = nc.dram_tensor("dnsl", [HSL * H_M, H_DH], F32, kind="ExternalInput").ap()
    io["perm_t"] = nc.dram_tensor("perm_t", [128, 32], I32, kind="ExternalInput").ap()

    io["upsl_bf"] = nc.dram_tensor("upsl_bf", [HSL * H_DH, H_M], BF16).ap()
    io["dnsl_bf"] = nc.dram_tensor("dnsl_bf", [HSL * H_M, H_DH], BF16).ap()
    io["xln_tm"] = nc.dram_tensor("xln_tm", [TOK, D], BF16).ap()
    io["xln_fm"] = nc.dram_tensor("xln_fm", [D, TOK], BF16).ap()
    io["deltaT"] = nc.dram_tensor("deltaT", [D, TOK], F32).ap()
    io["sh_up"] = nc.dram_tensor("sh_up", [H_HEADS * H_DH, H_M], BF16,
                                 addr_space="Shared").ap()
    io["sh_dn"] = nc.dram_tensor("sh_dn", [H_HEADS * H_M, H_DH], BF16,
                                 addr_space="Shared").ap()

    with ExitStack() as es:
        p = P(nc, es, cores)
        block = es.enter_context(nc.Block())
        emit_step4(p, block, io)
    return nc


def step4_in_maps(inputs, x_full, cores=8):
    up = np.asarray(inputs["L1_ffn_up"], np.float32)
    dn = np.asarray(inputs["L1_ffn_down"], np.float32)
    perm = np.asarray(inputs["L1_ffn_perm"], np.int32)
    perm_t = np.ascontiguousarray(perm.reshape(32, 128).T.astype(np.int32))
    HSL = H_HEADS // cores
    maps = []
    for k in range(cores):
        maps.append(dict(
            x_in=np.ascontiguousarray(x_full[k * TOK:(k + 1) * TOK]),
            upsl=np.ascontiguousarray(
                up[k * HSL:(k + 1) * HSL].reshape(HSL * H_DH, H_M)),
            dnsl=np.ascontiguousarray(
                dn[k * HSL:(k + 1) * HSL].reshape(HSL * H_M, H_DH)),
            perm_t=perm_t,
        ))
    return maps



def emit_step2(p, block, io, tag="s2", prev_done=None):
    nc = p.nc
    cores = p.cores
    AD = A_DIM
    NCT = AD // 128
    NFT = A_DFF // 128

    s_gpw = p.sem("s_gpw")
    s_gpl = p.sem("s_gpl")
    s_gpt = p.sem("s_gpt")
    s_gpz = p.sem("s_gpz")
    s_gpc = p.sem("s_gpc")
    s_mm = p.sem("s_mm")
    s_v = p.sem("s_v")
    s_s = p.sem("s_s")
    s_cc = p.sem("s_cc")
    s_trd = p.sem("s_trd")
    s_lwa = p.sem("s_lwa")
    s_lw2 = p.sem("s_lw2")
    s_ytm = [p.sem("s_ytm0"), p.sem("s_ytm1")]
    s_ytr = p.sem("s_ytr")
    s_tab = p.sem("s_tab")
    s_el = p.sem("s_el")
    s_sml = p.sem("s_sml")

    def vedge(v, ins):
        v.wait_ge(s_v.h, s_v.inc(ins, 1))

    def sedge(s, ins):
        s.wait_ge(s_s.h, s_s.inc(ins, 1))

    es = ExitStack()
    with es:
        h = [p.sb(f"{tag}_h{t}", [128, AD], F32, es) for t in range(NTT)]
        ident = p.sb(f"{tag}_ident", [128, 128], F32, es)
        ones1 = p.sb(f"{tag}_ones1", [1, 128], F32, es)
        onesw = p.sb(f"{tag}_onesw", [128, TOK], F32, es)
        zbre = p.sb(f"{tag}_zbre", [128, 2], F32, es)
        zbim = p.sb(f"{tag}_zbim", [128, 2], F32, es)
        zre_t = p.sb(f"{tag}_zret", [128, 16], F32, es)
        zim_t = p.sb(f"{tag}_zimt", [128, 16], F32, es)
        lc_t = p.sb(f"{tag}_lct", [128, 32], F32, es)
        scl_t = p.sb(f"{tag}_sclt", [128, 16], F32, es)
        b1_t = p.sb(f"{tag}_b1t", [128, 64], F32, es)
        qsel = p.sb(f"{tag}_qsel", [128, 4], F32, es)
        c_t1 = p.sb(f"{tag}_ct1", [128, 16], F32, es)
        c_t2 = p.sb(f"{tag}_ct2", [128, 16], F32, es)
        zabs = p.sb(f"{tag}_zabs", [128, 16], F32, es)
        rmagA = p.sb(f"{tag}_rmagA", [128, 16], F32, es)
        urA = p.sb(f"{tag}_urA", [128, 16], F32, es)
        uiA = p.sb(f"{tag}_uiA", [128, 16], F32, es)
        bin_b = p.sb(f"{tag}_binb", [128, AD], F32, es)
        lnlw_b = p.sb(f"{tag}_lnlwb", [128, AD], F32, es)
        lnlb_b = p.sb(f"{tag}_lnlbb", [128, AD], F32, es)
        vrow = p.sb(f"{tag}_vrow", [1, AD], F32, es)
        hfT = p.sb(f"{tag}_hfT", [128, NCT * TOK], BF16, es)
        yscr = [p.sb(f"{tag}_yscr{i}", [128, AD], F32, es) for i in range(2)]
        y_tm = [p.sb(f"{tag}_ytmb{i}", [128, AD], BF16, es) for i in range(2)]
        ident_bf = p.sb(f"{tag}_idbf", [128, 128], BF16, es)
        yT = [p.sb(f"{tag}_yT{i}", [128, TOK], BF16, es) for i in range(4)]
        stat6 = p.sb(f"{tag}_st6", [128, 6], F32, es)
        agg2 = p.sb(f"{tag}_agg2", [128, 2 * NTT], F32, es)
        vep = p.sb(f"{tag}_vep", [128, NTT], F32, es)
        sdv = p.sb(f"{tag}_sdv", [128, NTT], F32, es)
        rsv = p.sb(f"{tag}_rsv", [128, NTT], F32, es)
        nbv = p.sb(f"{tag}_nbv", [128, NTT], F32, es)
        tscr = p.sb(f"{tag}_tscr", [128, 8], F32, es)
        pb = [p.psum(f"{tag}_bps{i}", [128, 512], F32, es=es) for i in range(2)]
        hold = {}

        @block.vector
        def _(v):
            v.memset(ones1[:], 1.0)
            v.memset(onesw[:], 1.0)
            ins = v.memset(ident[:], 1.0)
            hold["id1"] = s_v.inc(ins, 1)

        @block.gpsimd
        def _(gp):
            ins = gp.dma_start(out=io["WinTsl_bf"], in_=io["WinTsl"])
            hold["wincast"] = s_gpw.inc(ins, 16)
            ins = gp.dma_start(out=io["W1Tsl_bf"], in_=io["W1Tsl"])
            s_gpl.inc(ins, 16)
            ins = gp.dma_start(out=io["W2Tsl_bf"], in_=io["W2Tsl"])
            hold["wcast"] = s_gpl.inc(ins, 16)
            ins = gp.dma_start(out=io["WoutTsl_bf"], in_=io["WoutTsl"])
            hold["woutcast"] = s_gpt.inc(ins, 16)
            gp.wait_ge(s_v.h, hold["id1"])
            ins = gp.affine_select(ident[:], ident[:], [[-1, 128]], ALU.is_equal,
                                   0.0, base=0, channel_multiplier=1)
            hold["ident"] = s_gpc.inc(ins, 1)
            gp.wait_ge(s_gpc.h, hold["ident"])
            ins = gp.tensor_copy(ident_bf[:], ident[:])
            hold["ident_bf"] = s_gpc.inc(ins, 1)
            if prev_done:
                for _s, _val in prev_done:
                    gp.wait_ge(_s.h, _val)
            for name, dst in [("zbre", zbre), ("zbim", zbim), ("zre_t", zre_t),
                              ("zim_t", zim_t), ("lc_t", lc_t), ("scl_t", scl_t),
                              ("b1_t", b1_t), ("qsel", qsel)]:
                ins = gp.dma_start(out=dst[:], in_=io[name])
                s_gpz.inc(ins, 16)
            hold["smalls"] = s_gpz.v

        hold["ag_win"] = emit_allgather(p, block, io["WinTsl_bf"], io["sh_WinT"],
                                        wait_gp=(s_gpw, hold["wincast"]))
        hold["ag_w"] = emit_allgather(p, block, io["W1Tsl_bf"], io["sh_W1T"],
                                      wait_gp=(s_gpl, hold["wcast"]))
        hold["ag_w2"] = emit_allgather(p, block, io["W2Tsl_bf"], io["sh_W2T"])
        hold["ag_wout"] = emit_allgather(p, block, io["WoutTsl_bf"], io["sh_WoutT"],
                                         wait_gp=(s_gpt, hold["woutcast"]))

        @block.vector
        def _(v):
            v.wait_ge(s_gpz.h, hold["smalls"])
            ins = v.tensor_tensor(c_t1[:], zre_t[:], zre_t[:], ALU.mult)
            vedge(v, ins)
            ins = v.tensor_tensor(c_t2[:], zim_t[:], zim_t[:], ALU.mult)
            vedge(v, ins)
            ins = v.tensor_tensor(c_t1[:], c_t1[:], c_t2[:], ALU.add)
            hold["zn2"] = s_v.inc(ins, 1)

        @block.scalar
        def _(s):
            s.wait_ge(s_v.h, hold["zn2"])
            ins = s.activation(zabs[:], c_t1[:], AF.Sqrt)
            sedge(s, ins)
            ins = s.activation(rmagA[:], zabs[:], AF.Tanh)
            hold["rmag"] = s_s.inc(ins, 1)

        @block.vector
        def _(v):
            v.wait_ge(s_s.h, hold["rmag"])
            ins = v.reciprocal(c_t2[:], zabs[:])
            vedge(v, ins)
            ins = v.tensor_tensor(urA[:], zre_t[:], c_t2[:], ALU.mult)
            vedge(v, ins)
            ins = v.tensor_tensor(uiA[:], zim_t[:], c_t2[:], ALU.mult)
            hold["uvec"] = s_v.inc(ins, 1)

        def bcast_row(t_sec, v_sec, src_row, dst, n, scale, idx):
            raise RuntimeError("inline only")

        bc_jobs0 = [(io["bin_r"], bin_b, AD), (io["lnlw_r"], lnlw_b, AD),
                    (io["lnlb_r"], lnlb_b, AD)]
        bc0_mm = [s_mm.v + ji + 1 for ji in range(3)]
        bc0_v = [s_v.v + ji + 1 for ji in range(3)]
        bc0_l = []

        @block.sync
        def _(sync):
            for ji, (src, dst, n) in enumerate(bc_jobs0):
                if ji >= 1:
                    sync.wait_ge(s_mm.h, bc0_mm[ji - 1])
                ins = sync.dma_start(out=vrow[:, 0:n], in_=src)
                bc0_l.append(s_sml.inc(ins, 16))

        @block.tensor
        def _(t):
            t.wait_ge(s_v.h, hold["id1"])
            for ji, (src, dst, n) in enumerate(bc_jobs0):
                t.wait_ge(s_sml.h, bc0_l[ji])
                if ji >= 2:
                    t.wait_ge(s_v.h, bc0_v[ji - 2])
                ins = t.matmul(pb[ji % 2][:], ones1[:, 0:128], vrow[:, 0:n],
                               start=True, stop=True)
                assert s_mm.inc(ins, 1) == bc0_mm[ji]

        @block.vector
        def _(v):
            for ji, (src, dst, n) in enumerate(bc_jobs0):
                v.wait_ge(s_mm.h, bc0_mm[ji])
                ins = v.tensor_copy(dst[:], pb[ji % 2][:])
                assert s_v.inc(ins, 1) == bc0_v[ji]

        es_e = ExitStack()
        xlnT = p.sb(f"{tag}_xlnT", [128, NDT * TOK], BF16, es_e)
        wincb = p.sb(f"{tag}_wincb", [128, NDT * 512], BF16, es_e)
        with ExitStack() as es_ln:
            tr_all, _ = emit_ln(p, block, es_ln, io["x_in"], io["xln_tm"],
                                xlnT, f"{tag}ln", x_ready=prev_done)

        ent_mm_p = [s_mm.v + g + 1 for g in range(NTT)]
        ent_v_p = [s_v.v + g + 1 for g in range(NTT)]
        hold["win_l"] = s_lwa.v + 16

        @block.sync
        def _(sync):
            sync.wait_ge(s_cc.h, hold["ag_win"])
            sync.wait_ge(s_trd.h, tr_all)
            ins = sync.dma_start(
                out=wincb[:], in_=io["sh_WinT"])
            assert s_lwa.inc(ins, 16) == hold["win_l"]

        ps_a = [p.psum(f"{tag}_psa{i}", [128, 512], F32, es=es) for i in range(2)]
        ps_t = [p.psum(f"{tag}_pst{i}", [128, 128], F32, es=es) for i in range(2)]

        @block.tensor
        def _(t):
            t.wait_ge(s_lwa.h, hold["win_l"])
            for g in range(NTT):
                if g >= 2:
                    t.wait_ge(s_v.h, ent_v_p[g - 2])
                for dt in range(NDT):
                    ins = t.matmul(
                        ps_a[g % 2][:],
                        xlnT[:, dt * TOK + g * 128: dt * TOK + (g + 1) * 128],
                        wincb[:, dt * 512:(dt + 1) * 512],
                        start=(dt == 0), stop=(dt == NDT - 1))
                assert s_mm.inc(ins, 1) == ent_mm_p[g]
        ent_done = ent_mm_p[-1]

        @block.vector
        def _(v):
            v.wait_ge(s_v.h, bc0_v[0])
            for g in range(NTT):
                v.wait_ge(s_mm.h, ent_mm_p[g])
                ins = v.tensor_tensor(h[g][:], ps_a[g % 2][:], bin_b[:], ALU.add)
                assert s_v.inc(ins, 1) == ent_v_p[g]
        hold["h_ready"] = list(ent_v_p)
        es_e.close()

        es_L = ExitStack()
        ps_ln = [p.psum(f"{tag}_psln{i}", [128, 128], BF16, es=es_L)
                 for i in range(2)]
        lnst = {"pse": [0, 0], "ytm": [0, 0]}
        bc_stage = p.sb(f"{tag}_bcst", [128, 5 * AD], F32, es_L)
        Tm_r = p.sb(f"{tag}_Tmr", [128, 4], F32, es_L)
        Tm_i = p.sb(f"{tag}_Tmi", [128, 4], F32, es_L)
        Cr = p.sb(f"{tag}_Cr", [128, 4], F32, es_L)
        Ci = p.sb(f"{tag}_Ci", [128, 4], F32, es_L)
        p1r = p.sb(f"{tag}_p1r", [128, 4], F32, es_L)
        p1i = p.sb(f"{tag}_p1i", [128, 4], F32, es_L)
        pcr = p.sb(f"{tag}_pcr", [128, 4], F32, es_L)
        npci = p.sb(f"{tag}_npci", [128, 4], F32, es_L)
        e_sb = p.sb(f"{tag}_esb", [128, 8], F32, es_L)
        e_all = p.sb(f"{tag}_eall", [128, 32], F32, es_L)
        scanAB = p.sb(f"{tag}_scanAB", [128, 8 * TOK], F32, es_L)
        scw = p.sb(f"{tag}_scw", [128, 2 * TOK], F32, es_L)
        cvw = [p.sb(f"{tag}_cvw{i}", [128, TOK], F32, es_L) for i in range(2)]
        tabs = p.sb(f"{tag}_tabs", [128, 3 * TOK], F32, es_L)
        rconst = p.sb(f"{tag}_rconst", [128, TOK], F32, es_L)
        tb1 = p.sb(f"{tag}_tb1", [128, 512], F32, es_L)
        tb2 = p.sb(f"{tag}_tb2", [128, 512], F32, es_L)
        y3T = p.sb(f"{tag}_y3T", [128, NFT * TOK], BF16, es_L)
        wbuf = p.sb(f"{tag}_wbuf", [128, NCT * A_DFF], BF16, es_L)
        w2b = p.sb(f"{tag}_w2b", [128, 16 * AD], BF16, es_L)
        m1w = scw[:, 0:TOK]
        m2w = scw[:, TOK:2 * TOK]
        tabu = scanAB[:, 0:6 * TOK]

        hold["tb_s0"] = s_s.v
        @block.vector
        def _(v):
            v.wait_ge(s_s.h, hold["rmag"])
            v.wait_ge(s_v.h, hold["uvec"])
            v.wait_ge(s_v.h, ent_v_p[-1])
            v.wait_ge(s_trd.h, tr_all)
            for u in range(2):
                base = u * 3 * TOK
                co = tabu[:, base:base + TOK]
                si = tabu[:, base + TOK:base + 2 * TOK]
                rp = tabu[:, base + 2 * TOK:base + 3 * TOK]
                ins = v.tensor_tensor(tscr[:, 0:1], zbre[:, u:u + 1],
                                      zbre[:, u:u + 1], ALU.mult)
                vedge(v, ins)
                ins = v.tensor_tensor(tscr[:, 1:2], zbim[:, u:u + 1],
                                      zbim[:, u:u + 1], ALU.mult)
                vedge(v, ins)
                ins = v.tensor_tensor(tscr[:, 0:1], tscr[:, 0:1], tscr[:, 1:2],
                                      ALU.add)
                hold[f"tbz_{u}"] = s_v.inc(ins, 1)
                v.wait_ge(s_s.h, hold["tb_s0"] + 2 * (u + 1))
                ins = v.reciprocal(tscr[:, 4:5], tscr[:, 2:3])
                vedge(v, ins)
                ins = v.tensor_tensor(tscr[:, 5:6], zbre[:, u:u + 1],
                                      tscr[:, 4:5], ALU.mult)
                vedge(v, ins)
                ins = v.tensor_tensor(tscr[:, 6:7], zbim[:, u:u + 1],
                                      tscr[:, 4:5], ALU.mult)
                vedge(v, ins)
                ins = v.memset(co[:, 0:1], 1.0)
                vedge(v, ins)
                ins = v.memset(si[:, 0:1], 0.0)
                vedge(v, ins)
                ins = v.memset(rp[:, 0:1], 1.0)
                vedge(v, ins)
                ins = v.tensor_copy(co[:, 1:2], tscr[:, 5:6])
                vedge(v, ins)
                ins = v.tensor_copy(si[:, 1:2], tscr[:, 6:7])
                vedge(v, ins)
                ins = v.tensor_copy(rp[:, 1:2], tscr[:, 3:4])
                vedge(v, ins)
                w = 2
                while w < TOK:
                    ins = v.tensor_tensor(tscr[:, 0:1], co[:, w - 1:w],
                                          co[:, 1:2], ALU.mult)
                    vedge(v, ins)
                    ins = v.tensor_tensor(tscr[:, 1:2], si[:, w - 1:w],
                                          si[:, 1:2], ALU.mult)
                    vedge(v, ins)
                    ins = v.tensor_tensor(co[:, w:w + 1], tscr[:, 0:1],
                                          tscr[:, 1:2], ALU.subtract)
                    vedge(v, ins)
                    ins = v.tensor_tensor(tscr[:, 0:1], si[:, w - 1:w],
                                          co[:, 1:2], ALU.mult)
                    vedge(v, ins)
                    ins = v.tensor_tensor(tscr[:, 1:2], co[:, w - 1:w],
                                          si[:, 1:2], ALU.mult)
                    vedge(v, ins)
                    ins = v.tensor_tensor(si[:, w:w + 1], tscr[:, 0:1],
                                          tscr[:, 1:2], ALU.add)
                    vedge(v, ins)
                    ins = v.tensor_tensor(rp[:, w:w + 1], rp[:, w - 1:w],
                                          rp[:, 1:2], ALU.mult)
                    vedge(v, ins)
                    nb = w - 1
                    if nb > 0:
                        ins = v.tensor_scalar(tb1[:, 0:nb], co[:, 1:w],
                                              co[:, w:w + 1], None, ALU.mult)
                        vedge(v, ins)
                        ins = v.tensor_scalar(tb2[:, 0:nb], si[:, 1:w],
                                              si[:, w:w + 1], None, ALU.mult)
                        vedge(v, ins)
                        ins = v.tensor_tensor(co[:, w + 1:2 * w], tb1[:, 0:nb],
                                              tb2[:, 0:nb], ALU.subtract)
                        vedge(v, ins)
                        ins = v.tensor_scalar(tb1[:, 0:nb], si[:, 1:w],
                                              co[:, w:w + 1], None, ALU.mult)
                        vedge(v, ins)
                        ins = v.tensor_scalar(tb2[:, 0:nb], co[:, 1:w],
                                              si[:, w:w + 1], None, ALU.mult)
                        vedge(v, ins)
                        ins = v.tensor_tensor(si[:, w + 1:2 * w], tb1[:, 0:nb],
                                              tb2[:, 0:nb], ALU.add)
                        vedge(v, ins)
                        ins = v.tensor_scalar(rp[:, w + 1:2 * w], rp[:, 1:w],
                                              rp[:, w:w + 1], None, ALU.mult)
                        vedge(v, ins)
                    w *= 2
            ins = v.tensor_copy(tscr[:, 7:8], tscr[:, 6:7])
            hold["tab_done"] = s_v.inc(ins, 1)

        @block.scalar
        def _(s):
            assert s_s.v == hold["tb_s0"]
            for u in range(2):
                s.wait_ge(s_v.h, hold[f"tbz_{u}"])
                ins = s.activation(tscr[:, 2:3], tscr[:, 0:1], AF.Sqrt)
                sedge(s, ins)
                ins = s.activation(tscr[:, 3:4], tscr[:, 2:3], AF.Tanh)
                s_s.inc(ins, 1)

        tabw_p = s_el.v + 16

        @block.sync
        def _(sync):
            sync.wait_ge(s_v.h, hold["tab_done"])
            ins = sync.dma_start(
                out=io["tabloc"].rearrange("(p ut) j -> p ut j", ut=6),
                in_=tabu[:].rearrange("p (ut j) -> p ut j", j=TOK))
            assert s_el.inc(ins, 16) == tabw_p

        @block.gpsimd
        def _(gp):
            gp.wait_ge(s_el.h, tabw_p)
        hold["ag_tab"] = emit_allgather(p, block, io["tabloc"], io["sh_tab"])
        def inner_ln(wb, bb, do_silu, h_ready):
            v0, s0 = s_v.v, s_s.v
            stats_p = [v0 + 3 * tt + 3 for tt in range(NTT)]
            sqrt_p = s0 + 1
            rs_p = v0 + 3 * NTT + 3
            norm_p = [s0 + 1 + 2 * tt + 1 for tt in range(NTT)]
            act_p = [s0 + 1 + 2 * tt + 2 for tt in range(NTT)]
            aff_p = [rs_p + 2 * tt + 4 * max(tt - 1, 0) + 2 for tt in range(NTT)]
            mm0t = s_mm.v
            tr_p = {(tt, ct): mm0t + NCT * tt + ct + 1
                    for tt in range(NTT) for ct in range(NCT)}
            ev_p = {}
            for e in range(NTT):
                base = aff_p[e + 1] if e < NTT - 1 else aff_p[NTT - 1] + 4
                for ct in range(NCT):
                    ev_p[(e, ct)] = base + ct + 1
            ev_flat = [ev_p[(i // NCT, i % NCT)] for i in range(NTT * NCT)]

            @block.vector
            def _(v):
                for tt in range(NTT):
                    v.wait_ge(s_v.h, h_ready[tt])
                    ins = v.bn_stats(stat6[:], h[tt][:])
                    vedge(v, ins)
                    ins = v.bn_aggr(agg2[:, 2 * tt:2 * tt + 2], stat6[:])
                    vedge(v, ins)
                    ins = v.tensor_scalar(vep[:, tt:tt + 1],
                                          agg2[:, 2 * tt + 1:2 * tt + 2],
                                          EPS, None, ALU.add)
                    assert s_v.inc(ins, 1) == stats_p[tt]
                v.wait_ge(s_s.h, sqrt_p)
                ins = v.reciprocal(rsv[:], sdv[:])
                vedge(v, ins)
                means = agg2[:].rearrange("p (t two) -> p two t", two=2)[:, 0, :]
                ins = v.tensor_tensor(nbv[:], means, rsv[:], ALU.mult)
                vedge(v, ins)
                ins = v.tensor_scalar(nbv[:], nbv[:], -1.0, None, ALU.mult)
                assert s_v.inc(ins, 1) == rs_p
                def evac_group(v, e):
                    for ct in range(NCT):
                        v.wait_ge(s_mm.h, tr_p[(e, ct)])
                        ins = v.tensor_copy(yT[ct][:, e * 128:(e + 1) * 128],
                                            ps_ln[(NCT * e + ct) % 2][:])
                        assert s_v.inc(ins, 1) == ev_p[(e, ct)]

                for tt in range(NTT):
                    v.wait_ge(s_s.h, norm_p[tt])
                    ins = v.tensor_tensor(yscr[tt % 2][:], yscr[tt % 2][:],
                                          wb, ALU.mult)
                    vedge(v, ins)
                    ins = v.tensor_tensor(yscr[tt % 2][:], yscr[tt % 2][:],
                                          bb, ALU.add)
                    assert s_v.inc(ins, 1) == aff_p[tt]
                    if tt >= 1:
                        evac_group(v, tt - 1)
                evac_group(v, NTT - 1)

            @block.scalar
            def _(s):
                s.wait_ge(s_v.h, stats_p[-1])
                ins = s.activation(sdv[:], vep[:], AF.Sqrt)
                assert s_s.inc(ins, 1) == sqrt_p
                for tt in range(NTT):
                    if tt == 0:
                        s.wait_ge(s_v.h, rs_p)
                        s.wait_ge(s_v.h, h_ready[-1])
                    if tt >= 2:
                        s.wait_ge(s_s.h, act_p[tt - 2])
                        s.wait_ge(s_mm.h, tr_p[(tt - 2, NCT - 1)])
                    else:
                        s.wait_ge(s_mm.h, lnst["ytm"][tt % 2])
                    ins = s.activation(yscr[tt % 2][:], h[tt][:], AF.Identity,
                                       bias=nbv[:, tt:tt + 1],
                                       scale=rsv[:, tt:tt + 1])
                    assert s_s.inc(ins, 1) == norm_p[tt]
                    s.wait_ge(s_v.h, aff_p[tt])
                    ins = s.activation(y_tm[tt % 2][:], yscr[tt % 2][:],
                                       AF.Silu if do_silu else AF.Identity)
                    assert s_s.inc(ins, 1) == act_p[tt]

            @block.tensor
            def _(t):
                t.wait_ge(s_gpc.h, hold["ident_bf"])
                for tt in range(NTT):
                    t.wait_ge(s_s.h, act_p[tt])
                    for ct in range(NCT):
                        idx = NCT * tt + ct
                        if idx >= 2:
                            t.wait_ge(s_v.h, ev_flat[idx - 2])
                        else:
                            t.wait_ge(s_v.h, lnst["pse"][idx % 2])
                        ins = t.transpose(
                            ps_ln[idx % 2][:],
                            y_tm[tt % 2][:, ct * 128:(ct + 1) * 128], ident_bf[:])
                        assert s_mm.inc(ins, 1) == tr_p[(tt, ct)]

            lnst["pse"] = [ev_p[(NTT - 1, NCT - 2)], ev_p[(NTT - 1, NCT - 1)]]
            lnst["ytm"] = [tr_p[(NTT - 2, NCT - 1)], tr_p[(NTT - 1, NCT - 1)]]
            return ev_p[(NTT - 1, NCT - 1)]

        def bcast5(l, rows_done_guard):
            srcs = [io["ln1w"], io["ln1b"], io["ln2w"], io["ln2b"], io["b2r"]]
            mm_c = [s_mm.v + ji + 1 for ji in range(5)]
            v_c = [s_v.v + ji + 1 for ji in range(5)]
            l_c = []

            @block.sync
            def _(sync):
                for ji in range(5):
                    if ji >= 1:
                        sync.wait_ge(s_mm.h, mm_c[ji - 1])
                    else:
                        sync.wait_ge(s_mm.h, vrow_guard[0])
                        if rows_done_guard is not None:
                            sync.wait_ge(s_v.h, rows_done_guard)
                    ins = sync.dma_start(out=vrow[:, 0:AD],
                                         in_=srcs[ji][l:l + 1, :])
                    l_c.append(s_sml.inc(ins, 16))

            @block.tensor
            def _(t):
                for ji in range(5):
                    t.wait_ge(s_sml.h, l_c[ji])
                    if ji >= 2:
                        t.wait_ge(s_v.h, v_c[ji - 2])
                    elif rows_done_guard is not None:
                        t.wait_ge(s_v.h, rows_done_guard)
                    ins = t.matmul(pb[ji % 2][:], ones1[:, 0:128],
                                   vrow[:, 0:AD], start=True, stop=True)
                    assert s_mm.inc(ins, 1) == mm_c[ji]

            @block.vector
            def _(v):
                for ji in range(5):
                    v.wait_ge(s_mm.h, mm_c[ji])
                    ins = v.tensor_copy(bc_stage[:, ji * AD:(ji + 1) * AD],
                                        pb[ji % 2][:])
                    assert s_v.inc(ins, 1) == v_c[ji]

            vrow_guard[0] = mm_c[-1]
            return v_c[-1]

        prev_ffn_done = ent_done
        prev_f1_done = ent_done
        prev_bc_guard = ent_v_p[-1]
        hup2_guard = [ent_v_p[-1]]
        y3T_guard = [ent_done]
        vrow_guard = [max(bc0_mm[-1], ent_done)]
        for l in range(4):
            bc_done = bcast5(l, prev_bc_guard)
            l1w_b = bc_stage[:, 0 * AD:1 * AD]
            l1b_b = bc_stage[:, 1 * AD:2 * AD]
            l2w_b = bc_stage[:, 2 * AD:3 * AD]
            l2b_b = bc_stage[:, 3 * AD:4 * AD]
            b2_b = bc_stage[:, 4 * AD:5 * AD]

            @block.vector
            def _(v):
                v.wait_ge(s_v.h, bc_done)

            ytr_p = inner_ln(l1w_b, l1b_b, True, hold["h_ready"])

            tab0 = s_tab.v
            tabl_p = [tab0 + 16 * (ct + 1) for ct in range(NCT)]
            v0 = s_v.v
            scan_p = {}
            for ct in range(NCT):
                for k in range(11):
                    scan_p[(ct, k)] = v0 + 11 * ct + k + 1
            escan_done = v0 + 44

            @block.sync
            def _(sync):
                for ct in range(NCT):
                    u = l * 4 + ct
                    if ct == 0:
                        sync.wait_ge(s_cc.h, hold["ag_tab"])
                        if l > 0:
                            sync.wait_ge(s_v.h, hold[f"fix_done_{l-1}"])
                        else:
                            sync.wait_ge(s_v.h, hold["tab_done"])
                    else:
                        sync.wait_ge(s_v.h, scan_p[(ct - 1, 10)])
                    q_u, h_u = divmod(u, 2)
                    src = io["sh_tab"][q_u * 768:(q_u + 1) * 768, :]
                    ins = sync.dma_start(
                        out=tabs[:].rearrange("p (t j) -> p t j", j=TOK),
                        in_=src.rearrange("(p t) j -> p t j", t=6)[
                            :, 3 * h_u:3 * h_u + 3, :])
                    assert s_tab.inc(ins, 16) == tabl_p[ct]

            @block.vector
            def _(v):
                for ct in range(NCT):
                    u = l * 4 + ct
                    co = tabs[:, 0:TOK]
                    si = tabs[:, TOK:2 * TOK]
                    A = scanAB[:, (2 * ct) * TOK:(2 * ct + 1) * TOK]
                    B = scanAB[:, (2 * ct + 1) * TOK:(2 * ct + 2) * TOK]
                    v.wait_ge(s_tab.h, tabl_p[ct])
                    v.wait_ge(s_v.h, ytr_p)
                    ins = v.tensor_scalar(rconst[:], onesw[:],
                                          rmagA[:, u:u + 1], None, ALU.mult)
                    assert s_v.inc(ins, 1) == scan_p[(ct, 0)]
                    v.wait_ge(s_v.h, scan_p[(ct, 0)])
                    ins = v.tensor_tensor(scw[:, 0:TOK], yT[ct][:], co, ALU.mult)
                    assert s_v.inc(ins, 1) == scan_p[(ct, 1)]
                    ins = v.tensor_tensor(scw[:, TOK:2 * TOK], yT[ct][:], si,
                                          ALU.mult)
                    v.wait_ge(s_v.h, s_v.inc(ins, 1))
                    ins = v.tensor_tensor_scan(A, rconst[:], scw[:, 0:TOK],
                                               0.0, ALU.mult, ALU.add)
                    assert s_v.inc(ins, 1) == scan_p[(ct, 3)]
                    ins = v.tensor_tensor_scan(B, rconst[:], scw[:, TOK:2 * TOK],
                                               0.0, ALU.mult, ALU.add)
                    v.wait_ge(s_v.h, s_v.inc(ins, 1))
                    ins = v.tensor_tensor(tscr[:, 0:1], co[:, TOK - 1:TOK],
                                          A[:, TOK - 1:TOK], ALU.mult)
                    vedge(v, ins)
                    ins = v.tensor_tensor(tscr[:, 1:2], si[:, TOK - 1:TOK],
                                          B[:, TOK - 1:TOK], ALU.mult)
                    vedge(v, ins)
                    ins = v.tensor_tensor(e_sb[:, 2 * ct:2 * ct + 1],
                                          tscr[:, 0:1], tscr[:, 1:2], ALU.add)
                    vedge(v, ins)
                    ins = v.tensor_tensor(tscr[:, 0:1], si[:, TOK - 1:TOK],
                                          A[:, TOK - 1:TOK], ALU.mult)
                    vedge(v, ins)
                    ins = v.tensor_tensor(tscr[:, 1:2], co[:, TOK - 1:TOK],
                                          B[:, TOK - 1:TOK], ALU.mult)
                    vedge(v, ins)
                    ins = v.tensor_tensor(e_sb[:, 2 * ct + 1:2 * ct + 2],
                                          tscr[:, 0:1], tscr[:, 1:2],
                                          ALU.subtract)
                    assert s_v.inc(ins, 1) == scan_p[(ct, 10)]
            assert s_v.v == escan_done

            el_w = s_el.v + 16

            @block.sync
            def _(sync):
                sync.wait_ge(s_v.h, escan_done)
                ins = sync.dma_start(out=io[f"eloc{l}"], in_=e_sb[:])
                assert s_el.inc(ins, 16) == el_w

            @block.gpsimd
            def _(gp):
                gp.wait_ge(s_el.h, el_w)
                ins = gp.collective_compute(
                    "AllGather", ALU.bypass,
                    replica_groups=[[0, 1, 2, 3], [4, 5, 6, 7]],
                    ins=[io[f"eloc{l}"]], outs=[io[f"egath{l}"]])
                s_cc.inc(ins, 1)
            eg_done = s_cc.v
            el_r = s_el.v + 16

            @block.sync
            def _(sync):
                sync.wait_ge(s_cc.h, eg_done)
                ins = sync.dma_start(
                    out=e_all[:],
                    in_=io[f"egath{l}"].rearrange("(m p) c -> p m c", p=128))
                assert s_el.inc(ins, 16) == el_r

            @block.vector
            def _(v):
                v.wait_ge(s_el.h, el_r)
                ins = v.tensor_tensor(p1r[:], urA[:, 4 * l:4 * l + 4],
                                      rmagA[:, 4 * l:4 * l + 4], ALU.mult)
                vedge(v, ins)
                ins = v.tensor_tensor(p1i[:], uiA[:, 4 * l:4 * l + 4],
                                      rmagA[:, 4 * l:4 * l + 4], ALU.mult)
                vedge(v, ins)
                for _ in range(10):
                    ins = v.tensor_tensor(Tm_r[:], p1r[:], p1r[:], ALU.mult)
                    vedge(v, ins)
                    ins = v.tensor_tensor(Tm_i[:], p1i[:], p1i[:], ALU.mult)
                    vedge(v, ins)
                    ins = v.tensor_tensor(Cr[:], p1r[:], p1i[:], ALU.mult)
                    vedge(v, ins)
                    ins = v.tensor_tensor(p1r[:], Tm_r[:], Tm_i[:], ALU.subtract)
                    vedge(v, ins)
                    ins = v.tensor_tensor(p1i[:], Cr[:], Cr[:], ALU.add)
                    vedge(v, ins)
                lcr = lc_t[:].rearrange("p (l ct two) -> p l ct two", l=4, two=2)
                ins = v.tensor_copy(Tm_r[:], lcr[:, l, :, 0])
                vedge(v, ins)
                ins = v.tensor_copy(Tm_i[:], lcr[:, l, :, 1])
                vedge(v, ins)
                ins = v.memset(Cr[:], 0.0)
                vedge(v, ins)
                ins = v.memset(Ci[:], 0.0)
                vedge(v, ins)
                eview = e_all[:].rearrange("p (m ct two) -> p m ct two",
                                           m=4, two=2)
                for m in range(4):
                    ins = v.tensor_scalar(tscr[:, 0:4], Tm_r[:],
                                          qsel[:, m:m + 1], None, ALU.mult)
                    vedge(v, ins)
                    ins = v.tensor_tensor(Cr[:], Cr[:], tscr[:, 0:4], ALU.add)
                    vedge(v, ins)
                    ins = v.tensor_scalar(tscr[:, 0:4], Tm_i[:],
                                          qsel[:, m:m + 1], None, ALU.mult)
                    vedge(v, ins)
                    ins = v.tensor_tensor(Ci[:], Ci[:], tscr[:, 0:4], ALU.add)
                    vedge(v, ins)
                    if m < 3:
                        ins = v.tensor_tensor(tscr[:, 0:4], p1r[:], Tm_r[:],
                                              ALU.mult)
                        vedge(v, ins)
                        ins = v.tensor_tensor(tscr[:, 4:8], p1i[:], Tm_i[:],
                                              ALU.mult)
                        vedge(v, ins)
                        ins = v.tensor_tensor(tscr[:, 0:4], tscr[:, 0:4],
                                              tscr[:, 4:8], ALU.subtract)
                        vedge(v, ins)
                        ins = v.tensor_tensor(tscr[:, 4:8], p1r[:], Tm_i[:],
                                              ALU.mult)
                        vedge(v, ins)
                        ins = v.tensor_tensor(Tm_i[:], p1i[:], Tm_r[:], ALU.mult)
                        vedge(v, ins)
                        ins = v.tensor_tensor(Tm_i[:], Tm_i[:], tscr[:, 4:8],
                                              ALU.add)
                        vedge(v, ins)
                        ins = v.tensor_tensor(Tm_r[:], tscr[:, 0:4],
                                              eview[:, m, :, 0], ALU.add)
                        vedge(v, ins)
                        ins = v.tensor_tensor(Tm_i[:], Tm_i[:],
                                              eview[:, m, :, 1], ALU.add)
                        vedge(v, ins)
                ins = v.tensor_tensor(tscr[:, 0:4], urA[:, 4 * l:4 * l + 4],
                                      rmagA[:, 4 * l:4 * l + 4], ALU.mult)
                vedge(v, ins)
                ins = v.tensor_tensor(tscr[:, 4:8], uiA[:, 4 * l:4 * l + 4],
                                      rmagA[:, 4 * l:4 * l + 4], ALU.mult)
                vedge(v, ins)
                ins = v.tensor_tensor(pcr[:], tscr[:, 0:4], Cr[:], ALU.mult)
                vedge(v, ins)
                ins = v.tensor_tensor(npci[:], tscr[:, 4:8], Ci[:], ALU.mult)
                vedge(v, ins)
                ins = v.tensor_tensor(pcr[:], pcr[:], npci[:], ALU.subtract)
                vedge(v, ins)
                ins = v.tensor_tensor(npci[:], tscr[:, 0:4], Ci[:], ALU.mult)
                vedge(v, ins)
                ins = v.tensor_tensor(Tm_r[:], tscr[:, 4:8], Cr[:], ALU.mult)
                vedge(v, ins)
                ins = v.tensor_tensor(npci[:], npci[:], Tm_r[:], ALU.add)
                vedge(v, ins)
                ins = v.tensor_scalar(npci[:], npci[:], -1.0, None, ALU.mult)
                hold[f"carry_{l}"] = s_v.inc(ins, 1)

            tabr_p = [s_tab.v + 16 * (ct + 1) for ct in range(NCT)]
            v0f = s_v.v
            mm0c = s_mm.v
            trc_p = {}
            for ct in range(NCT):
                for tt in range(NTT):
                    trc_p[(ct, tt)] = mm0c + NTT * ct + tt + 1
            vorder = []
            for ct in range(NCT):
                vorder.append(("fix", ct))
                if ct >= 1:
                    vorder.append(("upd", ct - 1))
            vorder.append(("upd", NCT - 1))
            vbase = v0f
            fix_p = {}
            hupd_p = {}
            for kind, ct in vorder:
                if kind == "fix":
                    for k in range(6):
                        fix_p[(ct, k)] = vbase + k + 1
                    vbase += 6
                else:
                    for tt in range(NTT):
                        hupd_p[(ct, tt)] = vbase + tt + 1
                    vbase += NTT
            hold[f"fix_done_{l}"] = fix_p[(NCT - 1, 5)]

            @block.sync
            def _(sync):
                for ct in range(NCT):
                    u = l * 4 + ct
                    sync.wait_ge(s_v.h, hold[f"carry_{l}"])
                    if ct >= 1:
                        sync.wait_ge(s_v.h, fix_p[(ct - 1, 3)])
                    q_u, h_u = divmod(u, 2)
                    src = io["sh_tab"][q_u * 768:(q_u + 1) * 768, :]
                    ins = sync.dma_start(
                        out=tabs[:].rearrange("p (t j) -> p t j", j=TOK),
                        in_=src.rearrange("(p t) j -> p t j", t=6)[
                            :, 3 * h_u:3 * h_u + 3, :])
                    assert s_tab.inc(ins, 16) == tabr_p[ct]

            @block.vector
            def _(v):
                for kind, ct in vorder:
                    if kind == "fix":
                        co = tabs[:, 0:TOK]
                        si = tabs[:, TOK:2 * TOK]
                        rp = tabs[:, 2 * TOK:3 * TOK]
                        A = scanAB[:, (2 * ct) * TOK:(2 * ct + 1) * TOK]
                        B = scanAB[:, (2 * ct + 1) * TOK:(2 * ct + 2) * TOK]
                        u = l * 4 + ct
                        v.wait_ge(s_tab.h, tabr_p[ct])
                        ins = v.scalar_tensor_tensor(A, rp, pcr[:, ct:ct + 1],
                                                     A, ALU.mult, ALU.add)
                        assert s_v.inc(ins, 1) == fix_p[(ct, 0)]
                        v.wait_ge(s_v.h, fix_p[(ct, 0)])
                        ins = v.scalar_tensor_tensor(B, rp, npci[:, ct:ct + 1],
                                                     B, ALU.mult, ALU.add)
                        v.wait_ge(s_v.h, s_v.inc(ins, 1))
                        ins = v.tensor_tensor(m1w, co, A, ALU.mult)
                        assert s_v.inc(ins, 1) == fix_p[(ct, 2)]
                        ins = v.tensor_tensor(m2w, si, B, ALU.mult)
                        v.wait_ge(s_v.h, s_v.inc(ins, 1))
                        if ct >= 2:
                            v.wait_ge(s_mm.h, trc_p[(ct - 2, NTT - 1)])
                        ins = v.tensor_tensor(cvw[ct % 2][:], m1w, m2w, ALU.add)
                        vedge(v, ins)
                        ins = v.tensor_scalar(cvw[ct % 2][:], cvw[ct % 2][:],
                                              scl_t[:, u:u + 1], None, ALU.mult)
                        assert s_v.inc(ins, 1) == fix_p[(ct, 5)]
                    else:
                        for tt in range(NTT):
                            v.wait_ge(s_mm.h, trc_p[(ct, tt)])
                            ins = v.tensor_tensor(
                                h[tt][:, ct * 128:(ct + 1) * 128],
                                ps_t[(NTT * ct + tt) % 2][:],
                                h[tt][:, ct * 128:(ct + 1) * 128], ALU.add)
                            assert s_v.inc(ins, 1) == hupd_p[(ct, tt)]

            @block.tensor
            def _(t):
                t.wait_ge(s_gpc.h, hold["ident"])
                for ct in range(NCT):
                    t.wait_ge(s_v.h, fix_p[(ct, 5)])
                    for tt in range(NTT):
                        idx = NTT * ct + tt
                        if idx >= 2:
                            pct, ptt = divmod(idx - 2, NTT)
                            t.wait_ge(s_v.h, hupd_p[(pct, ptt)])
                        ins = t.transpose(ps_t[idx % 2][:],
                                          cvw[ct % 2][:, tt * 128:(tt + 1) * 128],
                                          ident[:])
                        assert s_mm.inc(ins, 1) == trc_p[(ct, tt)]

            h_ready2 = [hupd_p[(NCT - 1, tt)] for tt in range(NTT)]
            hold["h_ready"] = h_ready2
            ytr2_p = inner_ln(l2w_b, l2b_b, False, h_ready2)

            w1l_p = s_lwa.v + 16
            w2l_p = s_lw2.v + 16
            mm0f = s_mm.v
            f1_p = {}
            for ft in range(NFT):
                for th in range(2):
                    f1_p[(ft, th)] = mm0f + 2 * ft + th + 1
            s0f = s_s.v
            sil_p = {}
            for ft in range(NFT):
                for th in range(2):
                    sil_p[(ft, th)] = s0f + 2 * ft + th + 1
            mm0g = mm0f + 32
            f2_p = [mm0g + tt + 1 for tt in range(NTT)]
            v0g = s_v.v
            hup2_p = [v0g + 2 * tt + 2 for tt in range(NTT)]

            @block.sync
            def _(sync):
                sync.wait_ge(s_cc.h, hold["ag_w"])
                sync.wait_ge(s_mm.h, prev_f1_done)
                ins = sync.dma_start(
                    out=wbuf[:], in_=io["sh_W1T"][l * 128:(l + 1) * 128, :])
                assert s_lwa.inc(ins, 16) == w1l_p
                sync.wait_ge(s_cc.h, hold["ag_w2"])
                sync.wait_ge(s_mm.h, prev_ffn_done)
                ins = sync.dma_start(
                    out=w2b[:], in_=io["sh_W2T"][l * 128:(l + 1) * 128, :])
                assert s_lw2.inc(ins, 16) == w2l_p

            @block.tensor
            def _(t):
                t.wait_ge(s_lwa.h, w1l_p)
                t.wait_ge(s_v.h, ytr2_p)
                for ft in range(NFT):
                    for th in range(2):
                        k = 2 * ft + th
                        if k >= 2:
                            t.wait_ge(s_s.h, sil_p[((k - 2) // 2, (k - 2) % 2)])
                        else:
                            t.wait_ge(s_v.h, hup2_guard[0])
                        for dt in range(NCT):
                            ins = t.matmul(
                                ps_a[k % 2][:],
                                wbuf[:, dt * A_DFF + ft * 128:
                                     dt * A_DFF + (ft + 1) * 128],
                                yT[dt][:, th * 512:(th + 1) * 512],
                                start=(dt == 0), stop=(dt == NCT - 1))
                        assert s_mm.inc(ins, 1) == f1_p[(ft, th)]
                t.wait_ge(s_lw2.h, w2l_p)
                t.wait_ge(s_s.h, sil_p[(NFT - 1, 1)])
                for tt in range(NTT):
                    if tt >= 2:
                        t.wait_ge(s_v.h, hup2_p[tt - 2])
                    for ft in range(NFT):
                        ins = t.matmul(
                            ps_a[tt % 2][:],
                            y3T[:, ft * TOK + tt * 128: ft * TOK + (tt + 1) * 128],
                            w2b[:, ft * AD:(ft + 1) * AD],
                            start=(ft == 0), stop=(ft == NFT - 1))
                    assert s_mm.inc(ins, 1) == f2_p[tt]
            prev_ffn_done = f2_p[-1]
            prev_f1_done = f1_p[(NFT - 1, 1)]

            @block.scalar
            def _(s):
                for ft in range(NFT):
                    for th in range(2):
                        s.wait_ge(s_mm.h, f1_p[(ft, th)])
                        if ft == 0 and th == 0:
                            s.wait_ge(s_mm.h, y3T_guard[0])
                        ins = s.activation(
                            y3T[:, ft * TOK + th * 512: ft * TOK + (th + 1) * 512],
                            ps_a[(2 * ft + th) % 2][:], AF.Silu,
                            bias=b1_t[:, l * 16 + ft:l * 16 + ft + 1], scale=1.0)
                        assert s_s.inc(ins, 1) == sil_p[(ft, th)]

            @block.vector
            def _(v):
                for tt in range(NTT):
                    v.wait_ge(s_mm.h, f2_p[tt])
                    ins = v.tensor_tensor(yscr[tt % 2][:], ps_a[tt % 2][:],
                                          b2_b, ALU.add)
                    vedge(v, ins)
                    ins = v.tensor_tensor(h[tt][:], h[tt][:], yscr[tt % 2][:],
                                          ALU.add)
                    assert s_v.inc(ins, 1) == hup2_p[tt]
            hold["h_ready"] = list(hup2_p)
            prev_bc_guard = hup2_p[-1]
            hup2_guard = [hup2_p[-1]]
            y3T_guard = [prev_ffn_done]

        ytrf_p = inner_ln(lnlw_b[:], lnlb_b[:], False, hold["h_ready"])
        v0x = s_v.v
        hft_p = [v0x + ct + 1 for ct in range(NCT)]

        @block.vector
        def _(v):
            v.wait_ge(s_v.h, ytrf_p)
            for ct in range(NCT):
                ins = v.tensor_copy(hfT[:, ct * TOK:(ct + 1) * TOK], yT[ct][:])
                assert s_v.inc(ins, 1) == hft_p[ct]

        es_L.close()

        bout_b = p.sb(f"{tag}_boutb", [128, D], F32, es)
        NBC = D // 512
        bo_mm = [s_mm.v + c + 1 for c in range(NBC)]
        bo_v = [s_v.v + c + 1 for c in range(NBC)]
        bo_l = [s_sml.v + 16 * (c + 1) for c in range(NBC)]

        @block.sync
        def _(sync):
            for c in range(NBC):
                if c == 0:
                    sync.wait_ge(s_v.h, hft_p[-1])
                    sync.wait_ge(s_mm.h, vrow_guard[0])
                else:
                    sync.wait_ge(s_mm.h, bo_mm[c - 1])
                ins = sync.dma_start(out=vrow[:, 0:512],
                                     in_=io["bout_r"][0:1, c * 512:(c + 1) * 512])
                assert s_sml.inc(ins, 16) == bo_l[c]

        @block.tensor
        def _(t):
            t.wait_ge(s_v.h, hft_p[-1])
            for c in range(NBC):
                t.wait_ge(s_sml.h, bo_l[c])
                if c >= 2:
                    t.wait_ge(s_v.h, bo_v[c - 2])
                ins = t.matmul(pb[c % 2][:], ones1[:, 0:128], vrow[:, 0:512],
                               start=True, stop=True)
                assert s_mm.inc(ins, 1) == bo_mm[c]

        @block.vector
        def _(v):
            for c in range(NBC):
                v.wait_ge(s_mm.h, bo_mm[c])
                ins = v.tensor_copy(bout_b[:, c * 512:(c + 1) * 512],
                                    pb[c % 2][:])
                assert s_v.inc(ins, 1) == bo_v[c]

        def wout_colblock(sync, ob, dest, sem):
            if ob == 0:
                sync.wait_ge(s_cc.h, hold["ag_wout"])
                sync.wait_ge(s_v.h, hft_p[-1])
            srcap = io["sh_WoutT"].rearrange("(dt pp) o -> pp dt o", pp=128)[:, :, ob * 512:(ob + 1) * 512]
            ins = sync.dma_start(out=dest[:].rearrange("p (dt j) -> p dt j", j=512),
                                 in_=srcap)
            sem.inc(ins, 16)

        st_last = emit_mm_main(
            p, block, es, hfT, s_trd.v, wout_colblock, D,
            tag=f"{tag}o", nk=NCT,
            epi=dict(scale=1.0, bias_b=bout_b, act=None,
                     xold=io["x_in"], xout=io["x_out"],
                     bias_ready=bo_v[-1]))

        @block.sync
        def _(sync):
            for sem, val in st_last:
                sync.wait_ge(sem.h, val)

    return st_last


def build_step2_nc(cores=8):
    nc = bass.Bass("TRN2", target_bir_lowering=False)
    io = {}
    AD = A_DIM
    io["x_in"] = nc.dram_tensor("x_in", [TOK, D], F32, kind="ExternalInput").ap()
    io["x_out"] = nc.dram_tensor("x_out", [TOK, D], F32, kind="ExternalOutput").ap()
    io["WinTsl"] = nc.dram_tensor("WinTsl", [128 // cores, NDT * AD], F32,
                                  kind="ExternalInput").ap()
    io["W1Tsl"] = nc.dram_tensor("W1Tsl", [512 // cores, 4 * A_DFF], F32,
                                 kind="ExternalInput").ap()
    io["W2Tsl"] = nc.dram_tensor("W2Tsl", [512 // cores, 16 * AD], F32,
                                 kind="ExternalInput").ap()
    io["WoutTsl"] = nc.dram_tensor("WoutTsl", [AD // cores, D], F32,
                                   kind="ExternalInput").ap()
    for name, shp in [("zbre", [128, 2]), ("zbim", [128, 2]),
                      ("zre_t", [128, 16]), ("zim_t", [128, 16]),
                      ("lc_t", [128, 32]), ("scl_t", [128, 16]),
                      ("b1_t", [128, 64]), ("qsel", [128, 4])]:
        io[name] = nc.dram_tensor(name, shp, F32, kind="ExternalInput").ap()
    for name in ["ln1w", "ln1b", "ln2w", "ln2b", "b2r"]:
        io[name] = nc.dram_tensor(name, [4, AD], F32, kind="ExternalInput").ap()
    io["bin_r"] = nc.dram_tensor("bin_r", [1, AD], F32, kind="ExternalInput").ap()
    io["lnlw_r"] = nc.dram_tensor("lnlw_r", [1, AD], F32, kind="ExternalInput").ap()
    io["lnlb_r"] = nc.dram_tensor("lnlb_r", [1, AD], F32, kind="ExternalInput").ap()
    io["bout_r"] = nc.dram_tensor("bout_r", [1, D], F32, kind="ExternalInput").ap()

    io["WinTsl_bf"] = nc.dram_tensor("WinTsl_bf", [128 // cores, NDT * AD], BF16).ap()
    io["W1Tsl_bf"] = nc.dram_tensor("W1Tsl_bf", [512 // cores, 4 * A_DFF], BF16).ap()
    io["W2Tsl_bf"] = nc.dram_tensor("W2Tsl_bf", [512 // cores, 16 * AD], BF16).ap()
    io["WoutTsl_bf"] = nc.dram_tensor("WoutTsl_bf", [AD // cores, D], BF16).ap()
    io["xln_tm"] = nc.dram_tensor("xln_tm", [TOK, D], BF16).ap()
    io["y_dram"] = nc.dram_tensor("y_dram", [TOK, AD], BF16).ap()
    io["tabloc"] = nc.dram_tensor("tabloc", [768, TOK], F32).ap()
    for l in range(4):
        io[f"eloc{l}"] = nc.dram_tensor(f"eloc{l}", [128, 8], F32).ap()
        io[f"egath{l}"] = nc.dram_tensor(f"egath{l}", [512, 8], F32).ap()
    io["sh_WinT"] = nc.dram_tensor("sh_WinT", [128, NDT * AD], BF16, addr_space="Shared").ap()
    io["sh_W1T"] = nc.dram_tensor("sh_W1T", [512, 4 * A_DFF], BF16,
                                  addr_space="Shared").ap()
    io["sh_W2T"] = nc.dram_tensor("sh_W2T", [512, 16 * AD], BF16,
                                  addr_space="Shared").ap()
    io["sh_WoutT"] = nc.dram_tensor("sh_WoutT", [AD, D], BF16,
                                    addr_space="Shared").ap()
    io["sh_tab"] = nc.dram_tensor("sh_tab", [cores * 768, TOK], F32,
                                  addr_space="Shared").ap()

    with ExitStack() as es:
        p = P(nc, es, cores)
        block = es.enter_context(nc.Block())
        emit_step2(p, block, io)
    return nc


def step2_in_maps(inputs, x_full, cores=8):
    AD = A_DIM
    Win = np.asarray(inputs["L0_attn_Win"], np.float32)
    W1 = np.asarray(inputs["L0_attn_W1"], np.float32)
    W2 = np.asarray(inputs["L0_attn_W2"], np.float32)
    Wout = np.asarray(inputs["L0_attn_Wout"], np.float32)
    WinT = np.ascontiguousarray(
        Win.T.reshape(NDT, 128, AD).transpose(1, 0, 2).reshape(128, NDT * AD))
    W1T = np.ascontiguousarray(
        W1.transpose(0, 2, 1).reshape(4, 4, 128, A_DFF).transpose(0, 2, 1, 3)
        .reshape(512, 4 * A_DFF))
    W2T = np.ascontiguousarray(
        W2.transpose(0, 2, 1).reshape(4, 16, 128, AD).transpose(0, 2, 1, 3)
        .reshape(512, 16 * AD))
    WoutT = np.ascontiguousarray(Wout.T)
    phr = np.asarray(inputs["L0_attn_phz_re"], np.float32)
    phi = np.asarray(inputs["L0_attn_phz_im"], np.float32)
    lcr = np.asarray(inputs["L0_attn_lc_re"], np.float32)
    lci = np.asarray(inputs["L0_attn_lc_im"], np.float32)
    scl = np.asarray(inputs["L0_attn_sc_lin"], np.float32)
    b1 = np.asarray(inputs["L0_attn_b1"], np.float32)
    zre_t = np.ascontiguousarray(phr.reshape(4, 4, 128).transpose(2, 0, 1)
                                 .reshape(128, 16))
    zim_t = np.ascontiguousarray(phi.reshape(4, 4, 128).transpose(2, 0, 1)
                                 .reshape(128, 16))
    lc_t = np.ascontiguousarray(
        np.stack([lcr, lci], -1).reshape(4, 4, 128, 2).transpose(2, 0, 1, 3)
        .reshape(128, 32))
    scl_t = np.ascontiguousarray(scl.reshape(4, 4, 128).transpose(2, 0, 1)
                                 .reshape(128, 16))
    b1_t = np.ascontiguousarray(b1.reshape(4, 16, 128).transpose(2, 0, 1)
                                .reshape(128, 64))
    WSL, FSL, OSL = AD // cores, A_DFF // cores, D // cores
    maps = []
    for k in range(cores):
        q = k % 4
        qs = np.zeros((128, 4), np.float32)
        qs[:, q] = 1.0
        u0 = 2 * k
        maps.append(dict(
            x_in=np.ascontiguousarray(x_full[k * TOK:(k + 1) * TOK]),
            WinTsl=np.ascontiguousarray(WinT[k * 16:(k + 1) * 16]),
            W1Tsl=np.ascontiguousarray(W1T[k * 64:(k + 1) * 64]),
            W2Tsl=np.ascontiguousarray(W2T[k * 64:(k + 1) * 64]),
            WoutTsl=np.ascontiguousarray(WoutT[k * WSL:(k + 1) * WSL]),
            zbre=np.ascontiguousarray(zre_t[:, u0:u0 + 2]),
            zbim=np.ascontiguousarray(zim_t[:, u0:u0 + 2]),
            zre_t=zre_t, zim_t=zim_t, lc_t=lc_t, scl_t=scl_t, b1_t=b1_t,
            qsel=qs,
            ln1w=np.ascontiguousarray(np.asarray(inputs["L0_attn_ln1_w"], np.float32)),
            ln1b=np.ascontiguousarray(np.asarray(inputs["L0_attn_ln1_b"], np.float32)),
            ln2w=np.ascontiguousarray(np.asarray(inputs["L0_attn_ln2_w"], np.float32)),
            ln2b=np.ascontiguousarray(np.asarray(inputs["L0_attn_ln2_b"], np.float32)),
            b2r=np.ascontiguousarray(np.asarray(inputs["L0_attn_b2"], np.float32)),
            bin_r=np.ascontiguousarray(
                np.asarray(inputs["L0_attn_bin"], np.float32).reshape(1, AD)),
            lnlw_r=np.ascontiguousarray(
                np.asarray(inputs["L0_attn_lnl_w"], np.float32).reshape(1, AD)),
            lnlb_r=np.ascontiguousarray(
                np.asarray(inputs["L0_attn_lnl_b"], np.float32).reshape(1, AD)),
            bout_r=np.ascontiguousarray(
                np.asarray(inputs["L0_attn_bout"], np.float32).reshape(1, D)),
        ))
    return maps



def build_step1_nc(cores=8):
    nc = bass.Bass("TRN2", target_bir_lowering=False)
    OSL = D // cores
    io = {}
    io["x_in"] = nc.dram_tensor("x_in", [TOK, D], F32, kind="ExternalInput").ap()
    io["x_out"] = nc.dram_tensor("x_out", [TOK, D], F32, kind="ExternalOutput").ap()
    io["WTsl"] = nc.dram_tensor("WTsl", [OSL, D], F32, kind="ExternalInput").ap()
    io["loraAT"] = nc.dram_tensor("loraAT", [D, 128], F32, kind="ExternalInput").ap()
    io["loraBT"] = nc.dram_tensor("loraBT", [128, D], F32, kind="ExternalInput").ap()
    io["lora_w"] = nc.dram_tensor("lora_w", [128, 1], F32, kind="ExternalInput").ap()
    io["bias"] = nc.dram_tensor("bias", [1, D], F32, kind="ExternalInput").ap()
    io["WTsl_bf"] = nc.dram_tensor("WTsl_bf", [OSL, D], BF16).ap()
    io["AT_bf"] = nc.dram_tensor("AT_bf", [D, 128], BF16).ap()
    io["BT_bf"] = nc.dram_tensor("BT_bf", [128, D], BF16).ap()
    io["xln_tm"] = nc.dram_tensor("xln_tm", [TOK, D], BF16).ap()

    io["sh_WT"] = nc.dram_tensor("sh_WT", [D, D], BF16, addr_space="Shared").ap()

    with ExitStack() as es:
        p = P(nc, es, cores)
        block = es.enter_context(nc.Block())
        emit_step1(p, block, io)
    return nc


def step1_in_maps(inputs, x_full, cores=8):
    W = np.asarray(inputs["L0_ffn_W"], np.float32)
    WT = np.ascontiguousarray(W.T)
    AT = np.ascontiguousarray(
        np.asarray(inputs["L0_ffn_lora_A"], np.float32).reshape(128, D).T)
    BT = np.ascontiguousarray(
        np.asarray(inputs["L0_ffn_lora_B"], np.float32).transpose(0, 2, 1).reshape(128, D))
    OSL = D // cores
    maps = []
    for k in range(cores):
        maps.append(dict(
            x_in=np.ascontiguousarray(x_full[k * TOK:(k + 1) * TOK]),
            WTsl=np.ascontiguousarray(WT[k * OSL:(k + 1) * OSL]),
            loraAT=AT,
            loraBT=BT,
            lora_w=np.ascontiguousarray(np.repeat(
                np.asarray(inputs["L0_ffn_lora_weights"], np.float32), 16).reshape(128, 1)),
            bias=np.ascontiguousarray(
                np.asarray(inputs["L0_ffn_bias"], np.float32).reshape(1, D)),
        ))
    return maps





def _decl_step1_io(nc, pre, cores, x_in, x_out):
    OSL = D // cores
    io = {"x_in": x_in, "x_out": x_out}
    io["WTsl"] = nc.dram_tensor(pre + "WTsl", [OSL, D], F32, kind="ExternalInput").ap()
    io["loraAT"] = nc.dram_tensor(pre + "loraAT", [D, 128], F32, kind="ExternalInput").ap()
    io["loraBT"] = nc.dram_tensor(pre + "loraBT", [128, D], F32, kind="ExternalInput").ap()
    io["lora_w"] = nc.dram_tensor(pre + "lora_w", [128, 1], F32, kind="ExternalInput").ap()
    io["bias"] = nc.dram_tensor(pre + "bias", [1, D], F32, kind="ExternalInput").ap()
    io["WTsl_bf"] = nc.dram_tensor(pre + "WTsl_bf", [OSL, D], BF16).ap()
    io["AT_bf"] = nc.dram_tensor(pre + "AT_bf", [D, 128], BF16).ap()
    io["BT_bf"] = nc.dram_tensor(pre + "BT_bf", [128, D], BF16).ap()
    io["xln_tm"] = nc.dram_tensor(pre + "xln_tm", [TOK, D], BF16).ap()
    io["sh_WT"] = nc.dram_tensor(pre + "sh_WT", [D, D], BF16, addr_space="Shared").ap()
    return io


def _decl_step2_io(nc, pre, cores, x_in, x_out):
    AD = A_DIM
    io = {"x_in": x_in, "x_out": x_out}
    io["WinTsl"] = nc.dram_tensor(pre + "WinTsl", [128 // cores, NDT * AD], F32, kind="ExternalInput").ap()
    io["W1Tsl"] = nc.dram_tensor(pre + "W1Tsl", [512 // cores, 4 * A_DFF], F32, kind="ExternalInput").ap()
    io["W2Tsl"] = nc.dram_tensor(pre + "W2Tsl", [512 // cores, 16 * AD], F32, kind="ExternalInput").ap()
    io["WoutTsl"] = nc.dram_tensor(pre + "WoutTsl", [AD // cores, D], F32, kind="ExternalInput").ap()
    for name, shp in [("zbre", [128, 2]), ("zbim", [128, 2]),
                      ("zre_t", [128, 16]), ("zim_t", [128, 16]),
                      ("lc_t", [128, 32]), ("scl_t", [128, 16]),
                      ("b1_t", [128, 64]), ("qsel", [128, 4])]:
        io[name] = nc.dram_tensor(pre + name, shp, F32, kind="ExternalInput").ap()
    for name in ["ln1w", "ln1b", "ln2w", "ln2b", "b2r"]:
        io[name] = nc.dram_tensor(pre + name, [4, AD], F32, kind="ExternalInput").ap()
    io["bin_r"] = nc.dram_tensor(pre + "bin_r", [1, AD], F32, kind="ExternalInput").ap()
    io["lnlw_r"] = nc.dram_tensor(pre + "lnlw_r", [1, AD], F32, kind="ExternalInput").ap()
    io["lnlb_r"] = nc.dram_tensor(pre + "lnlb_r", [1, AD], F32, kind="ExternalInput").ap()
    io["bout_r"] = nc.dram_tensor(pre + "bout_r", [1, D], F32, kind="ExternalInput").ap()
    io["WinTsl_bf"] = nc.dram_tensor(pre + "WinTsl_bf", [128 // cores, NDT * AD], BF16).ap()
    io["W1Tsl_bf"] = nc.dram_tensor(pre + "W1Tsl_bf", [512 // cores, 4 * A_DFF], BF16).ap()
    io["W2Tsl_bf"] = nc.dram_tensor(pre + "W2Tsl_bf", [512 // cores, 16 * AD], BF16).ap()
    io["WoutTsl_bf"] = nc.dram_tensor(pre + "WoutTsl_bf", [AD // cores, D], BF16).ap()
    io["xln_tm"] = nc.dram_tensor(pre + "xln_tm", [TOK, D], BF16).ap()
    io["y_dram"] = nc.dram_tensor(pre + "y_dram", [TOK, AD], BF16).ap()
    io["tabloc"] = nc.dram_tensor(pre + "tabloc", [768, TOK], F32).ap()
    for l in range(4):
        io[f"eloc{l}"] = nc.dram_tensor(pre + f"eloc{l}", [128, 8], F32).ap()
        io[f"egath{l}"] = nc.dram_tensor(pre + f"egath{l}", [512, 8], F32).ap()
    io["sh_WinT"] = nc.dram_tensor(pre + "sh_WinT", [128, NDT * AD], BF16, addr_space="Shared").ap()
    io["sh_W1T"] = nc.dram_tensor(pre + "sh_W1T", [512, 4 * A_DFF], BF16, addr_space="Shared").ap()
    io["sh_W2T"] = nc.dram_tensor(pre + "sh_W2T", [512, 16 * AD], BF16, addr_space="Shared").ap()
    io["sh_WoutT"] = nc.dram_tensor(pre + "sh_WoutT", [AD, D], BF16, addr_space="Shared").ap()
    io["sh_tab"] = nc.dram_tensor(pre + "sh_tab", [cores * 768, TOK], F32, addr_space="Shared").ap()
    return io


def _decl_step4_io(nc, pre, cores, x_in, x_out, weights=None):
    HSL = H_HEADS // cores
    io = {"x_in": x_in, "x_out": x_out}
    if weights is None:
        io["upsl"] = nc.dram_tensor(pre + "upsl", [HSL * H_DH, H_M], F32, kind="ExternalInput").ap()
        io["dnsl"] = nc.dram_tensor(pre + "dnsl", [HSL * H_M, H_DH], F32, kind="ExternalInput").ap()
        io["perm_t"] = nc.dram_tensor(pre + "perm_t", [128, 32], I32, kind="ExternalInput").ap()
        io["upsl_bf"] = nc.dram_tensor(pre + "upsl_bf", [HSL * H_DH, H_M], BF16).ap()
        io["dnsl_bf"] = nc.dram_tensor(pre + "dnsl_bf", [HSL * H_M, H_DH], BF16).ap()
        io["sh_up"] = nc.dram_tensor(pre + "sh_up", [H_HEADS * H_DH, H_M], BF16, addr_space="Shared").ap()
        io["sh_dn"] = nc.dram_tensor(pre + "sh_dn", [H_HEADS * H_M, H_DH], BF16, addr_space="Shared").ap()
    else:
        for k in ["upsl", "dnsl", "perm_t", "upsl_bf", "dnsl_bf", "sh_up", "sh_dn"]:
            io[k] = weights[k]
    io["xln_tm"] = nc.dram_tensor(pre + "xln_tm", [TOK, D], BF16).ap()
    io["xln_fm"] = nc.dram_tensor(pre + "xln_fm", [D, TOK], BF16).ap()
    io["deltaT"] = nc.dram_tensor(pre + "deltaT", [D, TOK], F32).ap()
    return io


def _decl_step5_io(nc, pre, cores, x_in, x_out):
    TSL = D // cores
    io = {"x_in": x_in, "x_out": x_out}
    io["w1sl"] = nc.dram_tensor(pre + "w1sl", [TSL, D], F32, kind="ExternalInput").ap()
    io["w2Tsl"] = nc.dram_tensor(pre + "w2Tsl", [D, TSL], F32, kind="ExternalInput").ap()
    io["mask_sl"] = nc.dram_tensor(pre + "mask_sl", [D, TSL], F32, kind="ExternalInput").ap()
    io["b2"] = nc.dram_tensor(pre + "b2", [1, D], F32, kind="ExternalInput").ap()
    io["w1sl_bf"] = nc.dram_tensor(pre + "w1sl_bf", [TSL, D], BF16).ap()
    io["w2Tsl_bf"] = nc.dram_tensor(pre + "w2Tsl_bf", [D, TSL], BF16).ap()
    io["attn_sl"] = nc.dram_tensor(pre + "attn_sl", [D, TSL], BF16).ap()
    io["xln_tm"] = nc.dram_tensor(pre + "xln_tm", [TOK, D], BF16).ap()
    io["sh_w1"] = nc.dram_tensor(pre + "sh_w1", [D, D], BF16, addr_space="Shared").ap()
    io["sh_attn"] = nc.dram_tensor(pre + "sh_attn", [cores * D, TSL], BF16, addr_space="Shared").ap()
    return io


def emit_gate(p, block, prev_done):
    if prev_done:
        @block.vector
        def _(v):
            for s, val in prev_done:
                v.wait_ge(s.h, val)

        @block.scalar
        def _(sc):
            for s, val in prev_done:
                sc.wait_ge(s.h, val)

        @block.tensor
        def _(t):
            for s, val in prev_done:
                t.wait_ge(s.h, val)

        @block.sync
        def _(sy):
            for s, val in prev_done:
                sy.wait_ge(s.h, val)

    gp_sems = ["s_gp", "s_gpw", "s_gpl", "s_gpt", "s_gpz", "s_gplw",
               "s_gpp", "s_gx0", "s_gx1", "s_gs0", "s_gs1"]
    targets = [(p.sems[n], p.sems[n].v) for n in gp_sems if n in p.sems]
    if targets:
        @block.gpsimd
        def _(gp):
            for s, tgt in targets:
                gp.wait_ge(s.h, tgt)


def build_fused_nc(cores=8, nsteps=6):
    nc = bass.Bass("TRN2", target_bir_lowering=False)
    x0 = nc.dram_tensor("x0", [TOK, D], F32, kind="ExternalInput").ap()
    xf = nc.dram_tensor("x_out", [TOK, D], F32, kind="ExternalOutput").ap()
    xA = nc.dram_tensor("xA", [TOK, D], F32).ap()
    xB = nc.dram_tensor("xB", [TOK, D], F32).ap()

    outs = [xA, xB, xA, xB, xA, xf]
    outs[nsteps - 1] = xf
    io1 = _decl_step1_io(nc, "p1_", cores, x0, outs[0])
    io2 = _decl_step2_io(nc, "p2_", cores, outs[0], outs[1])
    io3 = dict(io1)
    io3.update({"x_in": outs[1], "x_out": outs[2],
                "xln_tm": nc.dram_tensor("p3_xln_tm", [TOK, D], BF16).ap()})
    io4 = _decl_step4_io(nc, "p4_", cores, outs[2], outs[3])
    io5 = _decl_step5_io(nc, "p5_", cores, outs[3], outs[4])
    io6 = _decl_step4_io(nc, "p6_", cores, outs[4], outs[5], weights=io4)

    with ExitStack() as es:
        p = P(nc, es, cores)
        block = es.enter_context(nc.Block())
        sh1 = {}
        sh4 = {}
        d1 = emit_step1(p, block, io1, tag="s1", share=sh1)
        if nsteps >= 2:
            emit_gate(p, block, d1)
            d2 = emit_step2(p, block, io2, tag="s2", prev_done=d1)
        if nsteps >= 3:
            emit_gate(p, block, d2)
            d3 = emit_step1(p, block, io3, tag="s3", prev_done=d2, share=sh1)
        if nsteps >= 4:
            emit_gate(p, block, d3)
            d4 = emit_step4(p, block, io4, tag="s4", prev_done=d3, share=sh4)
        if nsteps >= 5:
            emit_gate(p, block, d4)
            d5 = emit_step5(p, block, io5, tag="s5", prev_done=d4)
        if nsteps >= 6:
            emit_gate(p, block, d5)
            d6 = emit_step4(p, block, io6, tag="s6", prev_done=d5, share=sh4)
    return nc


def fused_in_maps(inputs, cores=8):
    x = np.ascontiguousarray(
        np.asarray(inputs["L0_attn_x"], np.float32).reshape(8192, D))
    m1 = step1_in_maps(inputs, x, cores)
    m2 = step2_in_maps(inputs, x, cores)
    m4 = step4_in_maps(inputs, x, cores)
    m5 = step5_in_maps(inputs, x, cores)
    maps = []
    for k in range(cores):
        m = {}
        m["x0"] = m1[k].pop("x_in")
        m2[k].pop("x_in"); m4[k].pop("x_in"); m5[k].pop("x_in")
        for pre, mm in [("p1_", m1[k]), ("p2_", m2[k]), ("p4_", m4[k]),
                        ("p5_", m5[k])]:
            for name, arr in mm.items():
                m[pre + name] = arr
        maps.append(m)
    return maps


_NC_CACHE = {}


def _get_nc(name, builder):
    if name not in _NC_CACHE:
        _NC_CACHE[name] = builder(cores=8)
    return _NC_CACHE[name]


def _launch(nc, maps):
    res = run_bass_kernel_spmd(nc, maps, core_ids=list(range(8)))
    return np.concatenate([res.results[c]["x_out"] for c in range(8)], 0)


def kernel_unfused(**inputs):
    x = np.ascontiguousarray(
        np.asarray(inputs["L0_attn_x"], np.float32).reshape(8192, D))
    nc1 = _get_nc("s1", build_step1_nc)
    nc2 = _get_nc("s2", build_step2_nc)
    nc4 = _get_nc("s4", build_step4_nc)
    nc5 = _get_nc("s5", build_step5_nc)
    x = _launch(nc1, step1_in_maps(inputs, x))
    x = _launch(nc2, step2_in_maps(inputs, x))
    x = _launch(nc1, step1_in_maps(inputs, x))
    x = _launch(nc4, step4_in_maps(inputs, x))
    x = _launch(nc5, step5_in_maps(inputs, x))
    x = _launch(nc4, step4_in_maps(inputs, x))
    return x.reshape(2, 4096, D)


def kernel(**inputs):
    nc = _get_nc("fused", build_fused_nc)
    maps = fused_in_maps(inputs, cores=8)
    res = run_bass_kernel_spmd(nc, maps, core_ids=list(range(8)))
    x = np.concatenate([res.results[c]["x_out"] for c in range(8)], 0)
    return x.reshape(2, 4096, D)


# version 07 (final)
# speedup vs baseline: 1.1398x; 1.0099x over previous
import numpy as np
from contextlib import ExitStack

import concourse.bass as bass
import concourse.mybir as mybir
from concourse.bass_utils import run_bass_kernel_spmd
from concourse.bass import IndirectOffsetOnAxis

F32 = mybir.dt.float32
BF16 = mybir.dt.bfloat16
I32 = mybir.dt.int32
AF = mybir.ActivationFunctionType
ALU = mybir.AluOpType
AX = mybir.AxisListType

TOK = 1024
NTT = TOK // 128
D = 4096
NDT = D // 128
EPS = 1e-5

A_DEPTH, A_DIM, A_DFF = 4, 512, 2048
H_HEADS, H_DH, H_M = 16, 256, 1024


class Sem:
    __slots__ = ("h", "v", "_cm")

    def __init__(self, nc, name):
        self._cm = nc.semaphore(name)
        self.h = self._cm.__enter__()
        self.v = 0

    def inc(self, instr, n):
        instr.then_inc(self.h, n)
        self.v += n
        return self.v


class P:
    def __init__(self, nc, es, cores):
        self.nc = nc
        self.es = es
        self.cores = cores
        self.sems = {}
        self.n_bar = 0
        self.zrow = self.sb("p_zrow", [1, 8], F32)

    def sem(self, name):
        if name not in self.sems:
            self.sems[name] = Sem(self.nc, name)
        return self.sems[name]

    def sb(self, name, shape, dtype, es=None):
        return (es or self.es).enter_context(self.nc.sbuf_tensor(name, shape, dtype))

    def psum(self, name, shape, dtype=F32, es=None):
        return (es or self.es).enter_context(self.nc.psum_tensor(name, shape, dtype))


def emit_barrier(p, block):
    nc = p.nc
    s_cc = p.sem("s_cc")
    i = p.n_bar
    p.n_bar += 1
    bar_in = nc.dram_tensor(f"bar_in{i}", [1, 8], F32)
    bar_out = nc.dram_tensor(f"bar_out{i}", [p.cores, 8], F32, addr_space="Shared")
    s_gp = p.sem("s_gp")

    s_gpc = p.sem("s_gpc")

    @block.gpsimd
    def _(gp):
        ins = gp.memset(p.zrow[:], 0.0)
        zc = s_gpc.inc(ins, 1)
        gp.wait_ge(s_gpc.h, zc)
        ins = gp.dma_start(out=bar_in.ap(), in_=p.zrow[:])
        zv = s_gp.inc(ins, 16)
        gp.wait_ge(s_gp.h, zv)
        ins = gp.collective_compute(
            "AllGather", ALU.bypass, replica_groups=[list(range(p.cores))],
            ins=[bar_in.ap()], outs=[bar_out.ap()],
        )
        s_cc.inc(ins, 1)

    return s_cc.v


def emit_allgather(p, block, local_in_ap, shared_out_ap, wait_gp=None):
    s_cc = p.sem("s_cc")

    @block.gpsimd
    def _(gp):
        if wait_gp is not None:
            gp.wait_ge(wait_gp[0].h, wait_gp[1])
        ins = gp.collective_compute(
            "AllGather", ALU.bypass, replica_groups=[list(range(p.cores))],
            ins=[local_in_ap], outs=[shared_out_ap],
        )
        s_cc.inc(ins, 1)

    return s_cc.v



def emit_ln(p, block, es, x_dram, xln_tm, xlnT, tag, fm_dram=None,
            x_ready=None):
    s_lx = [p.sem("s_lx0"), p.sem("s_lx1")]
    s_v = p.sem("s_v")
    s_s = p.sem("s_s")
    s_wm = [p.sem("s_wm0"), p.sem("s_wm1")]
    s_trd = p.sem("s_trd")

    NC5 = D // 512
    xt = [p.sb(f"{tag}_xt{i}", [128, D], F32, es) for i in range(2)]
    stats = p.sb(f"{tag}_bst", [128, 6 * NC5], F32, es)
    agg = p.sb(f"{tag}_agg", [128, 2 * NTT], F32, es)
    vep = p.sb(f"{tag}_vep", [128, NTT], F32, es)
    sd = p.sb(f"{tag}_sd", [128, NTT], F32, es)
    rs = p.sb(f"{tag}_rs", [128, NTT], F32, es)
    nb = p.sb(f"{tag}_nb", [128, NTT], F32, es)
    xlno = [p.sb(f"{tag}_xo{i}", [128, D], BF16, es) for i in range(2)]

    v0, s0 = s_v.v, s_s.v
    lx0 = [s_lx[0].v, s_lx[1].v]
    wm0 = [s_wm[0].v, s_wm[1].v]
    trd0 = s_trd.v
    lx1_p = [lx0[tt % 2] + 16 * (tt // 2 + 1) for tt in range(NTT)]
    lx2_p = [lx0[tt % 2] + 16 * (NTT // 2 + tt // 2 + 1) for tt in range(NTT)]
    v1_p = [v0 + tt + 1 for tt in range(NTT)]
    sqrt_p = s0 + 1
    rs_p = v0 + NTT + 1
    s2_p = [s0 + 1 + tt + 1 for tt in range(NTT)]
    wm_p = [wm0[tt % 2] + 16 * (tt // 2 + 1) for tt in range(NTT)]
    n_tr = NDT + (NDT if fm_dram is not None else 0)
    tr_all = trd0 + 16 * n_tr

    @block.sync
    def _(sync):
        for tt in range(NTT):
            if tt == 0 and x_ready:
                for sem, val in x_ready:
                    sync.wait_ge(sem.h, val)
            if tt >= 2:
                sync.wait_ge(s_v.h, v1_p[tt - 2])
            ins = sync.dma_start(out=xt[tt % 2][:],
                                 in_=x_dram[tt * 128:(tt + 1) * 128, :])
            assert s_lx[tt % 2].inc(ins, 16) == lx1_p[tt]
        for tt in range(NTT):
            if tt < 2:
                sync.wait_ge(s_v.h, v1_p[-1])
            ins = sync.dma_start(out=xt[tt % 2][:],
                                 in_=x_dram[tt * 128:(tt + 1) * 128, :])
            assert s_lx[tt % 2].inc(ins, 16) == lx2_p[tt]
            if tt >= 1:
                w = tt - 1
                sync.wait_ge(s_s.h, s2_p[w])
                ins = sync.dma_start(out=xln_tm[w * 128:(w + 1) * 128, :],
                                     in_=xlno[w % 2][:])
                assert s_wm[w % 2].inc(ins, 16) == wm_p[w]
        for w in (NTT - 1,):
            sync.wait_ge(s_s.h, s2_p[w])
            ins = sync.dma_start(out=xln_tm[w * 128:(w + 1) * 128, :],
                                 in_=xlno[w % 2][:])
            assert s_wm[w % 2].inc(ins, 16) == wm_p[w]

    @block.vector
    def _(v):
        s_ve = p.sem("s_ve")
        for tt in range(NTT):
            v.wait_ge(s_lx[tt % 2].h, lx1_p[tt])
            for c in range(NC5):
                ins = v.bn_stats(stats[:, 6 * c:6 * c + 6],
                                 xt[tt % 2][:, 512 * c:512 * (c + 1)])
            v.wait_ge(s_ve.h, s_ve.inc(ins, 1))
            ins = v.bn_aggr(agg[:, 2 * tt:2 * tt + 2], stats[:])
            v.wait_ge(s_ve.h, s_ve.inc(ins, 1))
            ins = v.tensor_scalar(vep[:, tt:tt + 1], agg[:, 2 * tt + 1:2 * tt + 2],
                                  EPS, None, ALU.add)
            assert s_v.inc(ins, 1) == v1_p[tt]
        v.wait_ge(s_s.h, sqrt_p)
        ins = v.reciprocal(rs[:], sd[:])
        v.wait_ge(s_ve.h, s_ve.inc(ins, 1))
        means = agg[:].rearrange("p (t two) -> p two t", two=2)[:, 0, :]
        ins = v.tensor_tensor(nb[:], means, rs[:], ALU.mult)
        v.wait_ge(s_ve.h, s_ve.inc(ins, 1))
        ins = v.tensor_scalar(nb[:], nb[:], -1.0, None, ALU.mult)
        assert s_v.inc(ins, 1) == rs_p

    @block.scalar
    def _(s):
        s.wait_ge(s_v.h, v1_p[-1])
        ins = s.activation(sd[:], vep[:], AF.Sqrt)
        assert s_s.inc(ins, 1) == sqrt_p
        for tt in range(NTT):
            s.wait_ge(s_lx[tt % 2].h, lx2_p[tt])
            if tt == 0:
                s.wait_ge(s_v.h, rs_p)
            if tt >= 2:
                s.wait_ge(s_wm[tt % 2].h, wm_p[tt - 2])
            ins = s.activation(xlno[tt % 2][:], xt[tt % 2][:], AF.Identity,
                               bias=nb[:, tt:tt + 1], scale=rs[:, tt:tt + 1])
            assert s_s.inc(ins, 1) == s2_p[tt]
        for i, w in enumerate(wm_p[-2:]):
            s.wait_ge(s_wm[(NTT - 2 + i) % 2].h, w)
        for dt in range(NDT):
            ins = s.dma_start(out=xlnT[:, dt * TOK:(dt + 1) * TOK],
                              in_=xln_tm[:, dt * 128:(dt + 1) * 128],
                              transpose=True)
            s_trd.inc(ins, 16)
        if fm_dram is not None:
            s.wait_ge(s_trd.h, trd0 + 16 * NDT)
            for dt in range(NDT):
                ins = s.dma_start(out=fm_dram[dt * 128:(dt + 1) * 128, :],
                                  in_=xlnT[:, dt * TOK:(dt + 1) * TOK])
                s_trd.inc(ins, 16)
    assert s_trd.v == tr_all

    return tr_all, tr_all



def emit_bcast_row(p, block, es, vec_dram_row, n, out_sb, scale, tag, ones1,
                   ones_ready):
    s_lb = p.sem("s_lx")
    s_mm = p.sem("s_mm")
    s_v = p.sem("s_v")
    vrow = p.sb(f"{tag}_vrow", [1, n], F32, es)
    pb = [p.psum(f"{tag}_bps{i}", [128, 512], F32, es=es) for i in range(2)]
    hold = {}
    mm_c, v_c = [], []
    nch = (n + 511) // 512

    @block.sync
    def _(sync):
        ins = sync.dma_start(out=vrow[:], in_=vec_dram_row)
        hold["ld"] = s_lb.inc(ins, 16)

    v0 = s_v.v
    v_pred = [v0 + c + 1 for c in range(nch)]

    @block.tensor
    def _(t):
        t.wait_ge(s_lb.h, hold["ld"])
        t.wait_ge(s_v.h, ones_ready)
        for c in range(nch):
            w = min(512, n - 512 * c)
            if c >= 2:
                t.wait_ge(s_v.h, v_pred[c - 2])
            ins = t.matmul(pb[c % 2][:, 0:w], ones1[:, 0:128],
                           vrow[:, 512 * c:512 * c + w], start=True, stop=True)
            mm_c.append(s_mm.inc(ins, 1))

    @block.vector
    def _(v):
        for c in range(nch):
            w = min(512, n - 512 * c)
            v.wait_ge(s_mm.h, mm_c[c])
            ins = v.tensor_scalar(out_sb[:, 512 * c:512 * c + w], pb[c % 2][:, 0:w],
                                  scale, None, ALU.mult)
            v_c.append(s_v.inc(ins, 1))

    assert v_c == v_pred, (v_c, v_pred)
    return v_c[-1]



def emit_mm_main(p, block, es, xlnT, tr_all, w_colblock_dma, n_out,
                 extra_mm=None, extra_ready=0, tag="mm", epi=None, nk=NDT):
    s_lw = [p.sem("s_lw0"), p.sem("s_lw1")]
    s_lxo = [p.sem(f"s_lxo{i}") for i in range(4)]
    s_mm = p.sem("s_mm")
    s_v = p.sem("s_v")
    s_s = p.sem("s_s")
    s_st = [p.sem("s_st0"), p.sem("s_st1")]
    s_trd = p.sem("s_trd")

    NOB = n_out // 512
    wcb = [p.sb(f"{tag}_wcb{i}", [128, nk * 512], BF16, es) for i in range(2)]
    ps = [p.psum(f"{tag}_ps{i}", [128, 512], F32, es=es) for i in range(2)]
    tmp = [p.sb(f"{tag}_tmp{i}", [128, 512], F32, es) for i in range(2)]
    xnew = [p.sb(f"{tag}_xn{i}", [128, 512], F32, es) for i in range(2)]
    xo = [p.sb(f"{tag}_xo{i}", [128, 512], F32, es) for i in range(4)]

    groups = [(ob, tt) for ob in range(NOB) for tt in range(NTT)]
    NG = len(groups)
    scale, bias_b, act = epi["scale"], epi.get("bias_b"), epi.get("act")
    assert act in (None, "gelu")

    mm0, v0, s0 = s_mm.v, s_v.v, s_s.v
    lw0 = [s.v for s in s_lw]
    lxo0 = [s.v for s in s_lxo]
    st0 = [s.v for s in s_st]
    mm_p = [mm0 + g + 1 for g in range(NG)]
    v1_p = [v0 + 2 * g + 1 for g in range(NG)]
    v2_p = [v0 + 2 * g + 2 for g in range(NG)]
    sact_p = [s0 + g + 1 for g in range(NG)]
    lw_p = [lw0[ob % 2] + 16 * (ob // 2 + 1) for ob in range(NOB)]
    lxo_p = [lxo0[g % 4] + 16 * (g // 4 + 1) for g in range(NG)]
    st_p = [st0[g % 2] + 16 * (g // 2 + 1) for g in range(NG)]

    def emit_store(sync, g):
        ob, tt = groups[g]
        sync.wait_ge(s_v.h, v2_p[g])
        ins = sync.dma_start(
            out=epi["xout"][tt * 128:(tt + 1) * 128, ob * 512:(ob + 1) * 512],
            in_=xnew[g % 2][:])
        assert s_st[g % 2].inc(ins, 16) == st_p[g]

    @block.sync
    def _(sync):
        for ob in range(NOB):
            if ob == 0:
                sync.wait_ge(s_trd.h, tr_all)
            if ob >= 2:
                for g in range((ob - 2) * NTT, (ob - 1) * NTT):
                    emit_store(sync, g)
                sync.wait_ge(s_mm.h, mm_p[(ob - 2) * NTT + NTT - 1])
            w_colblock_dma(sync, ob, wcb[ob % 2], s_lw[ob % 2])
            assert s_lw[ob % 2].v == lw_p[ob]
        for g in range((NOB - 2) * NTT, NOB * NTT):
            emit_store(sync, g)

    def emit_xo(s, g):
        ob, tt = groups[g]
        if g == 0:
            s.wait_ge(s_trd.h, tr_all)
        if g >= 4:
            s.wait_ge(s_v.h, v2_p[g - 4])
        ins = s.dma_start(
            out=xo[g % 4][:],
            in_=epi["xold"][tt * 128:(tt + 1) * 128, ob * 512:(ob + 1) * 512])
        assert s_lxo[g % 4].inc(ins, 16) == lxo_p[g]

    def emit_gelu(s, g):
        s.wait_ge(s_v.h, v1_p[g])
        ins = s.activation(tmp[g % 2][:], tmp[g % 2][:], AF.Gelu)
        assert s_s.inc(ins, 1) == sact_p[g]

    @block.scalar
    def _(s):
        if act == "gelu":
            for g in range(4):
                emit_xo(s, g)
            for g in range(NG):
                emit_gelu(s, g)
                if g + 4 < NG:
                    emit_xo(s, g + 4)
        else:
            for g in range(NG):
                emit_xo(s, g)

    @block.tensor
    def _(t):
        t.wait_ge(s_trd.h, tr_all)
        if extra_ready:
            t.wait_ge(s_v.h, extra_ready)
        for g, (ob, tt) in enumerate(groups):
            if tt == 0:
                t.wait_ge(s_lw[ob % 2].h, lw_p[ob])
            if g >= 2:
                t.wait_ge(s_v.h, v1_p[g - 2])
            pg = ps[g % 2]
            for dt in range(nk):
                last = (dt == nk - 1) and (extra_mm is None)
                ins = t.matmul(pg[:],
                               xlnT[:, dt * TOK + tt * 128: dt * TOK + (tt + 1) * 128],
                               wcb[ob % 2][:, dt * 512:(dt + 1) * 512],
                               start=(dt == 0), stop=last)
            if extra_mm is not None:
                ins = extra_mm(t, ob, tt, pg)
            assert s_mm.inc(ins, 1) == mm_p[g]

    @block.vector
    def _(v):
        if epi.get("bias_ready"):
            v.wait_ge(s_v.h, epi["bias_ready"])
        for g, (ob, tt) in enumerate(groups):
            v.wait_ge(s_mm.h, mm_p[g])
            pg = ps[g % 2]
            if bias_b is not None:
                ins = v.scalar_tensor_tensor(
                    tmp[g % 2][:], pg[:], scale,
                    bias_b[:, ob * 512:(ob + 1) * 512], ALU.mult, ALU.add)
            else:
                ins = v.tensor_scalar(tmp[g % 2][:], pg[:], scale, None, ALU.mult)
            assert s_v.inc(ins, 1) == v1_p[g]
            v.wait_ge(s_v.h, v1_p[g])
            if act == "gelu":
                v.wait_ge(s_s.h, sact_p[g])
            v.wait_ge(s_lxo[g % 4].h, lxo_p[g])
            if g >= 2:
                v.wait_ge(s_st[g % 2].h, st_p[g - 2])
            ins = v.tensor_tensor(xnew[g % 2][:], tmp[g % 2][:], xo[g % 4][:],
                                  ALU.add)
            assert s_v.inc(ins, 1) == v2_p[g]

    return [(s_st[0], st_p[-2]), (s_st[1], st_p[-1])] if NG % 2 == 0 else None



def emit_step1(p, block, io, tag="s1", prev_done=None, share=None):
    nc = p.nc
    cores = p.cores
    OSL = D // cores

    s_gp = p.sem("s_gp")
    s_trw = p.sem("s_trw")
    s_shw = p.sem("s_shw")
    s_mm = p.sem("s_mm")
    s_v = p.sem("s_v")
    s_cc = p.sem("s_cc")
    s_trd = p.sem("s_trd")

    es = ExitStack()
    with es:
        xlnT = p.sb(f"{tag}_xlnT", [128, NDT * TOK], BF16, es)
        A_T = p.sb(f"{tag}_AT", [128, NDT * 128], BF16, es)
        B_T = p.sb(f"{tag}_BT", [128, D], BF16, es)
        hT = p.sb(f"{tag}_hT", [128, TOK], BF16, es)
        bias_h = p.sb(f"{tag}_biash", [128, D], F32, es)
        coef = p.sb(f"{tag}_coef", [128, 1], F32, es)
        ones1 = p.sb(f"{tag}_ones1", [1, 128], F32, es)
        lw_sb = p.sb(f"{tag}_lwsb", [128, 1], F32, es)
        ps_h = p.psum(f"{tag}_psh", [128, 512], F32, es=es)

        hold = {}

        @block.vector
        def _(v):
            ins = v.memset(ones1[:], 1.0)
            hold["ones"] = s_v.inc(ins, 1)

        s_gpw = p.sem("s_gpw")
        s_gpl = p.sem("s_gpl")
        s_gplw = p.sem("s_gplw")

        if share is not None and "bar" in share:
            hold["casts"] = share["casts"]
            hold["bar"] = share["bar"]
        else:
            @block.gpsimd
            def _(gp):
                ins = gp.dma_start(out=io["WTsl_bf"], in_=io["WTsl"])
                hold["wcast"] = s_gpw.inc(ins, 16)
                ins = gp.dma_start(out=io["AT_bf"], in_=io["loraAT"])
                s_gpl.inc(ins, 16)
                ins = gp.dma_start(out=io["BT_bf"], in_=io["loraBT"])
                hold["casts"] = s_gpl.inc(ins, 16)

            hold["bar"] = emit_allgather(p, block, io["WTsl_bf"], io["sh_WT"],
                                         wait_gp=(s_gpw, hold["wcast"]))
            if share is not None:
                share["casts"] = hold["casts"]
                share["bar"] = hold["bar"]

        @block.sync
        def _(sync):
            sync.wait_ge(p.sem("s_gpl").h, hold["casts"])
            ins = sync.dma_start(
                out=A_T[:].rearrange("p (dt r) -> p dt r", r=128),
                in_=io["AT_bf"].rearrange("(dt p) r -> p dt r", p=128))
            s_trw.inc(ins, 16)
            ins = sync.dma_start(out=B_T[:], in_=io["BT_bf"])
            hold["lora_T"] = s_trw.inc(ins, 16)

        @block.gpsimd
        def _(gp):
            if prev_done:
                for _s, _val in prev_done:
                    gp.wait_ge(_s.h, _val)
            ins = gp.dma_start(out=lw_sb[:], in_=io["lora_w"])
            hold["lw"] = s_gplw.inc(ins, 16)

        @block.vector
        def _(v):
            v.wait_ge(s_gplw.h, hold["lw"])
            ins = v.tensor_scalar(coef[:], lw_sb[:], 2.0, None, ALU.mult)
            hold["coef"] = s_v.inc(ins, 1)

        bias_ready = emit_bcast_row(p, block, es, io["bias"], D, bias_h, 0.5,
                                    f"{tag}b", ones1, hold["ones"])

        with ExitStack() as es_ln:
            tr_all, _ = emit_ln(p, block, es_ln, io["x_in"], io["xln_tm"], xlnT,
                                f"{tag}ln", x_ready=prev_done)

        hT_mm_p = [s_mm.v + 1, s_mm.v + 2]
        hT_v_p = [s_v.v + 1, s_v.v + 2]

        @block.tensor
        def _(t):
            t.wait_ge(s_trw.h, hold["lora_T"])
            t.wait_ge(s_trd.h, tr_all)
            for th in range(2):
                if th == 1:
                    t.wait_ge(s_v.h, hT_v_p[0])
                for dt in range(NDT):
                    ins = t.matmul(
                        ps_h[:],
                        A_T[:, dt * 128:(dt + 1) * 128],
                        xlnT[:, dt * TOK + th * 512: dt * TOK + (th + 1) * 512],
                        start=(dt == 0), stop=(dt == NDT - 1))
                assert s_mm.inc(ins, 1) == hT_mm_p[th]

        @block.vector
        def _(v):
            v.wait_ge(s_v.h, hold["coef"])
            for th in range(2):
                v.wait_ge(s_mm.h, hT_mm_p[th])
                ins = v.tensor_scalar(hT[:, th * 512:(th + 1) * 512], ps_h[:],
                                      coef[:, 0:1], None, ALU.mult)
                assert s_v.inc(ins, 1) == hT_v_p[th]

        def w_colblock(sync, ob, dest, sem):
            if ob == 0:
                sync.wait_ge(s_cc.h, hold["bar"])
            srcap = io["sh_WT"].rearrange("(dt p) o -> p dt o", p=128)[:, :, ob * 512:(ob + 1) * 512]
            ins = sync.dma_start(out=dest[:].rearrange("p (dt j) -> p dt j", j=512),
                                 in_=srcap)
            sem.inc(ins, 16)

        def lora_mm(t, ob, tt, pg):
            return t.matmul(pg[:], hT[:, tt * 128:(tt + 1) * 128],
                            B_T[:, ob * 512:(ob + 1) * 512],
                            start=False, stop=True)

        st_last = emit_mm_main(
            p, block, es, xlnT, tr_all, w_colblock, D,
            extra_mm=lora_mm, extra_ready=hT_v_p[-1], tag=tag,
            epi=dict(scale=0.5, bias_b=bias_h, act=None,
                     xold=io["x_in"], xout=io["x_out"],
                     bias_ready=bias_ready))

        @block.sync
        def _(sync):
            for sem, val in st_last:
                sync.wait_ge(sem.h, val)

    return st_last




def emit_step5(p, block, io, tag="s5", prev_done=None):
    nc = p.nc
    cores = p.cores
    TSL = D // cores

    s_gp = p.sem("s_gp")
    s_gpw = p.sem("s_gpw")
    s_gpl = p.sem("s_gpl")
    s_mm = p.sem("s_mm")
    s_v = p.sem("s_v")
    s_cc = p.sem("s_cc")
    s_trd = p.sem("s_trd")

    es = ExitStack()
    with es:
        xlnT = p.sb(f"{tag}_xlnT", [128, NDT * TOK], BF16, es)
        bias_b = p.sb(f"{tag}_biasb", [128, D], F32, es)
        ones1 = p.sb(f"{tag}_ones1", [1, 128], F32, es)
        hold = {}

        @block.vector
        def _(v):
            ins = v.memset(ones1[:], 1.0)
            hold["ones"] = s_v.inc(ins, 1)

        @block.gpsimd
        def _(gp):
            ins = gp.dma_start(out=io["w1sl_bf"], in_=io["w1sl"])
            hold["w1cast"] = s_gpw.inc(ins, 16)
            ins = gp.dma_start(out=io["w2Tsl_bf"], in_=io["w2Tsl"])
            hold["w2cast"] = s_gpl.inc(ins, 16)

        hold["agw1"] = emit_allgather(p, block, io["w1sl_bf"], io["sh_w1"],
                                      wait_gp=(s_gpw, hold["w1cast"]))

        bias_ready = emit_bcast_row(p, block, es, io["b2"], D, bias_b, 1.0,
                                    f"{tag}b", ones1, hold["ones"])

        NST = D // 128
        s_lb = [p.sem("s_lb0"), p.sem("s_lb1")]
        s_msk = [p.sem("s_msk0"), p.sem("s_msk1")]
        s_w2r = p.sem("s_w2r")
        s_asw = [p.sem("s_asw0"), p.sem("s_asw1")]

        es_b = ExitStack()
        w2T_sb = p.sb(f"{tag}_w2T", [128, NDT * TSL], BF16, es_b)
        w1cb = [p.sb(f"{tag}_w1cb{i}", [128, NDT * 128], BF16, es_b)
                for i in range(2)]
        msk = [p.sb(f"{tag}_msk{i}", [128, TSL], F32, es_b) for i in range(2)]
        asl = [p.sb(f"{tag}_asl{i}", [128, TSL], BF16, es_b) for i in range(2)]
        ps_b = [p.psum(f"{tag}_psb{i}", [128, 512], F32, es=es_b)
                for i in range(2)]

        with ExitStack() as es_ln:
            tr_all, _ = emit_ln(p, block, es_ln, io["x_in"], io["xln_tm"], xlnT,
                                f"{tag}ln", x_ready=prev_done)

        if True:

            mm0b, v0b = s_mm.v, s_v.v
            mmb_p = [mm0b + st * (TSL // 512) + c + 1
                     for st in range(NST) for c in range(TSL // 512)]
            vb_p = [v0b + st + 1 for st in range(NST)]
            lb_p = [s_lb[st % 2].v + 16 * (st // 2 + 1) for st in range(NST)]
            msk_p = [s_msk[st % 2].v + 16 * (st // 2 + 1) for st in range(NST)]
            asw_p = [s_asw[st % 2].v + 16 * (st // 2 + 1) for st in range(NST)]

            @block.sync
            def _(sync):
                if prev_done:
                    for sem, val in prev_done:
                        sync.wait_ge(sem.h, val)
                sync.wait_ge(s_gpl.h, hold["w2cast"])
                ins = sync.dma_start(
                    out=w2T_sb[:].rearrange("p (dt j) -> p dt j", j=TSL),
                    in_=io["w2Tsl_bf"].rearrange("(dt p) j -> p dt j", p=128))
                hold["w2r"] = s_w2r.inc(ins, 16)
                sync.wait_ge(s_cc.h, hold["agw1"])
                for st in range(NST):
                    if st >= 2:
                        sync.wait_ge(s_mm.h, mmb_p[(st - 2) * (TSL // 512) + TSL // 512 - 1])
                        sync.wait_ge(s_v.h, vb_p[st - 2])
                        ins = sync.dma_start(
                            out=io["attn_sl"][(st - 2) * 128:(st - 1) * 128, :],
                            in_=asl[st % 2][:])
                        assert s_asw[st % 2].inc(ins, 16) == asw_p[st - 2]
                    ins = sync.dma_start(
                        out=w1cb[st % 2][:].rearrange("p (dt j) -> p dt j", j=128),
                        in_=io["sh_w1"].rearrange("(dt p) s -> p dt s", p=128)[:, :, st * 128:(st + 1) * 128])
                    assert s_lb[st % 2].inc(ins, 16) == lb_p[st]
                    ins = sync.dma_start(out=msk[st % 2][:],
                                         in_=io["mask_sl"][st * 128:(st + 1) * 128, :])
                    assert s_msk[st % 2].inc(ins, 16) == msk_p[st]
                for st in (NST - 2, NST - 1):
                    sync.wait_ge(s_mm.h, mmb_p[st * (TSL // 512) + TSL // 512 - 1])
                    sync.wait_ge(s_v.h, vb_p[st])
                    ins = sync.dma_start(
                        out=io["attn_sl"][st * 128:(st + 1) * 128, :],
                        in_=asl[st % 2][:])
                    assert s_asw[st % 2].inc(ins, 16) == asw_p[st]

            @block.tensor
            def _(t):
                t.wait_ge(s_w2r.h, hold["w2r"])
                for st in range(NST):
                    t.wait_ge(s_lb[st % 2].h, lb_p[st])
                    if st >= 2:
                        t.wait_ge(s_v.h, vb_p[st - 2])
                    for c in range(TSL // 512):
                        for dt in range(NDT):
                            ins = t.matmul(
                                ps_b[st % 2][:],
                                w1cb[st % 2][:, dt * 128:(dt + 1) * 128],
                                w2T_sb[:, dt * TSL + c * 512: dt * TSL + (c + 1) * 512],
                                start=(dt == 0), stop=(dt == NDT - 1))
                        assert s_mm.inc(ins, 1) == mmb_p[st * (TSL // 512) + c]

            @block.vector
            def _(v):
                for st in range(NST):
                    v.wait_ge(s_mm.h, mmb_p[st * (TSL // 512) + TSL // 512 - 1])
                    v.wait_ge(s_msk[st % 2].h, msk_p[st])
                    if st >= 2:
                        v.wait_ge(s_asw[st % 2].h, asw_p[st - 2])
                    for c in range(TSL // 512):
                        ins = v.tensor_tensor(
                            asl[st % 2][:, c * 512:(c + 1) * 512],
                            ps_b[st % 2][:, 0:512] if TSL == 512 else ps_b[st % 2][:],
                            msk[st % 2][:, c * 512:(c + 1) * 512], ALU.mult)
                    assert s_v.inc(ins, 1) == vb_p[st]

            @block.gpsimd
            def _(gp):
                gp.wait_ge(s_asw[0].h, asw_p[NST - 2])
                gp.wait_ge(s_asw[1].h, asw_p[NST - 1])
            hold["agattn"] = emit_allgather(p, block, io["attn_sl"], io["sh_attn"])

        es_b.close()

        def w_colblock(sync, ob, dest, sem):
            if ob == 0:
                sync.wait_ge(s_cc.h, hold["agattn"])
            srcap = io["sh_attn"].rearrange("(b dt p) j -> b p dt j", b=cores, p=128)[ob]
            ins = sync.dma_start(out=dest[:].rearrange("p (dt j) -> p dt j", j=512),
                                 in_=srcap)
            sem.inc(ins, 16)

        st_last = emit_mm_main(
            p, block, es, xlnT, tr_all, w_colblock, D,
            tag=tag,
            epi=dict(scale=1.0, bias_b=bias_b, act="gelu",
                     xold=io["x_in"], xout=io["x_out"],
                     bias_ready=bias_ready))

        @block.sync
        def _(sync):
            for sem, val in st_last:
                sync.wait_ge(sem.h, val)

    return st_last


def build_step5_nc(cores=8):
    nc = bass.Bass("TRN2", target_bir_lowering=False)
    TSL = D // cores
    io = {}
    io["x_in"] = nc.dram_tensor("x_in", [TOK, D], F32, kind="ExternalInput").ap()
    io["x_out"] = nc.dram_tensor("x_out", [TOK, D], F32, kind="ExternalOutput").ap()
    io["w1sl"] = nc.dram_tensor("w1sl", [TSL, D], F32, kind="ExternalInput").ap()
    io["w2Tsl"] = nc.dram_tensor("w2Tsl", [D, TSL], F32, kind="ExternalInput").ap()
    io["mask_sl"] = nc.dram_tensor("mask_sl", [D, TSL], F32, kind="ExternalInput").ap()
    io["b2"] = nc.dram_tensor("b2", [1, D], F32, kind="ExternalInput").ap()

    io["w1sl_bf"] = nc.dram_tensor("w1sl_bf", [TSL, D], BF16).ap()
    io["w2Tsl_bf"] = nc.dram_tensor("w2Tsl_bf", [D, TSL], BF16).ap()
    io["attn_sl"] = nc.dram_tensor("attn_sl", [D, TSL], BF16).ap()
    io["xln_tm"] = nc.dram_tensor("xln_tm", [TOK, D], BF16).ap()
    io["sh_w1"] = nc.dram_tensor("sh_w1", [D, D], BF16, addr_space="Shared").ap()
    io["sh_attn"] = nc.dram_tensor("sh_attn", [cores * D, TSL], BF16,
                                   addr_space="Shared").ap()

    with ExitStack() as es:
        p = P(nc, es, cores)
        block = es.enter_context(nc.Block())
        emit_step5(p, block, io)
    return nc


def step5_in_maps(inputs, x_full, cores=8):
    w1 = np.asarray(inputs["L1_attn_w1"], np.float32)
    w2 = np.asarray(inputs["L1_attn_w2"], np.float32)
    mask = np.asarray(inputs["L1_attn_sparse_mask"], np.float32)
    w2T = np.ascontiguousarray(w2.T)
    TSL = D // cores
    maps = []
    for k in range(cores):
        maps.append(dict(
            x_in=np.ascontiguousarray(x_full[k * TOK:(k + 1) * TOK]),
            w1sl=np.ascontiguousarray(w1[k * TSL:(k + 1) * TSL]),
            w2Tsl=np.ascontiguousarray(w2T[:, k * TSL:(k + 1) * TSL]),
            mask_sl=np.ascontiguousarray(mask[:, k * TSL:(k + 1) * TSL]),
            b2=np.ascontiguousarray(
                np.asarray(inputs["L1_attn_b2"], np.float32).reshape(1, D)),
        ))
    return maps




def emit_step4(p, block, io, tag="s4", prev_done=None, share=None):
    nc = p.nc
    cores = p.cores
    NH = H_HEADS

    s_gp = p.sem("s_gp")
    s_gpw = p.sem("s_gpw")
    s_gpl = p.sem("s_gpl")
    s_gpp = p.sem("s_gpp")
    s_mm = p.sem("s_mm")
    s_v = p.sem("s_v")
    s_s = p.sem("s_s")
    s_cc = p.sem("s_cc")
    s_trd = p.sem("s_trd")
    s_gx = [p.sem("s_gx0"), p.sem("s_gx1")]
    s_lwu = [p.sem("s_lwu0"), p.sem("s_lwu1")]
    s_gs = [p.sem("s_gs0"), p.sem("s_gs1")]
    s_dl = [p.sem("s_dl0"), p.sem("s_dl1"), p.sem("s_dl2"), p.sem("s_dl3")]
    s_axo = [p.sem("s_axo0"), p.sem("s_axo1")]
    s_ast = [p.sem("s_ast0"), p.sem("s_ast1")]
    s_gpc = p.sem("s_gpc")

    es = ExitStack()
    with es:
        xlnT = p.sb(f"{tag}_xlnT", [128, NDT * TOK], BF16, es)
        sperm = p.sb(f"{tag}_perm", [128, 32], I32, es)
        ident = p.sb(f"{tag}_ident", [128, 128], F32, es)
        xg = [[p.sb(f"{tag}_xg{i}{j}", [128, TOK], BF16, es) for j in range(2)]
              for i in range(2)]
        ub = [p.sb(f"{tag}_ub{i}", [128, 2 * H_M], BF16, es) for i in range(2)]
        db = [p.sb(f"{tag}_db{i}", [128, 8 * H_DH], BF16, es) for i in range(2)]
        ug = [p.sb(f"{tag}_ug{i}", [128, 8 * TOK], BF16, es) for i in range(2)]
        dnT = [[p.sb(f"{tag}_dn{i}{j}", [128, TOK], F32, es) for j in range(2)]
               for i in range(2)]
        ps_u = [p.psum(f"{tag}_psu{i}", [128, 512], F32, es=es) for i in range(2)]
        ps_d = [p.psum(f"{tag}_psd{i}", [128, 512], F32, es=es) for i in range(2)]
        ps_t = [p.psum(f"{tag}_pst{i}", [128, 128], F32, es=es) for i in range(2)]
        hold = {}

        @block.vector
        def _(v):
            ins = v.memset(ident[:], 1.0)
            hold["id1"] = s_v.inc(ins, 1)

        if share is not None and "agd" in share:
            @block.gpsimd
            def _(gp):
                gp.wait_ge(s_v.h, hold["id1"])
                ins = gp.affine_select(ident[:], ident[:], [[-1, 128]],
                                       ALU.is_equal, 0.0, base=0,
                                       channel_multiplier=1)
                hold["ident"] = s_gpc.inc(ins, 1)
                if prev_done:
                    for _s, _val in prev_done:
                        gp.wait_ge(_s.h, _val)
                ins = gp.dma_start(out=sperm[:], in_=io["perm_t"])
                hold["perm"] = s_gpp.inc(ins, 16)
            hold["agu"] = share["agu"]
            hold["agd"] = share["agd"]
        else:
            @block.gpsimd
            def _(gp):
                ins = gp.dma_start(out=io["upsl_bf"], in_=io["upsl"])
                hold["upcast"] = s_gpw.inc(ins, 16)
                ins = gp.dma_start(out=io["dnsl_bf"], in_=io["dnsl"])
                hold["dncast"] = s_gpl.inc(ins, 16)
                gp.wait_ge(s_v.h, hold["id1"])
                ins = gp.affine_select(ident[:], ident[:], [[-1, 128]],
                                       ALU.is_equal, 0.0, base=0,
                                       channel_multiplier=1)
                hold["ident"] = s_gpc.inc(ins, 1)
                if prev_done:
                    for _s, _val in prev_done:
                        gp.wait_ge(_s.h, _val)
                ins = gp.dma_start(out=sperm[:], in_=io["perm_t"])
                hold["perm"] = s_gpp.inc(ins, 16)

            hold["agu"] = emit_allgather(p, block, io["upsl_bf"], io["sh_up"],
                                         wait_gp=(s_gpw, hold["upcast"]))
            hold["agd"] = emit_allgather(p, block, io["dnsl_bf"], io["sh_dn"],
                                         wait_gp=(s_gpl, hold["dncast"]))
            if share is not None:
                share["agu"] = hold["agu"]
                share["agd"] = hold["agd"]

        with ExitStack() as es_ln:
            tr_all, _ = emit_ln(p, block, es_ln, io["x_in"], io["xln_tm"], xlnT,
                                f"{tag}ln", fm_dram=io["xln_fm"], x_ready=prev_done)

        mm0, v0, s0 = s_mm.v, s_v.v, s_s.v
        gx0 = [s.v for s in s_gx]
        lwu0 = [s.v for s in s_lwu]
        gs0 = [s.v for s in s_gs]
        up_p = {}
        dn_p = {}
        gelu_p = {}
        dnv_p = {}
        for h in range(NH):
            for k in range(16):
                up_p[(h, k)] = mm0 + 20 * h + k + 1
            for j in range(4):
                dn_p[(h, j)] = mm0 + 20 * h + 16 + j + 1
            for k in range(16):
                gelu_p[(h, k)] = s0 + 16 * h + k + 1
            for j in range(4):
                dnv_p[(h, j)] = v0 + 4 * h + j + 1
        gx_p = [gx0[h % 2] + 32 * (h // 2 + 1) for h in range(NH)]
        lwu_p = [lwu0[h % 2] + 32 * (h // 2 + 1) for h in range(NH)]
        gs_p = [gs0[h % 2] + 32 * (h // 2 + 1) for h in range(NH)]

        @block.gpsimd
        def _(gp):
            gp.wait_ge(s_gpp.h, hold["perm"])
            gp.wait_ge(s_trd.h, tr_all)
            for h in range(NH):
                if h >= 2:
                    gp.wait_ge(s_mm.h, up_p[(h - 2, 15)])
                for i in range(2):
                    ins = gp.indirect_dma_start(
                        xg[h % 2][i][:], None, io["xln_fm"],
                        IndirectOffsetOnAxis(sperm[:, 2 * h + i:2 * h + i + 1], 0))
                    s_gx[h % 2].inc(ins, 16)
                assert s_gx[h % 2].v == gx_p[h]
                if h >= 1:
                    hh = h - 1
                    gp.wait_ge(s_v.h, dnv_p[(hh, 3)])
                    for i in range(2):
                        ins = gp.indirect_dma_start(
                            io["deltaT"],
                            IndirectOffsetOnAxis(sperm[:, 2 * hh + i:2 * hh + i + 1], 0),
                            dnT[hh % 2][i][:], None)
                        s_gs[hh % 2].inc(ins, 16)
                    assert s_gs[hh % 2].v == gs_p[hh]
            hh = NH - 1
            gp.wait_ge(s_v.h, dnv_p[(hh, 3)])
            for i in range(2):
                ins = gp.indirect_dma_start(
                    io["deltaT"],
                    IndirectOffsetOnAxis(sperm[:, 2 * hh + i:2 * hh + i + 1], 0),
                    dnT[hh % 2][i][:], None)
                s_gs[hh % 2].inc(ins, 16)
            assert s_gs[hh % 2].v == gs_p[hh]

        @block.sync
        def _(sync):
            sync.wait_ge(s_cc.h, hold["agd"])
            for h in range(NH):
                if h >= 2:
                    sync.wait_ge(s_mm.h, dn_p[(h - 2, 3)])
                ins = sync.dma_start(
                    out=ub[h % 2][:].rearrange("p (dt m) -> p dt m", m=H_M),
                    in_=io["sh_up"].rearrange("(hh dt p) m -> hh p dt m", hh=NH, p=128)[h])
                s_lwu[h % 2].inc(ins, 16)
                ins = sync.dma_start(
                    out=db[h % 2][:].rearrange("p (mt d) -> p mt d", d=H_DH),
                    in_=io["sh_dn"].rearrange("(hh mt p) d -> hh p mt d", hh=NH, p=128)[h])
                s_lwu[h % 2].inc(ins, 16)
                assert s_lwu[h % 2].v == lwu_p[h]

        @block.tensor
        def _(t):
            for h in range(NH):
                t.wait_ge(s_gx[h % 2].h, gx_p[h])
                t.wait_ge(s_lwu[h % 2].h, lwu_p[h])
                for mt in range(8):
                    for th in range(2):
                        k = mt * 2 + th
                        if 20 * h + k + 1 > 2:
                            pass
                        if k >= 2:
                            t.wait_ge(s_s.h, gelu_p[(h, k - 2)])
                        elif h >= 1:
                            t.wait_ge(s_s.h, gelu_p[(h - 1, 14 + k)])
                        for dt in range(2):
                            ins = t.matmul(
                                ps_u[k % 2][:],
                                ub[h % 2][:, dt * H_M + mt * 128: dt * H_M + (mt + 1) * 128],
                                xg[h % 2][dt][:, th * 512:(th + 1) * 512],
                                start=(dt == 0), stop=(dt == 1))
                        assert s_mm.inc(ins, 1) == up_p[(h, k)]
                t.wait_ge(s_s.h, gelu_p[(h, 15)])
                for dtile in range(2):
                    for th in range(2):
                        j = dtile * 2 + th
                        if j >= 2:
                            t.wait_ge(s_v.h, dnv_p[(h, j - 2)])
                        elif h >= 1:
                            t.wait_ge(s_v.h, dnv_p[(h - 1, 2 + j)])
                        for mt in range(8):
                            ins = t.matmul(
                                ps_d[j % 2][:],
                                db[h % 2][:, mt * H_DH + dtile * 128: mt * H_DH + (dtile + 1) * 128],
                                ug[h % 2][:, mt * TOK + th * 512: mt * TOK + (th + 1) * 512],
                                start=(mt == 0), stop=(mt == 7))
                        assert s_mm.inc(ins, 1) == dn_p[(h, j)]

        @block.scalar
        def _(s):
            for h in range(NH):
                for mt in range(8):
                    for th in range(2):
                        k = mt * 2 + th
                        s.wait_ge(s_mm.h, up_p[(h, k)])
                        if k == 0 and h >= 2:
                            s.wait_ge(s_mm.h, dn_p[(h - 2, 3)])
                        ins = s.activation(
                            ug[h % 2][:, mt * TOK + th * 512: mt * TOK + (th + 1) * 512],
                            ps_u[k % 2][:], AF.Gelu)
                        assert s_s.inc(ins, 1) == gelu_p[(h, k)]

        @block.vector
        def _(v):
            for h in range(NH):
                for dtile in range(2):
                    for th in range(2):
                        j = dtile * 2 + th
                        v.wait_ge(s_mm.h, dn_p[(h, j)])
                        if j == 0 and h >= 2:
                            v.wait_ge(s_gs[h % 2].h, gs_p[h - 2])
                        ins = v.tensor_scalar(
                            dnT[h % 2][dtile][:, th * 512:(th + 1) * 512],
                            ps_d[j % 2][:], 0.5, None, ALU.mult)
                        assert s_v.inc(ins, 1) == dnv_p[(h, j)]

        TBT = 2
        NTB = NTT // TBT
        dld = [p.sb(f"{tag}_dld{i}", [128, TBT * 128], F32, es) for i in range(4)]
        xo_t = [p.sb(f"{tag}_xot{i}", [128, D], F32, es) for i in range(2)]

        mmA0, vA0 = s_mm.v, s_v.v
        dl0 = [s.v for s in s_dl]
        axo0 = [s.v for s in s_axo]
        ast0 = [s.v for s in s_ast]
        trp = {}
        vap = {}
        for tb in range(NTB):
            for vt in range(NDT):
                for i in range(TBT):
                    idx = tb * NDT * TBT + vt * TBT + i
                    trp[(tb, vt, i)] = mmA0 + idx + 1
                    vap[(tb, vt, i)] = vA0 + idx + 1
        dl_p = {}
        for tb in range(NTB):
            for vt in range(NDT):
                dl_p[(tb, vt)] = dl0[vt % 4] + 16 * (tb * (NDT // 4) + vt // 4 + 1)
        axo_p = {}
        ast_p = {}
        for tb in range(NTB):
            for i in range(TBT):
                axo_p[(tb, i)] = axo0[i] + 16 * (tb + 1)
                ast_p[(tb, i)] = ast0[i] + 16 * (tb + 1)

        @block.sync
        def _(sync):
            sync.wait_ge(s_gs[0].h, gs_p[NH - 2])
            sync.wait_ge(s_gs[1].h, gs_p[NH - 1])
            for tb in range(NTB):
                for vt in range(0, NDT, 2):
                    idx4 = tb * NDT + vt
                    if idx4 >= 4:
                        ptb, pv = divmod(idx4 - 4, NDT)
                        sync.wait_ge(s_mm.h, trp[(ptb, pv, TBT - 1)])
                    ins = sync.dma_start(
                        out=dld[vt % 4][:],
                        in_=io["deltaT"][vt * 128:(vt + 1) * 128,
                                         tb * TBT * 128:(tb + 1) * TBT * 128])
                    assert s_dl[vt % 4].inc(ins, 16) == dl_p[(tb, vt)]
                for i in range(TBT):
                    sync.wait_ge(s_v.h, vap[(tb, NDT - 1, i)])
                    tt = tb * TBT + i
                    ins = sync.dma_start(
                        out=io["x_out"][tt * 128:(tt + 1) * 128, :],
                        in_=xo_t[i][:])
                    assert s_ast[i].inc(ins, 16) == ast_p[(tb, i)]

        @block.scalar
        def _(s):
            s.wait_ge(s_gs[0].h, gs_p[NH - 2])
            s.wait_ge(s_gs[1].h, gs_p[NH - 1])
            for tb in range(NTB):
                for i in range(TBT):
                    if tb >= 1:
                        s.wait_ge(s_ast[i].h, ast_p[(tb - 1, i)])
                    tt = tb * TBT + i
                    ins = s.dma_start(out=xo_t[i][:],
                                      in_=io["x_in"][tt * 128:(tt + 1) * 128, :])
                    assert s_axo[i].inc(ins, 16) == axo_p[(tb, i)]
                for vt in range(1, NDT, 2):
                    idx4 = tb * NDT + vt
                    if idx4 >= 4:
                        ptb, pv = divmod(idx4 - 4, NDT)
                        s.wait_ge(s_mm.h, trp[(ptb, pv, TBT - 1)])
                    ins = s.dma_start(
                        out=dld[vt % 4][:],
                        in_=io["deltaT"][vt * 128:(vt + 1) * 128,
                                         tb * TBT * 128:(tb + 1) * TBT * 128])
                    assert s_dl[vt % 4].inc(ins, 16) == dl_p[(tb, vt)]

        @block.tensor
        def _(t):
            t.wait_ge(s_gpc.h, hold["ident"])
            for tb in range(NTB):
                for vt in range(NDT):
                    t.wait_ge(s_dl[vt % 4].h, dl_p[(tb, vt)])
                    for i in range(TBT):
                        idx = tb * NDT * TBT + vt * TBT + i
                        if idx >= 2:
                            pidx = idx - 2
                            ptb, r = divmod(pidx, NDT * TBT)
                            pv, pi = divmod(r, TBT)
                            t.wait_ge(s_v.h, vap[(ptb, pv, pi)])
                        ins = t.transpose(ps_t[idx % 2][:],
                                          dld[vt % 4][:, i * 128:(i + 1) * 128],
                                          ident[:])
                        assert s_mm.inc(ins, 1) == trp[(tb, vt, i)]

        @block.vector
        def _(v):
            for tb in range(NTB):
                for vt in range(NDT):
                    for i in range(TBT):
                        idx = tb * NDT * TBT + vt * TBT + i
                        v.wait_ge(s_mm.h, trp[(tb, vt, i)])
                        if vt == 0:
                            v.wait_ge(s_axo[i].h, axo_p[(tb, i)])
                        ins = v.tensor_tensor(
                            xo_t[i][:, vt * 128:(vt + 1) * 128],
                            ps_t[idx % 2][:],
                            xo_t[i][:, vt * 128:(vt + 1) * 128], ALU.add)
                        assert s_v.inc(ins, 1) == vap[(tb, vt, i)]

        @block.sync
        def _(sync):
            sync.wait_ge(s_ast[0].h, ast_p[(NTB - 1, 0)])
            sync.wait_ge(s_ast[1].h, ast_p[(NTB - 1, 1)])

    return [(s_ast[0], ast_p[(NTB - 1, 0)]), (s_ast[1], ast_p[(NTB - 1, 1)])]


def build_step4_nc(cores=8):
    nc = bass.Bass("TRN2", target_bir_lowering=False)
    HSL = H_HEADS // cores
    io = {}
    io["x_in"] = nc.dram_tensor("x_in", [TOK, D], F32, kind="ExternalInput").ap()
    io["x_out"] = nc.dram_tensor("x_out", [TOK, D], F32, kind="ExternalOutput").ap()
    io["upsl"] = nc.dram_tensor("upsl", [HSL * H_DH, H_M], F32, kind="ExternalInput").ap()
    io["dnsl"] = nc.dram_tensor("dnsl", [HSL * H_M, H_DH], F32, kind="ExternalInput").ap()
    io["perm_t"] = nc.dram_tensor("perm_t", [128, 32], I32, kind="ExternalInput").ap()

    io["upsl_bf"] = nc.dram_tensor("upsl_bf", [HSL * H_DH, H_M], BF16).ap()
    io["dnsl_bf"] = nc.dram_tensor("dnsl_bf", [HSL * H_M, H_DH], BF16).ap()
    io["xln_tm"] = nc.dram_tensor("xln_tm", [TOK, D], BF16).ap()
    io["xln_fm"] = nc.dram_tensor("xln_fm", [D, TOK], BF16).ap()
    io["deltaT"] = nc.dram_tensor("deltaT", [D, TOK], F32).ap()
    io["sh_up"] = nc.dram_tensor("sh_up", [H_HEADS * H_DH, H_M], BF16,
                                 addr_space="Shared").ap()
    io["sh_dn"] = nc.dram_tensor("sh_dn", [H_HEADS * H_M, H_DH], BF16,
                                 addr_space="Shared").ap()

    with ExitStack() as es:
        p = P(nc, es, cores)
        block = es.enter_context(nc.Block())
        emit_step4(p, block, io)
    return nc


def step4_in_maps(inputs, x_full, cores=8):
    up = np.asarray(inputs["L1_ffn_up"], np.float32)
    dn = np.asarray(inputs["L1_ffn_down"], np.float32)
    perm = np.asarray(inputs["L1_ffn_perm"], np.int32)
    perm_t = np.ascontiguousarray(perm.reshape(32, 128).T.astype(np.int32))
    HSL = H_HEADS // cores
    maps = []
    for k in range(cores):
        maps.append(dict(
            x_in=np.ascontiguousarray(x_full[k * TOK:(k + 1) * TOK]),
            upsl=np.ascontiguousarray(
                up[k * HSL:(k + 1) * HSL].reshape(HSL * H_DH, H_M)),
            dnsl=np.ascontiguousarray(
                dn[k * HSL:(k + 1) * HSL].reshape(HSL * H_M, H_DH)),
            perm_t=perm_t,
        ))
    return maps



def emit_step2(p, block, io, tag="s2", prev_done=None):
    nc = p.nc
    cores = p.cores
    AD = A_DIM
    NCT = AD // 128
    NFT = A_DFF // 128

    s_gpw = p.sem("s_gpw")
    s_gpl = p.sem("s_gpl")
    s_gpt = p.sem("s_gpt")
    s_gpz = p.sem("s_gpz")
    s_gpc = p.sem("s_gpc")
    s_mm = p.sem("s_mm")
    s_v = p.sem("s_v")
    s_s = p.sem("s_s")
    s_cc = p.sem("s_cc")
    s_trd = p.sem("s_trd")
    s_lwa = p.sem("s_lwa")
    s_lw2 = p.sem("s_lw2")
    s_ytm = [p.sem("s_ytm0"), p.sem("s_ytm1")]
    s_ytr = p.sem("s_ytr")
    s_tab = p.sem("s_tab")
    s_el = p.sem("s_el")
    s_sml = p.sem("s_sml")

    def vedge(v, ins):
        v.wait_ge(s_v.h, s_v.inc(ins, 1))

    def sedge(s, ins):
        s.wait_ge(s_s.h, s_s.inc(ins, 1))

    es = ExitStack()
    with es:
        h = [p.sb(f"{tag}_h{t}", [128, AD], F32, es) for t in range(NTT)]
        ident = p.sb(f"{tag}_ident", [128, 128], F32, es)
        ones1 = p.sb(f"{tag}_ones1", [1, 128], F32, es)
        onesw = p.sb(f"{tag}_onesw", [128, TOK], F32, es)
        zbre = p.sb(f"{tag}_zbre", [128, 2], F32, es)
        zbim = p.sb(f"{tag}_zbim", [128, 2], F32, es)
        zre_t = p.sb(f"{tag}_zret", [128, 16], F32, es)
        zim_t = p.sb(f"{tag}_zimt", [128, 16], F32, es)
        lc_t = p.sb(f"{tag}_lct", [128, 32], F32, es)
        scl_t = p.sb(f"{tag}_sclt", [128, 16], F32, es)
        b1_t = p.sb(f"{tag}_b1t", [128, 64], F32, es)
        qsel = p.sb(f"{tag}_qsel", [128, 4], F32, es)
        c_t1 = p.sb(f"{tag}_ct1", [128, 16], F32, es)
        c_t2 = p.sb(f"{tag}_ct2", [128, 16], F32, es)
        zabs = p.sb(f"{tag}_zabs", [128, 16], F32, es)
        rmagA = p.sb(f"{tag}_rmagA", [128, 16], F32, es)
        urA = p.sb(f"{tag}_urA", [128, 16], F32, es)
        uiA = p.sb(f"{tag}_uiA", [128, 16], F32, es)
        bin_b = p.sb(f"{tag}_binb", [128, AD], F32, es)
        lnlw_b = p.sb(f"{tag}_lnlwb", [128, AD], F32, es)
        lnlb_b = p.sb(f"{tag}_lnlbb", [128, AD], F32, es)
        vrow = p.sb(f"{tag}_vrow", [1, AD], F32, es)
        hfT = p.sb(f"{tag}_hfT", [128, NCT * TOK], BF16, es)
        yscr = [p.sb(f"{tag}_yscr{i}", [128, AD], F32, es) for i in range(2)]
        y_tm = [p.sb(f"{tag}_ytmb{i}", [128, AD], BF16, es) for i in range(2)]
        ident_bf = p.sb(f"{tag}_idbf", [128, 128], BF16, es)
        yT = [p.sb(f"{tag}_yT{i}", [128, TOK], BF16, es) for i in range(4)]
        stat6 = p.sb(f"{tag}_st6", [128, 6], F32, es)
        agg2 = p.sb(f"{tag}_agg2", [128, 2 * NTT], F32, es)
        vep = p.sb(f"{tag}_vep", [128, NTT], F32, es)
        sdv = p.sb(f"{tag}_sdv", [128, NTT], F32, es)
        rsv = p.sb(f"{tag}_rsv", [128, NTT], F32, es)
        nbv = p.sb(f"{tag}_nbv", [128, NTT], F32, es)
        tscr = p.sb(f"{tag}_tscr", [128, 8], F32, es)
        pb = [p.psum(f"{tag}_bps{i}", [128, 512], F32, es=es) for i in range(2)]
        hold = {}

        @block.vector
        def _(v):
            v.memset(ones1[:], 1.0)
            v.memset(onesw[:], 1.0)
            ins = v.memset(ident[:], 1.0)
            hold["id1"] = s_v.inc(ins, 1)

        @block.gpsimd
        def _(gp):
            ins = gp.dma_start(out=io["WinTsl_bf"], in_=io["WinTsl"])
            hold["wincast"] = s_gpw.inc(ins, 16)
            ins = gp.dma_start(out=io["W1Tsl_bf"], in_=io["W1Tsl"])
            s_gpl.inc(ins, 16)
            ins = gp.dma_start(out=io["W2Tsl_bf"], in_=io["W2Tsl"])
            hold["wcast"] = s_gpl.inc(ins, 16)
            ins = gp.dma_start(out=io["WoutTsl_bf"], in_=io["WoutTsl"])
            hold["woutcast"] = s_gpt.inc(ins, 16)
            gp.wait_ge(s_v.h, hold["id1"])
            ins = gp.affine_select(ident[:], ident[:], [[-1, 128]], ALU.is_equal,
                                   0.0, base=0, channel_multiplier=1)
            hold["ident"] = s_gpc.inc(ins, 1)
            gp.wait_ge(s_gpc.h, hold["ident"])
            ins = gp.tensor_copy(ident_bf[:], ident[:])
            hold["ident_bf"] = s_gpc.inc(ins, 1)
            if prev_done:
                for _s, _val in prev_done:
                    gp.wait_ge(_s.h, _val)
            for name, dst in [("zbre", zbre), ("zbim", zbim), ("zre_t", zre_t),
                              ("zim_t", zim_t), ("lc_t", lc_t), ("scl_t", scl_t),
                              ("b1_t", b1_t), ("qsel", qsel)]:
                ins = gp.dma_start(out=dst[:], in_=io[name])
                s_gpz.inc(ins, 16)
            hold["smalls"] = s_gpz.v

        hold["ag_win"] = emit_allgather(p, block, io["WinTsl_bf"], io["sh_WinT"],
                                        wait_gp=(s_gpw, hold["wincast"]))
        hold["ag_w"] = emit_allgather(p, block, io["W1Tsl_bf"], io["sh_W1T"],
                                      wait_gp=(s_gpl, hold["wcast"]))
        hold["ag_w2"] = emit_allgather(p, block, io["W2Tsl_bf"], io["sh_W2T"])
        hold["ag_wout"] = emit_allgather(p, block, io["WoutTsl_bf"], io["sh_WoutT"],
                                         wait_gp=(s_gpt, hold["woutcast"]))

        @block.vector
        def _(v):
            v.wait_ge(s_gpz.h, hold["smalls"])
            ins = v.tensor_tensor(c_t1[:], zre_t[:], zre_t[:], ALU.mult)
            vedge(v, ins)
            ins = v.tensor_tensor(c_t2[:], zim_t[:], zim_t[:], ALU.mult)
            vedge(v, ins)
            ins = v.tensor_tensor(c_t1[:], c_t1[:], c_t2[:], ALU.add)
            hold["zn2"] = s_v.inc(ins, 1)

        @block.scalar
        def _(s):
            s.wait_ge(s_v.h, hold["zn2"])
            ins = s.activation(zabs[:], c_t1[:], AF.Sqrt)
            sedge(s, ins)
            ins = s.activation(rmagA[:], zabs[:], AF.Tanh)
            hold["rmag"] = s_s.inc(ins, 1)

        @block.vector
        def _(v):
            v.wait_ge(s_s.h, hold["rmag"])
            ins = v.reciprocal(c_t2[:], zabs[:])
            vedge(v, ins)
            ins = v.tensor_tensor(urA[:], zre_t[:], c_t2[:], ALU.mult)
            vedge(v, ins)
            ins = v.tensor_tensor(uiA[:], zim_t[:], c_t2[:], ALU.mult)
            hold["uvec"] = s_v.inc(ins, 1)

        def bcast_row(t_sec, v_sec, src_row, dst, n, scale, idx):
            raise RuntimeError("inline only")

        bc_jobs0 = [(io["bin_r"], bin_b, AD), (io["lnlw_r"], lnlw_b, AD),
                    (io["lnlb_r"], lnlb_b, AD)]
        bc0_mm = [s_mm.v + ji + 1 for ji in range(3)]
        bc0_v = [s_v.v + ji + 1 for ji in range(3)]
        bc0_l = []

        @block.sync
        def _(sync):
            for ji, (src, dst, n) in enumerate(bc_jobs0):
                if ji >= 1:
                    sync.wait_ge(s_mm.h, bc0_mm[ji - 1])
                ins = sync.dma_start(out=vrow[:, 0:n], in_=src)
                bc0_l.append(s_sml.inc(ins, 16))

        @block.tensor
        def _(t):
            t.wait_ge(s_v.h, hold["id1"])
            for ji, (src, dst, n) in enumerate(bc_jobs0):
                t.wait_ge(s_sml.h, bc0_l[ji])
                if ji >= 2:
                    t.wait_ge(s_v.h, bc0_v[ji - 2])
                ins = t.matmul(pb[ji % 2][:], ones1[:, 0:128], vrow[:, 0:n],
                               start=True, stop=True)
                assert s_mm.inc(ins, 1) == bc0_mm[ji]

        @block.vector
        def _(v):
            for ji, (src, dst, n) in enumerate(bc_jobs0):
                v.wait_ge(s_mm.h, bc0_mm[ji])
                ins = v.tensor_copy(dst[:], pb[ji % 2][:])
                assert s_v.inc(ins, 1) == bc0_v[ji]

        es_e = ExitStack()
        xlnT = p.sb(f"{tag}_xlnT", [128, NDT * TOK], BF16, es_e)
        wincb = p.sb(f"{tag}_wincb", [128, NDT * 512], BF16, es_e)
        with ExitStack() as es_ln:
            tr_all, _ = emit_ln(p, block, es_ln, io["x_in"], io["xln_tm"],
                                xlnT, f"{tag}ln", x_ready=prev_done)

        ent_mm_p = [s_mm.v + g + 1 for g in range(NTT)]
        ent_v_p = [s_v.v + g + 1 for g in range(NTT)]
        hold["win_l"] = s_lwa.v + 16

        @block.sync
        def _(sync):
            sync.wait_ge(s_cc.h, hold["ag_win"])
            sync.wait_ge(s_trd.h, tr_all)
            ins = sync.dma_start(
                out=wincb[:].rearrange("p (dt j) -> p dt j", j=512),
                in_=io["sh_WinT"].rearrange("(dt p) j -> p dt j", p=128))
            assert s_lwa.inc(ins, 16) == hold["win_l"]

        ps_a = [p.psum(f"{tag}_psa{i}", [128, 512], F32, es=es) for i in range(2)]
        ps_t = [p.psum(f"{tag}_pst{i}", [128, 128], F32, es=es) for i in range(2)]

        @block.tensor
        def _(t):
            t.wait_ge(s_lwa.h, hold["win_l"])
            for g in range(NTT):
                if g >= 2:
                    t.wait_ge(s_v.h, ent_v_p[g - 2])
                for dt in range(NDT):
                    ins = t.matmul(
                        ps_a[g % 2][:],
                        xlnT[:, dt * TOK + g * 128: dt * TOK + (g + 1) * 128],
                        wincb[:, dt * 512:(dt + 1) * 512],
                        start=(dt == 0), stop=(dt == NDT - 1))
                assert s_mm.inc(ins, 1) == ent_mm_p[g]
        ent_done = ent_mm_p[-1]

        @block.vector
        def _(v):
            v.wait_ge(s_v.h, bc0_v[0])
            for g in range(NTT):
                v.wait_ge(s_mm.h, ent_mm_p[g])
                ins = v.tensor_tensor(h[g][:], ps_a[g % 2][:], bin_b[:], ALU.add)
                assert s_v.inc(ins, 1) == ent_v_p[g]
        hold["h_ready"] = list(ent_v_p)
        es_e.close()

        es_L = ExitStack()
        ps_ln = [p.psum(f"{tag}_psln{i}", [128, 128], BF16, es=es_L)
                 for i in range(2)]
        lnst = {"pse": [0, 0], "ytm": [0, 0]}
        bc_stage = p.sb(f"{tag}_bcst", [128, 5 * AD], F32, es_L)
        Tm_r = p.sb(f"{tag}_Tmr", [128, 4], F32, es_L)
        Tm_i = p.sb(f"{tag}_Tmi", [128, 4], F32, es_L)
        Cr = p.sb(f"{tag}_Cr", [128, 4], F32, es_L)
        Ci = p.sb(f"{tag}_Ci", [128, 4], F32, es_L)
        p1r = p.sb(f"{tag}_p1r", [128, 4], F32, es_L)
        p1i = p.sb(f"{tag}_p1i", [128, 4], F32, es_L)
        pcr = p.sb(f"{tag}_pcr", [128, 4], F32, es_L)
        npci = p.sb(f"{tag}_npci", [128, 4], F32, es_L)
        e_sb = p.sb(f"{tag}_esb", [128, 8], F32, es_L)
        e_all = p.sb(f"{tag}_eall", [128, 32], F32, es_L)
        scanAB = p.sb(f"{tag}_scanAB", [128, 8 * TOK], F32, es_L)
        scw = p.sb(f"{tag}_scw", [128, 2 * TOK], F32, es_L)
        cvw = [p.sb(f"{tag}_cvw{i}", [128, TOK], F32, es_L) for i in range(2)]
        tabs = p.sb(f"{tag}_tabs", [128, 3 * TOK], F32, es_L)
        rconst = p.sb(f"{tag}_rconst", [128, TOK], F32, es_L)
        tb1 = p.sb(f"{tag}_tb1", [128, 512], F32, es_L)
        tb2 = p.sb(f"{tag}_tb2", [128, 512], F32, es_L)
        y3T = p.sb(f"{tag}_y3T", [128, NFT * TOK], BF16, es_L)
        wbuf = p.sb(f"{tag}_wbuf", [128, NCT * A_DFF], BF16, es_L)
        w2b = p.sb(f"{tag}_w2b", [128, 16 * AD], BF16, es_L)
        m1w = scw[:, 0:TOK]
        m2w = scw[:, TOK:2 * TOK]
        tabu = scanAB[:, 0:6 * TOK]

        hold["tb_s0"] = s_s.v
        @block.vector
        def _(v):
            v.wait_ge(s_s.h, hold["rmag"])
            v.wait_ge(s_v.h, hold["uvec"])
            v.wait_ge(s_v.h, ent_v_p[-1])
            v.wait_ge(s_trd.h, tr_all)
            for u in range(2):
                base = u * 3 * TOK
                co = tabu[:, base:base + TOK]
                si = tabu[:, base + TOK:base + 2 * TOK]
                rp = tabu[:, base + 2 * TOK:base + 3 * TOK]
                ins = v.tensor_tensor(tscr[:, 0:1], zbre[:, u:u + 1],
                                      zbre[:, u:u + 1], ALU.mult)
                vedge(v, ins)
                ins = v.tensor_tensor(tscr[:, 1:2], zbim[:, u:u + 1],
                                      zbim[:, u:u + 1], ALU.mult)
                vedge(v, ins)
                ins = v.tensor_tensor(tscr[:, 0:1], tscr[:, 0:1], tscr[:, 1:2],
                                      ALU.add)
                hold[f"tbz_{u}"] = s_v.inc(ins, 1)
                v.wait_ge(s_s.h, hold["tb_s0"] + 2 * (u + 1))
                ins = v.reciprocal(tscr[:, 4:5], tscr[:, 2:3])
                vedge(v, ins)
                ins = v.tensor_tensor(tscr[:, 5:6], zbre[:, u:u + 1],
                                      tscr[:, 4:5], ALU.mult)
                vedge(v, ins)
                ins = v.tensor_tensor(tscr[:, 6:7], zbim[:, u:u + 1],
                                      tscr[:, 4:5], ALU.mult)
                vedge(v, ins)
                ins = v.memset(co[:, 0:1], 1.0)
                vedge(v, ins)
                ins = v.memset(si[:, 0:1], 0.0)
                vedge(v, ins)
                ins = v.memset(rp[:, 0:1], 1.0)
                vedge(v, ins)
                ins = v.tensor_copy(co[:, 1:2], tscr[:, 5:6])
                vedge(v, ins)
                ins = v.tensor_copy(si[:, 1:2], tscr[:, 6:7])
                vedge(v, ins)
                ins = v.tensor_copy(rp[:, 1:2], tscr[:, 3:4])
                vedge(v, ins)
                w = 2
                while w < TOK:
                    ins = v.tensor_tensor(tscr[:, 0:1], co[:, w - 1:w],
                                          co[:, 1:2], ALU.mult)
                    vedge(v, ins)
                    ins = v.tensor_tensor(tscr[:, 1:2], si[:, w - 1:w],
                                          si[:, 1:2], ALU.mult)
                    vedge(v, ins)
                    ins = v.tensor_tensor(co[:, w:w + 1], tscr[:, 0:1],
                                          tscr[:, 1:2], ALU.subtract)
                    vedge(v, ins)
                    ins = v.tensor_tensor(tscr[:, 0:1], si[:, w - 1:w],
                                          co[:, 1:2], ALU.mult)
                    vedge(v, ins)
                    ins = v.tensor_tensor(tscr[:, 1:2], co[:, w - 1:w],
                                          si[:, 1:2], ALU.mult)
                    vedge(v, ins)
                    ins = v.tensor_tensor(si[:, w:w + 1], tscr[:, 0:1],
                                          tscr[:, 1:2], ALU.add)
                    vedge(v, ins)
                    ins = v.tensor_tensor(rp[:, w:w + 1], rp[:, w - 1:w],
                                          rp[:, 1:2], ALU.mult)
                    vedge(v, ins)
                    nb = w - 1
                    if nb > 0:
                        ins = v.tensor_scalar(tb1[:, 0:nb], co[:, 1:w],
                                              co[:, w:w + 1], None, ALU.mult)
                        vedge(v, ins)
                        ins = v.tensor_scalar(tb2[:, 0:nb], si[:, 1:w],
                                              si[:, w:w + 1], None, ALU.mult)
                        vedge(v, ins)
                        ins = v.tensor_tensor(co[:, w + 1:2 * w], tb1[:, 0:nb],
                                              tb2[:, 0:nb], ALU.subtract)
                        vedge(v, ins)
                        ins = v.tensor_scalar(tb1[:, 0:nb], si[:, 1:w],
                                              co[:, w:w + 1], None, ALU.mult)
                        vedge(v, ins)
                        ins = v.tensor_scalar(tb2[:, 0:nb], co[:, 1:w],
                                              si[:, w:w + 1], None, ALU.mult)
                        vedge(v, ins)
                        ins = v.tensor_tensor(si[:, w + 1:2 * w], tb1[:, 0:nb],
                                              tb2[:, 0:nb], ALU.add)
                        vedge(v, ins)
                        ins = v.tensor_scalar(rp[:, w + 1:2 * w], rp[:, 1:w],
                                              rp[:, w:w + 1], None, ALU.mult)
                        vedge(v, ins)
                    w *= 2
            ins = v.tensor_copy(tscr[:, 7:8], tscr[:, 6:7])
            hold["tab_done"] = s_v.inc(ins, 1)

        @block.scalar
        def _(s):
            assert s_s.v == hold["tb_s0"]
            for u in range(2):
                s.wait_ge(s_v.h, hold[f"tbz_{u}"])
                ins = s.activation(tscr[:, 2:3], tscr[:, 0:1], AF.Sqrt)
                sedge(s, ins)
                ins = s.activation(tscr[:, 3:4], tscr[:, 2:3], AF.Tanh)
                s_s.inc(ins, 1)

        tabw_p = s_el.v + 16

        @block.sync
        def _(sync):
            sync.wait_ge(s_v.h, hold["tab_done"])
            ins = sync.dma_start(
                out=io["tabloc"].rearrange("(ut p) j -> p ut j", p=128),
                in_=tabu[:].rearrange("p (ut j) -> p ut j", j=TOK))
            assert s_el.inc(ins, 16) == tabw_p

        @block.gpsimd
        def _(gp):
            gp.wait_ge(s_el.h, tabw_p)
        hold["ag_tab"] = emit_allgather(p, block, io["tabloc"], io["sh_tab"])
        def inner_ln(wb, bb, do_silu, h_ready):
            v0, s0 = s_v.v, s_s.v
            stats_p = [v0 + 3 * tt + 3 for tt in range(NTT)]
            sqrt_p = s0 + 1
            rs_p = v0 + 3 * NTT + 3
            norm_p = [s0 + 1 + 2 * tt + 1 for tt in range(NTT)]
            act_p = [s0 + 1 + 2 * tt + 2 for tt in range(NTT)]
            aff_p = [rs_p + 2 * tt + 4 * max(tt - 1, 0) + 2 for tt in range(NTT)]
            mm0t = s_mm.v
            tr_p = {(tt, ct): mm0t + NCT * tt + ct + 1
                    for tt in range(NTT) for ct in range(NCT)}
            ev_p = {}
            for e in range(NTT):
                base = aff_p[e + 1] if e < NTT - 1 else aff_p[NTT - 1] + 4
                for ct in range(NCT):
                    ev_p[(e, ct)] = base + ct + 1
            ev_flat = [ev_p[(i // NCT, i % NCT)] for i in range(NTT * NCT)]

            @block.vector
            def _(v):
                for tt in range(NTT):
                    v.wait_ge(s_v.h, h_ready[tt])
                    ins = v.bn_stats(stat6[:], h[tt][:])
                    vedge(v, ins)
                    ins = v.bn_aggr(agg2[:, 2 * tt:2 * tt + 2], stat6[:])
                    vedge(v, ins)
                    ins = v.tensor_scalar(vep[:, tt:tt + 1],
                                          agg2[:, 2 * tt + 1:2 * tt + 2],
                                          EPS, None, ALU.add)
                    assert s_v.inc(ins, 1) == stats_p[tt]
                v.wait_ge(s_s.h, sqrt_p)
                ins = v.reciprocal(rsv[:], sdv[:])
                vedge(v, ins)
                means = agg2[:].rearrange("p (t two) -> p two t", two=2)[:, 0, :]
                ins = v.tensor_tensor(nbv[:], means, rsv[:], ALU.mult)
                vedge(v, ins)
                ins = v.tensor_scalar(nbv[:], nbv[:], -1.0, None, ALU.mult)
                assert s_v.inc(ins, 1) == rs_p
                def evac_group(v, e):
                    for ct in range(NCT):
                        v.wait_ge(s_mm.h, tr_p[(e, ct)])
                        ins = v.tensor_copy(yT[ct][:, e * 128:(e + 1) * 128],
                                            ps_ln[(NCT * e + ct) % 2][:])
                        assert s_v.inc(ins, 1) == ev_p[(e, ct)]

                for tt in range(NTT):
                    v.wait_ge(s_s.h, norm_p[tt])
                    ins = v.tensor_tensor(yscr[tt % 2][:], yscr[tt % 2][:],
                                          wb, ALU.mult)
                    vedge(v, ins)
                    ins = v.tensor_tensor(yscr[tt % 2][:], yscr[tt % 2][:],
                                          bb, ALU.add)
                    assert s_v.inc(ins, 1) == aff_p[tt]
                    if tt >= 1:
                        evac_group(v, tt - 1)
                evac_group(v, NTT - 1)

            @block.scalar
            def _(s):
                s.wait_ge(s_v.h, stats_p[-1])
                ins = s.activation(sdv[:], vep[:], AF.Sqrt)
                assert s_s.inc(ins, 1) == sqrt_p
                for tt in range(NTT):
                    if tt == 0:
                        s.wait_ge(s_v.h, rs_p)
                        s.wait_ge(s_v.h, h_ready[-1])
                    if tt >= 2:
                        s.wait_ge(s_s.h, act_p[tt - 2])
                        s.wait_ge(s_mm.h, tr_p[(tt - 2, NCT - 1)])
                    else:
                        s.wait_ge(s_mm.h, lnst["ytm"][tt % 2])
                    ins = s.activation(yscr[tt % 2][:], h[tt][:], AF.Identity,
                                       bias=nbv[:, tt:tt + 1],
                                       scale=rsv[:, tt:tt + 1])
                    assert s_s.inc(ins, 1) == norm_p[tt]
                    s.wait_ge(s_v.h, aff_p[tt])
                    ins = s.activation(y_tm[tt % 2][:], yscr[tt % 2][:],
                                       AF.Silu if do_silu else AF.Identity)
                    assert s_s.inc(ins, 1) == act_p[tt]

            @block.tensor
            def _(t):
                t.wait_ge(s_gpc.h, hold["ident_bf"])
                for tt in range(NTT):
                    t.wait_ge(s_s.h, act_p[tt])
                    for ct in range(NCT):
                        idx = NCT * tt + ct
                        if idx >= 2:
                            t.wait_ge(s_v.h, ev_flat[idx - 2])
                        else:
                            t.wait_ge(s_v.h, lnst["pse"][idx % 2])
                        ins = t.transpose(
                            ps_ln[idx % 2][:],
                            y_tm[tt % 2][:, ct * 128:(ct + 1) * 128], ident_bf[:])
                        assert s_mm.inc(ins, 1) == tr_p[(tt, ct)]

            lnst["pse"] = [ev_p[(NTT - 1, NCT - 2)], ev_p[(NTT - 1, NCT - 1)]]
            lnst["ytm"] = [tr_p[(NTT - 2, NCT - 1)], tr_p[(NTT - 1, NCT - 1)]]
            return ev_p[(NTT - 1, NCT - 1)]

        def bcast5(l, rows_done_guard):
            srcs = [io["ln1w"], io["ln1b"], io["ln2w"], io["ln2b"], io["b2r"]]
            mm_c = [s_mm.v + ji + 1 for ji in range(5)]
            v_c = [s_v.v + ji + 1 for ji in range(5)]
            l_c = []

            @block.sync
            def _(sync):
                for ji in range(5):
                    if ji >= 1:
                        sync.wait_ge(s_mm.h, mm_c[ji - 1])
                    else:
                        sync.wait_ge(s_mm.h, vrow_guard[0])
                        if rows_done_guard is not None:
                            sync.wait_ge(s_v.h, rows_done_guard)
                    ins = sync.dma_start(out=vrow[:, 0:AD],
                                         in_=srcs[ji][l:l + 1, :])
                    l_c.append(s_sml.inc(ins, 16))

            @block.tensor
            def _(t):
                for ji in range(5):
                    t.wait_ge(s_sml.h, l_c[ji])
                    if ji >= 2:
                        t.wait_ge(s_v.h, v_c[ji - 2])
                    elif rows_done_guard is not None:
                        t.wait_ge(s_v.h, rows_done_guard)
                    ins = t.matmul(pb[ji % 2][:], ones1[:, 0:128],
                                   vrow[:, 0:AD], start=True, stop=True)
                    assert s_mm.inc(ins, 1) == mm_c[ji]

            @block.vector
            def _(v):
                for ji in range(5):
                    v.wait_ge(s_mm.h, mm_c[ji])
                    ins = v.tensor_copy(bc_stage[:, ji * AD:(ji + 1) * AD],
                                        pb[ji % 2][:])
                    assert s_v.inc(ins, 1) == v_c[ji]

            vrow_guard[0] = mm_c[-1]
            return v_c[-1]

        prev_ffn_done = ent_done
        prev_f1_done = ent_done
        prev_bc_guard = ent_v_p[-1]
        hup2_guard = [ent_v_p[-1]]
        y3T_guard = [ent_done]
        vrow_guard = [max(bc0_mm[-1], ent_done)]
        for l in range(4):
            bc_done = bcast5(l, prev_bc_guard)
            l1w_b = bc_stage[:, 0 * AD:1 * AD]
            l1b_b = bc_stage[:, 1 * AD:2 * AD]
            l2w_b = bc_stage[:, 2 * AD:3 * AD]
            l2b_b = bc_stage[:, 3 * AD:4 * AD]
            b2_b = bc_stage[:, 4 * AD:5 * AD]

            @block.vector
            def _(v):
                v.wait_ge(s_v.h, bc_done)

            ytr_p = inner_ln(l1w_b, l1b_b, True, hold["h_ready"])

            tab0 = s_tab.v
            tabl_p = [tab0 + 16 * (ct + 1) for ct in range(NCT)]
            v0 = s_v.v
            scan_p = {}
            for ct in range(NCT):
                for k in range(11):
                    scan_p[(ct, k)] = v0 + 11 * ct + k + 1
            escan_done = v0 + 44

            @block.sync
            def _(sync):
                for ct in range(NCT):
                    u = l * 4 + ct
                    if ct == 0:
                        sync.wait_ge(s_cc.h, hold["ag_tab"])
                        if l > 0:
                            sync.wait_ge(s_v.h, hold[f"fix_done_{l-1}"])
                        else:
                            sync.wait_ge(s_v.h, hold["tab_done"])
                    else:
                        sync.wait_ge(s_v.h, scan_p[(ct - 1, 10)])
                    src = io["sh_tab"][(u // 2) * 768 + (u % 2) * 384:
                                       (u // 2) * 768 + (u % 2) * 384 + 384, :]
                    ins = sync.dma_start(
                        out=tabs[:].rearrange("p (t j) -> p t j", j=TOK),
                        in_=src.rearrange("(t p) j -> p t j", p=128))
                    assert s_tab.inc(ins, 16) == tabl_p[ct]

            @block.vector
            def _(v):
                for ct in range(NCT):
                    u = l * 4 + ct
                    co = tabs[:, 0:TOK]
                    si = tabs[:, TOK:2 * TOK]
                    A = scanAB[:, (2 * ct) * TOK:(2 * ct + 1) * TOK]
                    B = scanAB[:, (2 * ct + 1) * TOK:(2 * ct + 2) * TOK]
                    v.wait_ge(s_tab.h, tabl_p[ct])
                    v.wait_ge(s_v.h, ytr_p)
                    ins = v.tensor_scalar(rconst[:], onesw[:],
                                          rmagA[:, u:u + 1], None, ALU.mult)
                    assert s_v.inc(ins, 1) == scan_p[(ct, 0)]
                    v.wait_ge(s_v.h, scan_p[(ct, 0)])
                    ins = v.tensor_tensor(scw[:, 0:TOK], yT[ct][:], co, ALU.mult)
                    assert s_v.inc(ins, 1) == scan_p[(ct, 1)]
                    ins = v.tensor_tensor(scw[:, TOK:2 * TOK], yT[ct][:], si,
                                          ALU.mult)
                    v.wait_ge(s_v.h, s_v.inc(ins, 1))
                    ins = v.tensor_tensor_scan(A, rconst[:], scw[:, 0:TOK],
                                               0.0, ALU.mult, ALU.add)
                    assert s_v.inc(ins, 1) == scan_p[(ct, 3)]
                    ins = v.tensor_tensor_scan(B, rconst[:], scw[:, TOK:2 * TOK],
                                               0.0, ALU.mult, ALU.add)
                    v.wait_ge(s_v.h, s_v.inc(ins, 1))
                    ins = v.tensor_tensor(tscr[:, 0:1], co[:, TOK - 1:TOK],
                                          A[:, TOK - 1:TOK], ALU.mult)
                    vedge(v, ins)
                    ins = v.tensor_tensor(tscr[:, 1:2], si[:, TOK - 1:TOK],
                                          B[:, TOK - 1:TOK], ALU.mult)
                    vedge(v, ins)
                    ins = v.tensor_tensor(e_sb[:, 2 * ct:2 * ct + 1],
                                          tscr[:, 0:1], tscr[:, 1:2], ALU.add)
                    vedge(v, ins)
                    ins = v.tensor_tensor(tscr[:, 0:1], si[:, TOK - 1:TOK],
                                          A[:, TOK - 1:TOK], ALU.mult)
                    vedge(v, ins)
                    ins = v.tensor_tensor(tscr[:, 1:2], co[:, TOK - 1:TOK],
                                          B[:, TOK - 1:TOK], ALU.mult)
                    vedge(v, ins)
                    ins = v.tensor_tensor(e_sb[:, 2 * ct + 1:2 * ct + 2],
                                          tscr[:, 0:1], tscr[:, 1:2],
                                          ALU.subtract)
                    assert s_v.inc(ins, 1) == scan_p[(ct, 10)]
            assert s_v.v == escan_done

            el_w = s_el.v + 16

            @block.sync
            def _(sync):
                sync.wait_ge(s_v.h, escan_done)
                ins = sync.dma_start(out=io[f"eloc{l}"], in_=e_sb[:])
                assert s_el.inc(ins, 16) == el_w

            @block.gpsimd
            def _(gp):
                gp.wait_ge(s_el.h, el_w)
                ins = gp.collective_compute(
                    "AllGather", ALU.bypass,
                    replica_groups=[[0, 1, 2, 3], [4, 5, 6, 7]],
                    ins=[io[f"eloc{l}"]], outs=[io[f"egath{l}"]])
                s_cc.inc(ins, 1)
            eg_done = s_cc.v
            el_r = s_el.v + 16

            @block.sync
            def _(sync):
                sync.wait_ge(s_cc.h, eg_done)
                ins = sync.dma_start(
                    out=e_all[:],
                    in_=io[f"egath{l}"].rearrange("(m p) c -> p m c", p=128))
                assert s_el.inc(ins, 16) == el_r

            @block.vector
            def _(v):
                v.wait_ge(s_el.h, el_r)
                ins = v.tensor_tensor(p1r[:], urA[:, 4 * l:4 * l + 4],
                                      rmagA[:, 4 * l:4 * l + 4], ALU.mult)
                vedge(v, ins)
                ins = v.tensor_tensor(p1i[:], uiA[:, 4 * l:4 * l + 4],
                                      rmagA[:, 4 * l:4 * l + 4], ALU.mult)
                vedge(v, ins)
                for _ in range(10):
                    ins = v.tensor_tensor(Tm_r[:], p1r[:], p1r[:], ALU.mult)
                    vedge(v, ins)
                    ins = v.tensor_tensor(Tm_i[:], p1i[:], p1i[:], ALU.mult)
                    vedge(v, ins)
                    ins = v.tensor_tensor(Cr[:], p1r[:], p1i[:], ALU.mult)
                    vedge(v, ins)
                    ins = v.tensor_tensor(p1r[:], Tm_r[:], Tm_i[:], ALU.subtract)
                    vedge(v, ins)
                    ins = v.tensor_tensor(p1i[:], Cr[:], Cr[:], ALU.add)
                    vedge(v, ins)
                lcr = lc_t[:].rearrange("p (l ct two) -> p l ct two", l=4, two=2)
                ins = v.tensor_copy(Tm_r[:], lcr[:, l, :, 0])
                vedge(v, ins)
                ins = v.tensor_copy(Tm_i[:], lcr[:, l, :, 1])
                vedge(v, ins)
                ins = v.memset(Cr[:], 0.0)
                vedge(v, ins)
                ins = v.memset(Ci[:], 0.0)
                vedge(v, ins)
                eview = e_all[:].rearrange("p (m ct two) -> p m ct two",
                                           m=4, two=2)
                for m in range(4):
                    ins = v.tensor_scalar(tscr[:, 0:4], Tm_r[:],
                                          qsel[:, m:m + 1], None, ALU.mult)
                    vedge(v, ins)
                    ins = v.tensor_tensor(Cr[:], Cr[:], tscr[:, 0:4], ALU.add)
                    vedge(v, ins)
                    ins = v.tensor_scalar(tscr[:, 0:4], Tm_i[:],
                                          qsel[:, m:m + 1], None, ALU.mult)
                    vedge(v, ins)
                    ins = v.tensor_tensor(Ci[:], Ci[:], tscr[:, 0:4], ALU.add)
                    vedge(v, ins)
                    if m < 3:
                        ins = v.tensor_tensor(tscr[:, 0:4], p1r[:], Tm_r[:],
                                              ALU.mult)
                        vedge(v, ins)
                        ins = v.tensor_tensor(tscr[:, 4:8], p1i[:], Tm_i[:],
                                              ALU.mult)
                        vedge(v, ins)
                        ins = v.tensor_tensor(tscr[:, 0:4], tscr[:, 0:4],
                                              tscr[:, 4:8], ALU.subtract)
                        vedge(v, ins)
                        ins = v.tensor_tensor(tscr[:, 4:8], p1r[:], Tm_i[:],
                                              ALU.mult)
                        vedge(v, ins)
                        ins = v.tensor_tensor(Tm_i[:], p1i[:], Tm_r[:], ALU.mult)
                        vedge(v, ins)
                        ins = v.tensor_tensor(Tm_i[:], Tm_i[:], tscr[:, 4:8],
                                              ALU.add)
                        vedge(v, ins)
                        ins = v.tensor_tensor(Tm_r[:], tscr[:, 0:4],
                                              eview[:, m, :, 0], ALU.add)
                        vedge(v, ins)
                        ins = v.tensor_tensor(Tm_i[:], Tm_i[:],
                                              eview[:, m, :, 1], ALU.add)
                        vedge(v, ins)
                ins = v.tensor_tensor(tscr[:, 0:4], urA[:, 4 * l:4 * l + 4],
                                      rmagA[:, 4 * l:4 * l + 4], ALU.mult)
                vedge(v, ins)
                ins = v.tensor_tensor(tscr[:, 4:8], uiA[:, 4 * l:4 * l + 4],
                                      rmagA[:, 4 * l:4 * l + 4], ALU.mult)
                vedge(v, ins)
                ins = v.tensor_tensor(pcr[:], tscr[:, 0:4], Cr[:], ALU.mult)
                vedge(v, ins)
                ins = v.tensor_tensor(npci[:], tscr[:, 4:8], Ci[:], ALU.mult)
                vedge(v, ins)
                ins = v.tensor_tensor(pcr[:], pcr[:], npci[:], ALU.subtract)
                vedge(v, ins)
                ins = v.tensor_tensor(npci[:], tscr[:, 0:4], Ci[:], ALU.mult)
                vedge(v, ins)
                ins = v.tensor_tensor(Tm_r[:], tscr[:, 4:8], Cr[:], ALU.mult)
                vedge(v, ins)
                ins = v.tensor_tensor(npci[:], npci[:], Tm_r[:], ALU.add)
                vedge(v, ins)
                ins = v.tensor_scalar(npci[:], npci[:], -1.0, None, ALU.mult)
                hold[f"carry_{l}"] = s_v.inc(ins, 1)

            tabr_p = [s_tab.v + 16 * (ct + 1) for ct in range(NCT)]
            v0f = s_v.v
            mm0c = s_mm.v
            trc_p = {}
            for ct in range(NCT):
                for tt in range(NTT):
                    trc_p[(ct, tt)] = mm0c + NTT * ct + tt + 1
            vorder = []
            for ct in range(NCT):
                vorder.append(("fix", ct))
                if ct >= 1:
                    vorder.append(("upd", ct - 1))
            vorder.append(("upd", NCT - 1))
            vbase = v0f
            fix_p = {}
            hupd_p = {}
            for kind, ct in vorder:
                if kind == "fix":
                    for k in range(6):
                        fix_p[(ct, k)] = vbase + k + 1
                    vbase += 6
                else:
                    for tt in range(NTT):
                        hupd_p[(ct, tt)] = vbase + tt + 1
                    vbase += NTT
            hold[f"fix_done_{l}"] = fix_p[(NCT - 1, 5)]

            @block.sync
            def _(sync):
                for ct in range(NCT):
                    u = l * 4 + ct
                    sync.wait_ge(s_v.h, hold[f"carry_{l}"])
                    if ct >= 1:
                        sync.wait_ge(s_v.h, fix_p[(ct - 1, 3)])
                    src = io["sh_tab"][(u // 2) * 768 + (u % 2) * 384:
                                       (u // 2) * 768 + (u % 2) * 384 + 384, :]
                    ins = sync.dma_start(
                        out=tabs[:].rearrange("p (t j) -> p t j", j=TOK),
                        in_=src.rearrange("(t p) j -> p t j", p=128))
                    assert s_tab.inc(ins, 16) == tabr_p[ct]

            @block.vector
            def _(v):
                for kind, ct in vorder:
                    if kind == "fix":
                        co = tabs[:, 0:TOK]
                        si = tabs[:, TOK:2 * TOK]
                        rp = tabs[:, 2 * TOK:3 * TOK]
                        A = scanAB[:, (2 * ct) * TOK:(2 * ct + 1) * TOK]
                        B = scanAB[:, (2 * ct + 1) * TOK:(2 * ct + 2) * TOK]
                        u = l * 4 + ct
                        v.wait_ge(s_tab.h, tabr_p[ct])
                        ins = v.scalar_tensor_tensor(A, rp, pcr[:, ct:ct + 1],
                                                     A, ALU.mult, ALU.add)
                        assert s_v.inc(ins, 1) == fix_p[(ct, 0)]
                        v.wait_ge(s_v.h, fix_p[(ct, 0)])
                        ins = v.scalar_tensor_tensor(B, rp, npci[:, ct:ct + 1],
                                                     B, ALU.mult, ALU.add)
                        v.wait_ge(s_v.h, s_v.inc(ins, 1))
                        ins = v.tensor_tensor(m1w, co, A, ALU.mult)
                        assert s_v.inc(ins, 1) == fix_p[(ct, 2)]
                        ins = v.tensor_tensor(m2w, si, B, ALU.mult)
                        v.wait_ge(s_v.h, s_v.inc(ins, 1))
                        if ct >= 2:
                            v.wait_ge(s_mm.h, trc_p[(ct - 2, NTT - 1)])
                        ins = v.tensor_tensor(cvw[ct % 2][:], m1w, m2w, ALU.add)
                        vedge(v, ins)
                        ins = v.tensor_scalar(cvw[ct % 2][:], cvw[ct % 2][:],
                                              scl_t[:, u:u + 1], None, ALU.mult)
                        assert s_v.inc(ins, 1) == fix_p[(ct, 5)]
                    else:
                        for tt in range(NTT):
                            v.wait_ge(s_mm.h, trc_p[(ct, tt)])
                            ins = v.tensor_tensor(
                                h[tt][:, ct * 128:(ct + 1) * 128],
                                ps_t[(NTT * ct + tt) % 2][:],
                                h[tt][:, ct * 128:(ct + 1) * 128], ALU.add)
                            assert s_v.inc(ins, 1) == hupd_p[(ct, tt)]

            @block.tensor
            def _(t):
                t.wait_ge(s_gpc.h, hold["ident"])
                for ct in range(NCT):
                    t.wait_ge(s_v.h, fix_p[(ct, 5)])
                    for tt in range(NTT):
                        idx = NTT * ct + tt
                        if idx >= 2:
                            pct, ptt = divmod(idx - 2, NTT)
                            t.wait_ge(s_v.h, hupd_p[(pct, ptt)])
                        ins = t.transpose(ps_t[idx % 2][:],
                                          cvw[ct % 2][:, tt * 128:(tt + 1) * 128],
                                          ident[:])
                        assert s_mm.inc(ins, 1) == trc_p[(ct, tt)]

            h_ready2 = [hupd_p[(NCT - 1, tt)] for tt in range(NTT)]
            hold["h_ready"] = h_ready2
            ytr2_p = inner_ln(l2w_b, l2b_b, False, h_ready2)

            w1l_p = s_lwa.v + 16
            w2l_p = s_lw2.v + 16
            mm0f = s_mm.v
            f1_p = {}
            for ft in range(NFT):
                for th in range(2):
                    f1_p[(ft, th)] = mm0f + 2 * ft + th + 1
            s0f = s_s.v
            sil_p = {}
            for ft in range(NFT):
                for th in range(2):
                    sil_p[(ft, th)] = s0f + 2 * ft + th + 1
            mm0g = mm0f + 32
            f2_p = [mm0g + tt + 1 for tt in range(NTT)]
            v0g = s_v.v
            hup2_p = [v0g + 2 * tt + 2 for tt in range(NTT)]

            @block.sync
            def _(sync):
                sync.wait_ge(s_cc.h, hold["ag_w"])
                sync.wait_ge(s_mm.h, prev_f1_done)
                ins = sync.dma_start(
                    out=wbuf[:].rearrange("p (dt j) -> p dt j", j=A_DFF),
                    in_=io["sh_W1T"][l * AD:(l + 1) * AD, :].rearrange(
                        "(dt pp) m -> pp dt m", pp=128))
                assert s_lwa.inc(ins, 16) == w1l_p
                sync.wait_ge(s_cc.h, hold["ag_w2"])
                sync.wait_ge(s_mm.h, prev_ffn_done)
                ins = sync.dma_start(
                    out=w2b[:].rearrange("p (ft j) -> p ft j", j=AD),
                    in_=io["sh_W2T"][l * A_DFF:(l + 1) * A_DFF, :].rearrange(
                        "(ft pp) m -> pp ft m", pp=128))
                assert s_lw2.inc(ins, 16) == w2l_p

            @block.tensor
            def _(t):
                t.wait_ge(s_lwa.h, w1l_p)
                t.wait_ge(s_v.h, ytr2_p)
                for ft in range(NFT):
                    for th in range(2):
                        k = 2 * ft + th
                        if k >= 2:
                            t.wait_ge(s_s.h, sil_p[((k - 2) // 2, (k - 2) % 2)])
                        else:
                            t.wait_ge(s_v.h, hup2_guard[0])
                        for dt in range(NCT):
                            ins = t.matmul(
                                ps_a[k % 2][:],
                                wbuf[:, dt * A_DFF + ft * 128:
                                     dt * A_DFF + (ft + 1) * 128],
                                yT[dt][:, th * 512:(th + 1) * 512],
                                start=(dt == 0), stop=(dt == NCT - 1))
                        assert s_mm.inc(ins, 1) == f1_p[(ft, th)]
                t.wait_ge(s_lw2.h, w2l_p)
                t.wait_ge(s_s.h, sil_p[(NFT - 1, 1)])
                for tt in range(NTT):
                    if tt >= 2:
                        t.wait_ge(s_v.h, hup2_p[tt - 2])
                    for ft in range(NFT):
                        ins = t.matmul(
                            ps_a[tt % 2][:],
                            y3T[:, ft * TOK + tt * 128: ft * TOK + (tt + 1) * 128],
                            w2b[:, ft * AD:(ft + 1) * AD],
                            start=(ft == 0), stop=(ft == NFT - 1))
                    assert s_mm.inc(ins, 1) == f2_p[tt]
            prev_ffn_done = f2_p[-1]
            prev_f1_done = f1_p[(NFT - 1, 1)]

            @block.scalar
            def _(s):
                for ft in range(NFT):
                    for th in range(2):
                        s.wait_ge(s_mm.h, f1_p[(ft, th)])
                        if ft == 0 and th == 0:
                            s.wait_ge(s_mm.h, y3T_guard[0])
                        ins = s.activation(
                            y3T[:, ft * TOK + th * 512: ft * TOK + (th + 1) * 512],
                            ps_a[(2 * ft + th) % 2][:], AF.Silu,
                            bias=b1_t[:, l * 16 + ft:l * 16 + ft + 1], scale=1.0)
                        assert s_s.inc(ins, 1) == sil_p[(ft, th)]

            @block.vector
            def _(v):
                for tt in range(NTT):
                    v.wait_ge(s_mm.h, f2_p[tt])
                    ins = v.tensor_tensor(yscr[tt % 2][:], ps_a[tt % 2][:],
                                          b2_b, ALU.add)
                    vedge(v, ins)
                    ins = v.tensor_tensor(h[tt][:], h[tt][:], yscr[tt % 2][:],
                                          ALU.add)
                    assert s_v.inc(ins, 1) == hup2_p[tt]
            hold["h_ready"] = list(hup2_p)
            prev_bc_guard = hup2_p[-1]
            hup2_guard = [hup2_p[-1]]
            y3T_guard = [prev_ffn_done]

        ytrf_p = inner_ln(lnlw_b[:], lnlb_b[:], False, hold["h_ready"])
        v0x = s_v.v
        hft_p = [v0x + ct + 1 for ct in range(NCT)]

        @block.vector
        def _(v):
            v.wait_ge(s_v.h, ytrf_p)
            for ct in range(NCT):
                ins = v.tensor_copy(hfT[:, ct * TOK:(ct + 1) * TOK], yT[ct][:])
                assert s_v.inc(ins, 1) == hft_p[ct]

        es_L.close()

        bout_b = p.sb(f"{tag}_boutb", [128, D], F32, es)
        NBC = D // 512
        bo_mm = [s_mm.v + c + 1 for c in range(NBC)]
        bo_v = [s_v.v + c + 1 for c in range(NBC)]
        bo_l = [s_sml.v + 16 * (c + 1) for c in range(NBC)]

        @block.sync
        def _(sync):
            for c in range(NBC):
                if c == 0:
                    sync.wait_ge(s_v.h, hft_p[-1])
                    sync.wait_ge(s_mm.h, vrow_guard[0])
                else:
                    sync.wait_ge(s_mm.h, bo_mm[c - 1])
                ins = sync.dma_start(out=vrow[:, 0:512],
                                     in_=io["bout_r"][0:1, c * 512:(c + 1) * 512])
                assert s_sml.inc(ins, 16) == bo_l[c]

        @block.tensor
        def _(t):
            t.wait_ge(s_v.h, hft_p[-1])
            for c in range(NBC):
                t.wait_ge(s_sml.h, bo_l[c])
                if c >= 2:
                    t.wait_ge(s_v.h, bo_v[c - 2])
                ins = t.matmul(pb[c % 2][:], ones1[:, 0:128], vrow[:, 0:512],
                               start=True, stop=True)
                assert s_mm.inc(ins, 1) == bo_mm[c]

        @block.vector
        def _(v):
            for c in range(NBC):
                v.wait_ge(s_mm.h, bo_mm[c])
                ins = v.tensor_copy(bout_b[:, c * 512:(c + 1) * 512],
                                    pb[c % 2][:])
                assert s_v.inc(ins, 1) == bo_v[c]

        def wout_colblock(sync, ob, dest, sem):
            if ob == 0:
                sync.wait_ge(s_cc.h, hold["ag_wout"])
                sync.wait_ge(s_v.h, hft_p[-1])
            srcap = io["sh_WoutT"].rearrange("(dt pp) o -> pp dt o", pp=128)[:, :, ob * 512:(ob + 1) * 512]
            ins = sync.dma_start(out=dest[:].rearrange("p (dt j) -> p dt j", j=512),
                                 in_=srcap)
            sem.inc(ins, 16)

        st_last = emit_mm_main(
            p, block, es, hfT, s_trd.v, wout_colblock, D,
            tag=f"{tag}o", nk=NCT,
            epi=dict(scale=1.0, bias_b=bout_b, act=None,
                     xold=io["x_in"], xout=io["x_out"],
                     bias_ready=bo_v[-1]))

        @block.sync
        def _(sync):
            for sem, val in st_last:
                sync.wait_ge(sem.h, val)

    return st_last


def build_step2_nc(cores=8):
    nc = bass.Bass("TRN2", target_bir_lowering=False)
    io = {}
    AD = A_DIM
    io["x_in"] = nc.dram_tensor("x_in", [TOK, D], F32, kind="ExternalInput").ap()
    io["x_out"] = nc.dram_tensor("x_out", [TOK, D], F32, kind="ExternalOutput").ap()
    io["WinTsl"] = nc.dram_tensor("WinTsl", [D // cores, AD], F32,
                                  kind="ExternalInput").ap()
    io["W1Tsl"] = nc.dram_tensor("W1Tsl", [4 * (AD // cores), A_DFF], F32,
                                 kind="ExternalInput").ap()
    io["W2Tsl"] = nc.dram_tensor("W2Tsl", [4 * (A_DFF // cores), AD], F32,
                                 kind="ExternalInput").ap()
    io["WoutTsl"] = nc.dram_tensor("WoutTsl", [AD // cores, D], F32,
                                   kind="ExternalInput").ap()
    for name, shp in [("zbre", [128, 2]), ("zbim", [128, 2]),
                      ("zre_t", [128, 16]), ("zim_t", [128, 16]),
                      ("lc_t", [128, 32]), ("scl_t", [128, 16]),
                      ("b1_t", [128, 64]), ("qsel", [128, 4])]:
        io[name] = nc.dram_tensor(name, shp, F32, kind="ExternalInput").ap()
    for name in ["ln1w", "ln1b", "ln2w", "ln2b", "b2r"]:
        io[name] = nc.dram_tensor(name, [4, AD], F32, kind="ExternalInput").ap()
    io["bin_r"] = nc.dram_tensor("bin_r", [1, AD], F32, kind="ExternalInput").ap()
    io["lnlw_r"] = nc.dram_tensor("lnlw_r", [1, AD], F32, kind="ExternalInput").ap()
    io["lnlb_r"] = nc.dram_tensor("lnlb_r", [1, AD], F32, kind="ExternalInput").ap()
    io["bout_r"] = nc.dram_tensor("bout_r", [1, D], F32, kind="ExternalInput").ap()

    io["WinTsl_bf"] = nc.dram_tensor("WinTsl_bf", [D // cores, AD], BF16).ap()
    io["W1Tsl_bf"] = nc.dram_tensor("W1Tsl_bf", [4 * (AD // cores), A_DFF], BF16).ap()
    io["W2Tsl_bf"] = nc.dram_tensor("W2Tsl_bf", [4 * (A_DFF // cores), AD], BF16).ap()
    io["WoutTsl_bf"] = nc.dram_tensor("WoutTsl_bf", [AD // cores, D], BF16).ap()
    io["xln_tm"] = nc.dram_tensor("xln_tm", [TOK, D], BF16).ap()
    io["y_dram"] = nc.dram_tensor("y_dram", [TOK, AD], BF16).ap()
    io["tabloc"] = nc.dram_tensor("tabloc", [768, TOK], F32).ap()
    for l in range(4):
        io[f"eloc{l}"] = nc.dram_tensor(f"eloc{l}", [128, 8], F32).ap()
        io[f"egath{l}"] = nc.dram_tensor(f"egath{l}", [512, 8], F32).ap()
    io["sh_WinT"] = nc.dram_tensor("sh_WinT", [D, AD], BF16, addr_space="Shared").ap()
    io["sh_W1T"] = nc.dram_tensor("sh_W1T", [4 * AD, A_DFF], BF16,
                                  addr_space="Shared").ap()
    io["sh_W2T"] = nc.dram_tensor("sh_W2T", [4 * A_DFF, AD], BF16,
                                  addr_space="Shared").ap()
    io["sh_WoutT"] = nc.dram_tensor("sh_WoutT", [AD, D], BF16,
                                    addr_space="Shared").ap()
    io["sh_tab"] = nc.dram_tensor("sh_tab", [cores * 768, TOK], F32,
                                  addr_space="Shared").ap()

    with ExitStack() as es:
        p = P(nc, es, cores)
        block = es.enter_context(nc.Block())
        emit_step2(p, block, io)
    return nc


def step2_in_maps(inputs, x_full, cores=8):
    AD = A_DIM
    Win = np.asarray(inputs["L0_attn_Win"], np.float32)
    W1 = np.asarray(inputs["L0_attn_W1"], np.float32)
    W2 = np.asarray(inputs["L0_attn_W2"], np.float32)
    Wout = np.asarray(inputs["L0_attn_Wout"], np.float32)
    WinT = np.ascontiguousarray(Win.T)
    W1T = np.ascontiguousarray(W1.transpose(0, 2, 1))
    W2T = np.ascontiguousarray(W2.transpose(0, 2, 1))
    WoutT = np.ascontiguousarray(Wout.T)
    phr = np.asarray(inputs["L0_attn_phz_re"], np.float32)
    phi = np.asarray(inputs["L0_attn_phz_im"], np.float32)
    lcr = np.asarray(inputs["L0_attn_lc_re"], np.float32)
    lci = np.asarray(inputs["L0_attn_lc_im"], np.float32)
    scl = np.asarray(inputs["L0_attn_sc_lin"], np.float32)
    b1 = np.asarray(inputs["L0_attn_b1"], np.float32)
    zre_t = np.ascontiguousarray(phr.reshape(4, 4, 128).transpose(2, 0, 1)
                                 .reshape(128, 16))
    zim_t = np.ascontiguousarray(phi.reshape(4, 4, 128).transpose(2, 0, 1)
                                 .reshape(128, 16))
    lc_t = np.ascontiguousarray(
        np.stack([lcr, lci], -1).reshape(4, 4, 128, 2).transpose(2, 0, 1, 3)
        .reshape(128, 32))
    scl_t = np.ascontiguousarray(scl.reshape(4, 4, 128).transpose(2, 0, 1)
                                 .reshape(128, 16))
    b1_t = np.ascontiguousarray(b1.reshape(4, 16, 128).transpose(2, 0, 1)
                                .reshape(128, 64))
    WSL, FSL, OSL = AD // cores, A_DFF // cores, D // cores
    maps = []
    for k in range(cores):
        q = k % 4
        qs = np.zeros((128, 4), np.float32)
        qs[:, q] = 1.0
        u0 = 2 * k
        maps.append(dict(
            x_in=np.ascontiguousarray(x_full[k * TOK:(k + 1) * TOK]),
            WinTsl=np.ascontiguousarray(WinT[k * OSL:(k + 1) * OSL]),
            W1Tsl=np.ascontiguousarray(
                W1T.reshape(4 * AD, A_DFF)[k * 4 * WSL:(k + 1) * 4 * WSL]),
            W2Tsl=np.ascontiguousarray(
                W2T.reshape(4 * A_DFF, AD)[k * 4 * FSL:(k + 1) * 4 * FSL]),
            WoutTsl=np.ascontiguousarray(WoutT[k * WSL:(k + 1) * WSL]),
            zbre=np.ascontiguousarray(zre_t[:, u0:u0 + 2]),
            zbim=np.ascontiguousarray(zim_t[:, u0:u0 + 2]),
            zre_t=zre_t, zim_t=zim_t, lc_t=lc_t, scl_t=scl_t, b1_t=b1_t,
            qsel=qs,
            ln1w=np.ascontiguousarray(np.asarray(inputs["L0_attn_ln1_w"], np.float32)),
            ln1b=np.ascontiguousarray(np.asarray(inputs["L0_attn_ln1_b"], np.float32)),
            ln2w=np.ascontiguousarray(np.asarray(inputs["L0_attn_ln2_w"], np.float32)),
            ln2b=np.ascontiguousarray(np.asarray(inputs["L0_attn_ln2_b"], np.float32)),
            b2r=np.ascontiguousarray(np.asarray(inputs["L0_attn_b2"], np.float32)),
            bin_r=np.ascontiguousarray(
                np.asarray(inputs["L0_attn_bin"], np.float32).reshape(1, AD)),
            lnlw_r=np.ascontiguousarray(
                np.asarray(inputs["L0_attn_lnl_w"], np.float32).reshape(1, AD)),
            lnlb_r=np.ascontiguousarray(
                np.asarray(inputs["L0_attn_lnl_b"], np.float32).reshape(1, AD)),
            bout_r=np.ascontiguousarray(
                np.asarray(inputs["L0_attn_bout"], np.float32).reshape(1, D)),
        ))
    return maps



def build_step1_nc(cores=8):
    nc = bass.Bass("TRN2", target_bir_lowering=False)
    OSL = D // cores
    io = {}
    io["x_in"] = nc.dram_tensor("x_in", [TOK, D], F32, kind="ExternalInput").ap()
    io["x_out"] = nc.dram_tensor("x_out", [TOK, D], F32, kind="ExternalOutput").ap()
    io["WTsl"] = nc.dram_tensor("WTsl", [OSL, D], F32, kind="ExternalInput").ap()
    io["loraAT"] = nc.dram_tensor("loraAT", [D, 128], F32, kind="ExternalInput").ap()
    io["loraBT"] = nc.dram_tensor("loraBT", [128, D], F32, kind="ExternalInput").ap()
    io["lora_w"] = nc.dram_tensor("lora_w", [128, 1], F32, kind="ExternalInput").ap()
    io["bias"] = nc.dram_tensor("bias", [1, D], F32, kind="ExternalInput").ap()
    io["WTsl_bf"] = nc.dram_tensor("WTsl_bf", [OSL, D], BF16).ap()
    io["AT_bf"] = nc.dram_tensor("AT_bf", [D, 128], BF16).ap()
    io["BT_bf"] = nc.dram_tensor("BT_bf", [128, D], BF16).ap()
    io["xln_tm"] = nc.dram_tensor("xln_tm", [TOK, D], BF16).ap()

    io["sh_WT"] = nc.dram_tensor("sh_WT", [D, D], BF16, addr_space="Shared").ap()

    with ExitStack() as es:
        p = P(nc, es, cores)
        block = es.enter_context(nc.Block())
        emit_step1(p, block, io)
    return nc


def step1_in_maps(inputs, x_full, cores=8):
    W = np.asarray(inputs["L0_ffn_W"], np.float32)
    WT = np.ascontiguousarray(W.T)
    AT = np.ascontiguousarray(
        np.asarray(inputs["L0_ffn_lora_A"], np.float32).reshape(128, D).T)
    BT = np.ascontiguousarray(
        np.asarray(inputs["L0_ffn_lora_B"], np.float32).transpose(0, 2, 1).reshape(128, D))
    OSL = D // cores
    maps = []
    for k in range(cores):
        maps.append(dict(
            x_in=np.ascontiguousarray(x_full[k * TOK:(k + 1) * TOK]),
            WTsl=np.ascontiguousarray(WT[k * OSL:(k + 1) * OSL]),
            loraAT=AT,
            loraBT=BT,
            lora_w=np.ascontiguousarray(np.repeat(
                np.asarray(inputs["L0_ffn_lora_weights"], np.float32), 16).reshape(128, 1)),
            bias=np.ascontiguousarray(
                np.asarray(inputs["L0_ffn_bias"], np.float32).reshape(1, D)),
        ))
    return maps





def _decl_step1_io(nc, pre, cores, x_in, x_out):
    OSL = D // cores
    io = {"x_in": x_in, "x_out": x_out}
    io["WTsl"] = nc.dram_tensor(pre + "WTsl", [OSL, D], F32, kind="ExternalInput").ap()
    io["loraAT"] = nc.dram_tensor(pre + "loraAT", [D, 128], F32, kind="ExternalInput").ap()
    io["loraBT"] = nc.dram_tensor(pre + "loraBT", [128, D], F32, kind="ExternalInput").ap()
    io["lora_w"] = nc.dram_tensor(pre + "lora_w", [128, 1], F32, kind="ExternalInput").ap()
    io["bias"] = nc.dram_tensor(pre + "bias", [1, D], F32, kind="ExternalInput").ap()
    io["WTsl_bf"] = nc.dram_tensor(pre + "WTsl_bf", [OSL, D], BF16).ap()
    io["AT_bf"] = nc.dram_tensor(pre + "AT_bf", [D, 128], BF16).ap()
    io["BT_bf"] = nc.dram_tensor(pre + "BT_bf", [128, D], BF16).ap()
    io["xln_tm"] = nc.dram_tensor(pre + "xln_tm", [TOK, D], BF16).ap()
    io["sh_WT"] = nc.dram_tensor(pre + "sh_WT", [D, D], BF16, addr_space="Shared").ap()
    return io


def _decl_step2_io(nc, pre, cores, x_in, x_out):
    AD = A_DIM
    io = {"x_in": x_in, "x_out": x_out}
    io["WinTsl"] = nc.dram_tensor(pre + "WinTsl", [D // cores, AD], F32, kind="ExternalInput").ap()
    io["W1Tsl"] = nc.dram_tensor(pre + "W1Tsl", [4 * (AD // cores), A_DFF], F32, kind="ExternalInput").ap()
    io["W2Tsl"] = nc.dram_tensor(pre + "W2Tsl", [4 * (A_DFF // cores), AD], F32, kind="ExternalInput").ap()
    io["WoutTsl"] = nc.dram_tensor(pre + "WoutTsl", [AD // cores, D], F32, kind="ExternalInput").ap()
    for name, shp in [("zbre", [128, 2]), ("zbim", [128, 2]),
                      ("zre_t", [128, 16]), ("zim_t", [128, 16]),
                      ("lc_t", [128, 32]), ("scl_t", [128, 16]),
                      ("b1_t", [128, 64]), ("qsel", [128, 4])]:
        io[name] = nc.dram_tensor(pre + name, shp, F32, kind="ExternalInput").ap()
    for name in ["ln1w", "ln1b", "ln2w", "ln2b", "b2r"]:
        io[name] = nc.dram_tensor(pre + name, [4, AD], F32, kind="ExternalInput").ap()
    io["bin_r"] = nc.dram_tensor(pre + "bin_r", [1, AD], F32, kind="ExternalInput").ap()
    io["lnlw_r"] = nc.dram_tensor(pre + "lnlw_r", [1, AD], F32, kind="ExternalInput").ap()
    io["lnlb_r"] = nc.dram_tensor(pre + "lnlb_r", [1, AD], F32, kind="ExternalInput").ap()
    io["bout_r"] = nc.dram_tensor(pre + "bout_r", [1, D], F32, kind="ExternalInput").ap()
    io["WinTsl_bf"] = nc.dram_tensor(pre + "WinTsl_bf", [D // cores, AD], BF16).ap()
    io["W1Tsl_bf"] = nc.dram_tensor(pre + "W1Tsl_bf", [4 * (AD // cores), A_DFF], BF16).ap()
    io["W2Tsl_bf"] = nc.dram_tensor(pre + "W2Tsl_bf", [4 * (A_DFF // cores), AD], BF16).ap()
    io["WoutTsl_bf"] = nc.dram_tensor(pre + "WoutTsl_bf", [AD // cores, D], BF16).ap()
    io["xln_tm"] = nc.dram_tensor(pre + "xln_tm", [TOK, D], BF16).ap()
    io["y_dram"] = nc.dram_tensor(pre + "y_dram", [TOK, AD], BF16).ap()
    io["tabloc"] = nc.dram_tensor(pre + "tabloc", [768, TOK], F32).ap()
    for l in range(4):
        io[f"eloc{l}"] = nc.dram_tensor(pre + f"eloc{l}", [128, 8], F32).ap()
        io[f"egath{l}"] = nc.dram_tensor(pre + f"egath{l}", [512, 8], F32).ap()
    io["sh_WinT"] = nc.dram_tensor(pre + "sh_WinT", [D, AD], BF16, addr_space="Shared").ap()
    io["sh_W1T"] = nc.dram_tensor(pre + "sh_W1T", [4 * AD, A_DFF], BF16, addr_space="Shared").ap()
    io["sh_W2T"] = nc.dram_tensor(pre + "sh_W2T", [4 * A_DFF, AD], BF16, addr_space="Shared").ap()
    io["sh_WoutT"] = nc.dram_tensor(pre + "sh_WoutT", [AD, D], BF16, addr_space="Shared").ap()
    io["sh_tab"] = nc.dram_tensor(pre + "sh_tab", [cores * 768, TOK], F32, addr_space="Shared").ap()
    return io


def _decl_step4_io(nc, pre, cores, x_in, x_out, weights=None):
    HSL = H_HEADS // cores
    io = {"x_in": x_in, "x_out": x_out}
    if weights is None:
        io["upsl"] = nc.dram_tensor(pre + "upsl", [HSL * H_DH, H_M], F32, kind="ExternalInput").ap()
        io["dnsl"] = nc.dram_tensor(pre + "dnsl", [HSL * H_M, H_DH], F32, kind="ExternalInput").ap()
        io["perm_t"] = nc.dram_tensor(pre + "perm_t", [128, 32], I32, kind="ExternalInput").ap()
        io["upsl_bf"] = nc.dram_tensor(pre + "upsl_bf", [HSL * H_DH, H_M], BF16).ap()
        io["dnsl_bf"] = nc.dram_tensor(pre + "dnsl_bf", [HSL * H_M, H_DH], BF16).ap()
        io["sh_up"] = nc.dram_tensor(pre + "sh_up", [H_HEADS * H_DH, H_M], BF16, addr_space="Shared").ap()
        io["sh_dn"] = nc.dram_tensor(pre + "sh_dn", [H_HEADS * H_M, H_DH], BF16, addr_space="Shared").ap()
    else:
        for k in ["upsl", "dnsl", "perm_t", "upsl_bf", "dnsl_bf", "sh_up", "sh_dn"]:
            io[k] = weights[k]
    io["xln_tm"] = nc.dram_tensor(pre + "xln_tm", [TOK, D], BF16).ap()
    io["xln_fm"] = nc.dram_tensor(pre + "xln_fm", [D, TOK], BF16).ap()
    io["deltaT"] = nc.dram_tensor(pre + "deltaT", [D, TOK], F32).ap()
    return io


def _decl_step5_io(nc, pre, cores, x_in, x_out):
    TSL = D // cores
    io = {"x_in": x_in, "x_out": x_out}
    io["w1sl"] = nc.dram_tensor(pre + "w1sl", [TSL, D], F32, kind="ExternalInput").ap()
    io["w2Tsl"] = nc.dram_tensor(pre + "w2Tsl", [D, TSL], F32, kind="ExternalInput").ap()
    io["mask_sl"] = nc.dram_tensor(pre + "mask_sl", [D, TSL], F32, kind="ExternalInput").ap()
    io["b2"] = nc.dram_tensor(pre + "b2", [1, D], F32, kind="ExternalInput").ap()
    io["w1sl_bf"] = nc.dram_tensor(pre + "w1sl_bf", [TSL, D], BF16).ap()
    io["w2Tsl_bf"] = nc.dram_tensor(pre + "w2Tsl_bf", [D, TSL], BF16).ap()
    io["attn_sl"] = nc.dram_tensor(pre + "attn_sl", [D, TSL], BF16).ap()
    io["xln_tm"] = nc.dram_tensor(pre + "xln_tm", [TOK, D], BF16).ap()
    io["sh_w1"] = nc.dram_tensor(pre + "sh_w1", [D, D], BF16, addr_space="Shared").ap()
    io["sh_attn"] = nc.dram_tensor(pre + "sh_attn", [cores * D, TSL], BF16, addr_space="Shared").ap()
    return io


def emit_gate(p, block, prev_done):
    if prev_done:
        @block.vector
        def _(v):
            for s, val in prev_done:
                v.wait_ge(s.h, val)

        @block.scalar
        def _(sc):
            for s, val in prev_done:
                sc.wait_ge(s.h, val)

        @block.tensor
        def _(t):
            for s, val in prev_done:
                t.wait_ge(s.h, val)

        @block.sync
        def _(sy):
            for s, val in prev_done:
                sy.wait_ge(s.h, val)

    gp_sems = ["s_gp", "s_gpw", "s_gpl", "s_gpt", "s_gpz", "s_gplw",
               "s_gpp", "s_gx0", "s_gx1", "s_gs0", "s_gs1"]
    targets = [(p.sems[n], p.sems[n].v) for n in gp_sems if n in p.sems]
    if targets:
        @block.gpsimd
        def _(gp):
            for s, tgt in targets:
                gp.wait_ge(s.h, tgt)


def build_fused_nc(cores=8, nsteps=6):
    nc = bass.Bass("TRN2", target_bir_lowering=False)
    x0 = nc.dram_tensor("x0", [TOK, D], F32, kind="ExternalInput").ap()
    xf = nc.dram_tensor("x_out", [TOK, D], F32, kind="ExternalOutput").ap()
    xA = nc.dram_tensor("xA", [TOK, D], F32).ap()
    xB = nc.dram_tensor("xB", [TOK, D], F32).ap()

    outs = [xA, xB, xA, xB, xA, xf]
    outs[nsteps - 1] = xf
    io1 = _decl_step1_io(nc, "p1_", cores, x0, outs[0])
    io2 = _decl_step2_io(nc, "p2_", cores, outs[0], outs[1])
    io3 = dict(io1)
    io3.update({"x_in": outs[1], "x_out": outs[2],
                "xln_tm": nc.dram_tensor("p3_xln_tm", [TOK, D], BF16).ap()})
    io4 = _decl_step4_io(nc, "p4_", cores, outs[2], outs[3])
    io5 = _decl_step5_io(nc, "p5_", cores, outs[3], outs[4])
    io6 = _decl_step4_io(nc, "p6_", cores, outs[4], outs[5], weights=io4)

    with ExitStack() as es:
        p = P(nc, es, cores)
        block = es.enter_context(nc.Block())
        sh1 = {}
        sh4 = {}
        d1 = emit_step1(p, block, io1, tag="s1", share=sh1)
        if nsteps >= 2:
            emit_gate(p, block, d1)
            d2 = emit_step2(p, block, io2, tag="s2", prev_done=d1)
        if nsteps >= 3:
            emit_gate(p, block, d2)
            d3 = emit_step1(p, block, io3, tag="s3", prev_done=d2, share=sh1)
        if nsteps >= 4:
            emit_gate(p, block, d3)
            d4 = emit_step4(p, block, io4, tag="s4", prev_done=d3, share=sh4)
        if nsteps >= 5:
            emit_gate(p, block, d4)
            d5 = emit_step5(p, block, io5, tag="s5", prev_done=d4)
        if nsteps >= 6:
            emit_gate(p, block, d5)
            d6 = emit_step4(p, block, io6, tag="s6", prev_done=d5, share=sh4)
    return nc


def fused_in_maps(inputs, cores=8):
    x = np.ascontiguousarray(
        np.asarray(inputs["L0_attn_x"], np.float32).reshape(8192, D))
    m1 = step1_in_maps(inputs, x, cores)
    m2 = step2_in_maps(inputs, x, cores)
    m4 = step4_in_maps(inputs, x, cores)
    m5 = step5_in_maps(inputs, x, cores)
    maps = []
    for k in range(cores):
        m = {}
        m["x0"] = m1[k].pop("x_in")
        m2[k].pop("x_in"); m4[k].pop("x_in"); m5[k].pop("x_in")
        for pre, mm in [("p1_", m1[k]), ("p2_", m2[k]), ("p4_", m4[k]),
                        ("p5_", m5[k])]:
            for name, arr in mm.items():
                m[pre + name] = arr
        maps.append(m)
    return maps


_NC_CACHE = {}


def _get_nc(name, builder):
    if name not in _NC_CACHE:
        _NC_CACHE[name] = builder(cores=8)
    return _NC_CACHE[name]


def _launch(nc, maps):
    res = run_bass_kernel_spmd(nc, maps, core_ids=list(range(8)))
    return np.concatenate([res.results[c]["x_out"] for c in range(8)], 0)


def kernel_unfused(**inputs):
    x = np.ascontiguousarray(
        np.asarray(inputs["L0_attn_x"], np.float32).reshape(8192, D))
    nc1 = _get_nc("s1", build_step1_nc)
    nc2 = _get_nc("s2", build_step2_nc)
    nc4 = _get_nc("s4", build_step4_nc)
    nc5 = _get_nc("s5", build_step5_nc)
    x = _launch(nc1, step1_in_maps(inputs, x))
    x = _launch(nc2, step2_in_maps(inputs, x))
    x = _launch(nc1, step1_in_maps(inputs, x))
    x = _launch(nc4, step4_in_maps(inputs, x))
    x = _launch(nc5, step5_in_maps(inputs, x))
    x = _launch(nc4, step4_in_maps(inputs, x))
    return x.reshape(2, 4096, D)


def kernel(**inputs):
    nc = _get_nc("fused", build_fused_nc)
    maps = fused_in_maps(inputs, cores=8)
    res = run_bass_kernel_spmd(nc, maps, core_ids=list(range(8)))
    x = np.concatenate([res.results[c]["x_out"] for c in range(8)], 0)
    return x.reshape(2, 4096, D)
